# version 01
# baseline (speedup 1.0000x reference)
import numpy as np
import concourse.bass as bass
import concourse.mybir as mybir
from concourse.bass_utils import run_bass_kernel_spmd

F32 = mybir.dt.float32
BF16 = mybir.dt.bfloat16
AF = mybir.ActivationFunctionType
ALU = mybir.AluOpType
AX = mybir.AxisListType

T = 4096
DM = 1024
NCH = 8
N_CORES = 8

ENGINES = ("t", "v", "s", "p", "y")


class Em:

    def __init__(self, nc):
        self.nc = nc
        self.q = {e: [] for e in ENGINES}
        self.sem = {e: nc.alloc_semaphore("sem_" + e) for e in ENGINES}
        self.cnt = {e: 0 for e in ENGINES}
        self.hw = {e: {} for e in ENGINES}
        self.bufsem = {}

    def _waits(self, e, after):
        out = []
        for tok in after:
            if tok is None:
                continue
            sem, val = tok
            key = id(sem)
            if self.hw[e].get(key, -1) >= val:
                continue
            self.hw[e][key] = val
            out.append((sem, val))
        return out

    def op(self, e, fn, after=()):
        return self.ops(e, [fn], after)

    def ops(self, e, fns, after=()):
        waits = self._waits(e, after)
        self.cnt[e] += 1
        val = self.cnt[e]
        sem = self.sem[e]

        def emit(eng, fns=fns, waits=waits, sem=sem):
            for s, v in waits:
                eng.wait_ge(s, v)
            inst = None
            for fn in fns:
                inst = fn(eng)
            inst.then_inc(sem, 1)

        self.q[e].append(emit)
        return (sem, val)

    def dma(self, e, out, in_, after=()):
        waits = self._waits(e, after)
        name = out.tensor.name
        ent = self.bufsem.get(name)
        if ent is None:
            ent = [self.nc.alloc_semaphore("dsem_" + name), 0]
            self.bufsem[name] = ent
        ent[1] += 16
        val = ent[1]
        sem = ent[0]

        def emit(eng, out=out, in_=in_, waits=waits, sem=sem):
            for s, v in waits:
                eng.wait_ge(s, v)
            eng.dma_start(out=out, in_=in_).then_inc(sem, 16)

        self.q[e].append(emit)
        return (sem, val)

    def finalize(self, final_tokens):
        nc = self.nc
        with nc.Block() as block:
            def run(eng, e):
                for emit in self.q[e]:
                    emit(eng)

            @block.tensor
            def _(eng):
                run(eng, "t")

            @block.vector
            def _(eng):
                run(eng, "v")

            @block.scalar
            def _(eng):
                run(eng, "s")

            @block.gpsimd
            def _(eng):
                run(eng, "p")

            @block.sync
            def _(eng):
                run(eng, "y")
                for s, v in final_tokens:
                    eng.wait_ge(s, v)


class Core:

    def __init__(self, nc):
        self.nc = nc
        self.em = Em(nc)
        self.uid = 0
        self.pb = [nc.alloc_psum_tensor("pbank%d" % i, [128, 512], F32)
                   for i in range(8)]
        self.ones_col_bf = nc.alloc_sbuf_tensor("ones_col_bf", [128, 1], BF16)
        self.ones_row_bf = nc.alloc_sbuf_tensor("ones_row_bf", [1, 128], BF16)
        self.ones_col_f = nc.alloc_sbuf_tensor("ones_col_f", [128, 1], F32)
        self.ones_row_f = nc.alloc_sbuf_tensor("ones_row_f", [1, 128], F32)
        self.eps_col = nc.alloc_sbuf_tensor("eps_col", [128, 1], F32)
        em = self.em
        self.t_const = em.ops("v", [
            lambda v: v.memset(self.ones_col_bf[:], 1.0),
            lambda v: v.memset(self.ones_row_bf[:], 1.0),
            lambda v: v.memset(self.ones_col_f[:], 1.0),
            lambda v: v.memset(self.ones_row_f[:], 1.0),
            lambda v: v.memset(self.eps_col[:], 1e-5),
        ])

    def name(self, base):
        self.uid += 1
        return "%s_%d" % (base, self.uid)

    def sb(self, base, shape, dtype=F32):
        return self.nc.alloc_sbuf_tensor(self.name(base), shape, dtype)

    def ln_scratch(self):
        if not hasattr(self, "_lns"):
            self._lns = {
                "sq": self.sb("lnx_sq", [128, 512], BF16),
                "m": self.sb("lnx_m", [1, 512], F32),
                "q": self.sb("lnx_q", [1, 512], F32),
                "v": self.sb("lnx_v", [1, 512], F32),
                "r": self.sb("lnx_r", [1, 512], F32),
                "s": self.sb("lnx_s", [1, 512], F32),
                "t": self.sb("lnx_t", [128, 512], F32),
            }
        return self._lns


def emit_ln(C, x_hbm, xb, xln, blk, tb, after=(), f32_hbm=None, f32_bufs=None):
    em = C.em
    t0 = tb * blk
    xb_toks = [em.dma("p", xb[c][:, 0:tb],
                      x_hbm[128 * c:128 * (c + 1), t0:t0 + tb],
                      after=after) for c in range(NCH)]
    lns = C.ln_scratch()
    sq = lns["sq"]
    ps_s = C.pb[6]
    ps_q = C.pb[7]
    mm_toks = []
    for c in range(NCH):
        tsq = em.op("s", lambda s, c=c: s.activation(sq[:, 0:tb], xb[c][:, 0:tb],
                                                     AF.Square),
                    after=[xb_toks[c]] + ([mm_toks[-1]] if c else list(after)))
        mm_toks.append(em.ops("t", [
            lambda t, c=c: t.matmul(ps_s[0:1, 0:tb], C.ones_col_bf[:],
                                    xb[c][:, 0:tb], start=(c == 0),
                                    stop=(c == NCH - 1)),
            lambda t, c=c: t.matmul(ps_q[0:1, 0:tb], C.ones_col_bf[:],
                                    sq[:, 0:tb], start=(c == 0),
                                    stop=(c == NCH - 1)),
        ], after=[xb_toks[c], tsq, C.t_const]))
    m_row = lns["m"][:, 0:tb]
    q_row = lns["q"][:, 0:tb]
    v_row = lns["v"][:, 0:tb]
    r_row = lns["r"][:, 0:tb]
    s_row = lns["s"][:, 0:tb]
    tk = mm_toks[-1]
    tk1 = em.op("v", lambda v: v.tensor_scalar_mul(m_row, ps_s[0:1, 0:tb],
                                                   1.0 / DM), after=[tk])
    tk2 = em.op("v", lambda v: v.tensor_scalar_mul(q_row, ps_q[0:1, 0:tb],
                                                   1.0 / DM), after=[tk])
    tk3 = em.op("v", lambda v: v.tensor_mul(v_row, m_row, m_row),
                after=[tk1])
    tk4 = em.op("v", lambda v: v.tensor_sub(v_row, q_row, v_row),
                after=[tk2, tk3])
    tk5 = em.op("s", lambda s: s.activation(v_row, v_row, AF.Sqrt,
                                            bias=C.eps_col[0:1, :]),
                after=[tk4, C.t_const])
    tk6 = em.op("v", lambda v: v.reciprocal(r_row, v_row), after=[tk5])
    tk7 = em.op("v", lambda v: v.tensor_mul(s_row, m_row, r_row),
                after=[tk6])
    tb1 = em.op("t", lambda t: t.matmul(ps_s[:, 0:tb], C.ones_row_f[:],
                                        r_row, start=True, stop=True),
                after=[tk6])
    tb2 = em.op("t", lambda t: t.matmul(ps_q[:, 0:tb], C.ones_row_f[:],
                                        s_row, start=True, stop=True),
                after=[tk7])
    tmp = lns["t"]
    out_toks = []
    f32_st = [None, None]
    for c in range(NCH):
        ta = em.op("v", lambda v, c=c: v.tensor_mul(tmp[:, 0:tb], xb[c][:, 0:tb],
                                                    ps_s[:, 0:tb]),
                   after=[tb1] + ([out_toks[-1]] if c else []))
        out_toks.append(em.op("v", lambda v, c=c: v.tensor_sub(
            xln[c][:, 0:tb], tmp[:, 0:tb], ps_q[:, 0:tb]), after=[tb2, ta]))
        if f32_hbm is not None:
            fb = f32_bufs[c % 2]
            t32 = em.op("v", lambda v, c=c, fb=fb: v.tensor_sub(
                fb[:, 0:tb], tmp[:, 0:tb], ps_q[:, 0:tb]),
                after=[tb2, ta] + ([f32_st[c % 2]] if f32_st[c % 2] else []))
            f32_st[c % 2] = em.dma("y", f32_hbm[128 * c:128 * (c + 1),
                                                t0:t0 + tb],
                                   fb[:, 0:tb], after=[t32])
            out_toks[-1] = t32
    return out_toks, xb_toks



def host_prep_l1ffn(inputs):
    rm = np.asarray(inputs["L1_ffn_rail_maker"])
    up = np.asarray(inputs["L1_ffn_rail_up_proj"])
    cp = np.asarray(inputs["L1_ffn_w_cproj"])
    bd1 = np.zeros((4, 8, 128, 128), np.float32)
    for h in range(4):
        for r in range(64):
            c, j = divmod(r, 8)
            bd1[h, c, 16 * j:16 * j + 16, 16 * j:16 * j + 16] = rm[h, r]
    bd2 = np.zeros((4, 32, 128, 128), np.float32)
    for h in range(4):
        for cz in range(32):
            r0 = 2 * cz
            base = 32 * (cz % 4)
            bd2[h, cz, base:base + 16, 0:64] = up[h, r0].T
            bd2[h, cz, base + 16:base + 32, 64:128] = up[h, r0 + 1].T
    return {"l1f_bd1": np.ascontiguousarray(
                bd1.transpose(2, 0, 1, 3).reshape(128, 4 * 8 * 128)),
            "l1f_bd2": np.ascontiguousarray(
                bd2.transpose(2, 0, 1, 3).reshape(128, 4 * 32 * 128)),
            "l1f_cpT": np.ascontiguousarray(cp.T)}


def emit_l1ffn(C, x_hbm, xo_hbm, W, cpbf_hbm, cpbf_ready):
    em = C.em
    G = 256
    NG = T // G
    bd1_sb = C.sb("bd1", [128, 4 * 8 * 128], BF16)
    bd2_sb = C.sb("bd2", [128, 4 * 32 * 128], BF16)
    tok_bd1 = em.dma("p", bd1_sb[:], W["l1f_bd1"])
    tok_bd = em.dma("p", bd2_sb[:], W["l1f_bd2"], after=[tok_bd1])

    def bd1_t(h, c):
        i = h * 8 + c
        return bd1_sb[:, i * 128:(i + 1) * 128]

    def bd2_t(h, cz):
        i = h * 32 + cz
        return bd2_sb[:, i * 128:(i + 1) * 128]

    xb = [C.sb("xb", [128, G], BF16) for _ in range(NCH)]
    xf = [C.sb("xf", [128, G], F32) for _ in range(2)]
    xln = [C.sb("xln", [128, G], BF16) for _ in range(NCH)]
    y_sb = [C.sb("yrail", [128, G], BF16) for _ in range(NCH)]
    hbuf = C.sb("hbuf", [128, 128 * G], BF16)
    cp_sb = [C.sb("cpslice", [128, 1024], BF16) for _ in range(2)]
    osum = C.sb("osum", [128, G], F32)

    prev_cproj = cpbf_ready
    prev_resid = [None] * NCH
    prev_res_v = None
    cp_toks = [None, None]
    mm_last = None
    for g in range(NG):
        xln_toks, xb_toks = emit_ln(C, x_hbm, xb, xln, g, G,
                                    after=[tok_bd, prev_cproj, prev_res_v])
        gel_toks = []
        for h in range(4):
            y_toks = []
            for c in range(NCH):
                bank = C.pb[c % 2]
                tm = em.op("t", lambda t, h=h, c=c, bank=bank: t.matmul(
                    bank[:, 0:G], bd1_t(h, c), xln[c][:, 0:G],
                    start=True, stop=True),
                    after=[xln_toks[c], tok_bd] + ([y_toks[-2]] if len(y_toks) >= 2 else []))
                y_toks.append(em.op("v", lambda v, c=c, bank=bank: v.tensor_copy(
                    y_sb[c][:, 0:G], bank[:, 0:G]), after=[tm]))
            for cz in range(32):
                cj = cz // 4
                bank = C.pb[2 + cz % 2]
                tm = em.op("t", lambda t, h=h, cz=cz, cj=cj, bank=bank: t.matmul(
                    bank[:, 0:G], bd2_t(h, cz), y_sb[cj][:, 0:G],
                    start=True, stop=True),
                    after=[y_toks[cj]] + ([gel_toks[-2]] if len(gel_toks) >= 2 else []))
                slot = h * 32 + cz
                gel_toks.append(em.op("s", lambda s, slot=slot, bank=bank: s.activation(
                    hbuf[:, slot * G:(slot + 1) * G], bank[:, 0:G], AF.Gelu),
                    after=[tm, prev_cproj]))
        tok_h = gel_toks[-1]
        for kc in range(128):
            buf = kc % 2
            dep = [tok_bd]
            if cp_toks[buf] is not None:
                dep.append(mm_last)
            cp_toks[buf] = em.dma("y", cp_sb[buf][:],
                                  cpbf_hbm[kc * 128:(kc + 1) * 128, :],
                                  after=dep + [cpbf_ready])
            fns = []
            for oc in range(NCH):
                fns.append(lambda t, kc=kc, oc=oc, buf=buf: t.matmul(
                    C.pb[oc][:, 0:G], cp_sb[buf][:, oc * 128:(oc + 1) * 128],
                    hbuf[:, kc * G:(kc + 1) * G], start=(kc == 0),
                    stop=(kc == 127)))
            mm_last = em.ops("t", fns,
                             after=[cp_toks[buf], tok_h] +
                             ([prev_resid[-1]] if kc == 0 and prev_resid[-1] else []))
        res_toks = []
        for oc in range(NCH):
            xi = xf[oc % 2]
            tx = em.dma("y", xi[:, 0:G],
                        x_hbm[128 * oc:128 * (oc + 1), g * G:(g + 1) * G],
                        after=[res_toks[-2]] if len(res_toks) >= 2 else [])
            tv = em.op("v", lambda v, oc=oc, xi=xi: v.tensor_add(
                osum[:, 0:G], xi[:, 0:G], C.pb[oc][:, 0:G]),
                after=[mm_last, tx] + ([res_toks[-1]] if res_toks else []))
            res_toks.append(em.dma("y", xo_hbm[128 * oc:128 * (oc + 1),
                                               g * G:(g + 1) * G],
                                   osum[:, 0:G], after=[tv]))
            prev_res_v = tv
        prev_cproj = mm_last
        prev_resid = res_toks
    return res_toks



def host_prep_l0ffn(inputs):
    w1 = np.asarray(inputs["L0_ffn_w1"])
    w2 = np.asarray(inputs["L0_ffn_w2"])
    b1 = np.asarray(inputs["L0_ffn_b1"])
    b2 = np.asarray(inputs["L0_ffn_b2"])
    rw = np.asarray(inputs["L0_ffn_router_w"])
    rb = np.asarray(inputs["L0_ffn_router_b"])
    lw = np.asarray(inputs["L0_ffn_lang_w"])
    lb = np.asarray(inputs["L0_ffn_lang_b"])
    w1l = w1.reshape(8, 8, 128, 32, 128).transpose(0, 3, 2, 1, 4).reshape(
        8 * 32 * 128, 8 * 128)
    w2l = w2.reshape(8, 32, 128, 8, 128).transpose(0, 3, 2, 1, 4).reshape(
        8 * 8 * 128, 32 * 128)
    b1l = b1.reshape(8, 32, 128).transpose(2, 0, 1).reshape(128, 8 * 32)
    rl = np.concatenate([rw.T, lw.T], axis=1)
    rlb = np.concatenate([rb, lb])[None, :]
    oh = np.zeros((8, 8 * 128), np.float32)
    for e in range(8):
        oh[e, e * 128:(e + 1) * 128] = 1.0
    return {"ff_w1l": np.ascontiguousarray(w1l), "ff_w2l": np.ascontiguousarray(w2l),
            "ff_b1l": np.ascontiguousarray(b1l), "ff_b2": b2.astype(np.float32),
            "ff_rl": np.ascontiguousarray(rl), "ff_rlb": rlb.astype(np.float32),
            "ff_oh": oh}


def emit_l0ffn(C, x_hbm, xo_hbm, W, w1bf, w2bf, wcast_tok, xln_hbm):
    em = C.em
    TBL = 512
    NB = T // TBL
    rl_sb = C.sb("rl", [128, 8 * 12], BF16)
    rlb_sb = C.sb("rlb", [1, 12], BF16)
    b1_sb = C.sb("b1c", [128, 8 * 32], F32)
    b2_sb = C.sb("b2r", [8, 1024], F32)
    oh_sb = C.sb("ohr", [8, 8 * 128], F32)
    g_all = C.sb("gall", [8, T], F32)
    tok_w = [em.dma("p", rl_sb[:, kc * 12:(kc + 1) * 12],
                    W["ff_rl"][kc * 128:(kc + 1) * 128, :]) for kc in range(NCH)]
    tok_w.append(em.dma("p", rlb_sb[:], W["ff_rlb"]))
    tok_w.append(em.dma("y", b1_sb[:], W["ff_b1l"]))
    tok_w.append(em.dma("y", b2_sb[:], W["ff_b2"]))
    tok_w.append(em.dma("y", oh_sb[:], W["ff_oh"]))
    tok_w = tok_w[-1]

    xb = [C.sb("fxb", [128, TBL], BF16) for _ in range(NCH)]
    xln = [C.sb("fxln", [128, TBL], BF16) for _ in range(NCH)]
    ident_f = C.sb("identf", [128, 128], F32)
    iota_mp = C.sb("iotamp", [128, 128], F32)
    t_id0 = em.op("p", lambda g: g.iota(iota_mp[:], [[1, 128]], base=0,
                                        channel_multiplier=-1,
                                        allow_small_or_imprecise_dtypes=True))
    t_id = em.op("v", lambda v: v.tensor_single_scalar(
        ident_f[:], iota_mp[:], 0.0, ALU.is_equal), after=[t_id0])

    rt = C.sb("rt", [128, 12], F32)
    le = C.sb("rle", [128, 4], F32)
    ews = C.sb("rew", [128, 8], F32)
    rq = C.sb("rrq", [128, 8], F32)
    re_ = C.sb("rre", [128, 8], F32)
    rp = C.sb("rrp", [128, 8], F32)
    gt = C.sb("rgt", [128, 8], F32)
    c1 = C.sb("rc1", [128, 1], F32)
    c2 = C.sb("rc2", [128, 1], F32)
    c3 = C.sb("rc3", [128, 1], F32)

    last_route = None
    prev_st = None
    for blk in range(NB):
        xln_toks, _ = emit_ln(C, x_hbm, xb, xln, blk, TBL,
                              after=[tok_w, last_route, prev_st])
        st_toks = [em.dma("y", xln_hbm[128 * c:128 * (c + 1),
                                       blk * TBL:(blk + 1) * TBL],
                          xln[c][:, 0:TBL], after=[xln_toks[c]])
                   for c in range(NCH)]
        for ts_ in range(4):
            ps = C.pb[ts_ % 2]
            fns = []
            for kc in range(NCH):
                fns.append(lambda t, kc=kc, ts_=ts_, ps=ps: t.matmul(
                    ps[0:128, 0:12], xln[kc][:, ts_ * 128:(ts_ + 1) * 128],
                    rl_sb[:, kc * 12:(kc + 1) * 12],
                    start=(kc == 0), stop=False))
            fns.append(lambda t, ts_=ts_, ps=ps: t.matmul(
                ps[0:128, 0:12], C.ones_row_bf[:], rlb_sb[:],
                start=False, stop=True))
            tmm = em.ops("t", fns, after=xln_toks + [tok_w, C.t_const, last_route])
            o = em.op("v", lambda v, ps=ps: v.tensor_copy(rt[:], ps[0:128, 0:12]),
                      after=[tmm])
            o = em.op("v", lambda v: v.tensor_reduce(c1[:], rt[:, 8:12], AX.X,
                                                     ALU.max), after=[o])
            o = em.op("v", lambda v: v.tensor_scalar_mul(c1[:], c1[:], -1.0),
                      after=[o])
            o = em.op("s", lambda s: s.activation(le[:], rt[:, 8:12], AF.Exp,
                                                  bias=c1[:]), after=[o])
            o = em.op("v", lambda v: v.tensor_reduce(c2[:], le[:], AX.X, ALU.add),
                      after=[o])
            o = em.op("v", lambda v: v.reciprocal(c2[:], c2[:]), after=[o])
            o = em.op("v", lambda v: v.tensor_scalar_mul(le[:], le[:], c2[:]),
                      after=[o])
            fns = []
            for e in range(8):
                l = e % 4
                if l < 3:
                    fns.append(lambda v, e=e, l=l: v.tensor_add(
                        ews[:, e:e + 1], le[:, l:l + 1], le[:, l + 1:l + 2]))
                else:
                    fns.append(lambda v, e=e, l=l: v.tensor_copy(
                        ews[:, e:e + 1], le[:, l:l + 1]))
            o = em.ops("v", fns, after=[o])
            o = em.op("v", lambda v: v.scalar_tensor_tensor(
                rq[:], ews[:], 0.1, rt[:, 0:8], ALU.mult, ALU.add), after=[o])
            o = em.op("v", lambda v: v.tensor_reduce(c1[:], rq[:], AX.X, ALU.max),
                      after=[o])
            o = em.op("v", lambda v: v.tensor_scalar_mul(c1[:], c1[:], -1.0),
                      after=[o])
            o = em.op("s", lambda s: s.activation(re_[:], rq[:], AF.Exp,
                                                  bias=c1[:]), after=[o])
            o = em.op("v", lambda v: v.tensor_reduce(c2[:], re_[:], AX.X, ALU.add),
                      after=[o])
            o = em.op("v", lambda v: v.reciprocal(c2[:], c2[:]), after=[o])
            o = em.op("v", lambda v: v.tensor_scalar_mul(rp[:], re_[:], c2[:]),
                      after=[o])
            o = em.op("v", lambda v: v.tensor_reduce(c1[:], rp[:], AX.X, ALU.max),
                      after=[o])
            o = em.op("v", lambda v: v.tensor_single_scalar(gt[:], rp[:], c1[:],
                                                            ALU.is_equal),
                      after=[o])
            o = em.op("v", lambda v: v.scalar_tensor_tensor(
                gt[:], gt[:], -2.0, rp[:], ALU.mult, ALU.add), after=[o])
            o = em.op("v", lambda v: v.tensor_reduce(c2[:], gt[:], AX.X, ALU.max),
                      after=[o])
            o = em.op("v", lambda v: v.tensor_add(c3[:], c1[:], c2[:]), after=[o])
            o = em.op("v", lambda v: v.reciprocal(c3[:], c3[:]), after=[o])
            o = em.op("v", lambda v: v.tensor_single_scalar(gt[:], rp[:], c2[:],
                                                            ALU.is_ge), after=[o])
            o = em.op("v", lambda v: v.tensor_mul(gt[:], gt[:], rp[:]), after=[o])
            o = em.op("v", lambda v: v.tensor_scalar_mul(gt[:], gt[:], c3[:]),
                      after=[o])
            pst = C.pb[2 + ts_ % 2]
            o2 = em.op("t", lambda t, pst=pst: t.transpose(
                pst[0:8, 0:128], gt[:], ident_f[:]), after=[o, t_id])
            last_route = em.op("v", lambda v, blk=blk, ts_=ts_, pst=pst:
                               v.tensor_copy(g_all[0:8, blk * TBL + ts_ * 128:
                                                   blk * TBL + (ts_ + 1) * 128],
                                             pst[0:8, 0:128]), after=[o2])
        prev_st = st_toks[-1]
    route_done = last_route
    xlnst_done = st_toks[-1]

    w1sl = [C.sb("w1sl", [128, 8 * 128], BF16) for _ in range(2)]
    w2sl = [C.sb("w2sl", [128, 32 * 128], BF16) for _ in range(2)]
    hbuf = C.sb("fhbuf", [128, 32 * TBL], BF16)
    gbe = C.sb("gbe", [128, TBL], BF16)
    acc = [C.sb("facc", [128, TBL], F32) for _ in range(NCH)]
    tmpm = C.sb("ftmp", [128, TBL], F32)
    xfr = [C.sb("fxf", [128, TBL], F32) for _ in range(2)]
    osum = C.sb("fosum", [128, TBL], F32)

    w1_toks = [None, None]
    w2_toks = [None, None]
    w1_used = [None, None]
    w2_used = [None, None]
    xl_toks = [None] * NCH
    prev_blk_acc_read = None
    prev_store = None
    prev_gelu = None
    prev_accv = None
    for blk in range(NB):
        xl_toks = [em.dma("p", xln[c][:, 0:TBL],
                          xln_hbm[128 * c:128 * (c + 1),
                                  blk * TBL:(blk + 1) * TBL],
                          after=[xlnst_done, route_done, prev_gelu, prev_accv])
                   for c in range(NCH)]
        for dc in range(NCH):
            ps = C.pb[4 + dc % 2]
            tm = em.op("t", lambda t, dc=dc, blk=blk, ps=ps: t.matmul(
                ps[:, 0:TBL], b2_sb[0:8, dc * 128:(dc + 1) * 128],
                g_all[0:8, blk * TBL:(blk + 1) * TBL], start=True, stop=True),
                after=[route_done, tok_w, prev_accv])
            prev_accv = em.op("v", lambda v, dc=dc, ps=ps: v.tensor_copy(
                acc[dc][:, 0:TBL], ps[:, 0:TBL]), after=[tm, prev_store])
        for e in range(8):
            psg = C.pb[6]
            tg = em.op("t", lambda t, e=e, blk=blk: t.matmul(
                psg[:, 0:TBL], oh_sb[0:8, e * 128:(e + 1) * 128],
                g_all[0:8, blk * TBL:(blk + 1) * TBL], start=True, stop=True),
                after=[route_done, tok_w, prev_accv])
            tgb = em.op("v", lambda v: v.tensor_copy(gbe[:, 0:TBL],
                                                     psg[:, 0:TBL]), after=[tg])
            gelu_hist = [None, None]
            for hc in range(32):
                buf = hc % 2
                dep = [wcast_tok]
                if w1_used[buf] is not None:
                    dep.append(w1_used[buf])
                w1_toks[buf] = em.dma("y", w1sl[buf][:],
                                      w1bf[(e * 32 + hc) * 128:
                                           (e * 32 + hc + 1) * 128, :],
                                      after=dep)
                ps = C.pb[hc % 2]
                fns = []
                for kc in range(NCH):
                    fns.append(lambda t, kc=kc, buf=buf, ps=ps: t.matmul(
                        ps[:, 0:TBL], w1sl[buf][:, kc * 128:(kc + 1) * 128],
                        xln[kc][:, 0:TBL], start=(kc == 0), stop=(kc == NCH - 1)))
                tmm = em.ops("t", fns, after=[w1_toks[buf]] + xl_toks +
                             [gelu_hist[buf] if gelu_hist[buf] is not None
                              else prev_gelu])
                w1_used[buf] = tmm
                prev_gelu = em.op("s", lambda s, hc=hc, e=e, ps=ps: s.activation(
                    hbuf[:, hc * TBL:(hc + 1) * TBL], ps[:, 0:TBL], AF.Gelu,
                    bias=b1_sb[:, e * 32 + hc:e * 32 + hc + 1]),
                    after=[tmm, tok_w])
                gelu_hist[buf] = prev_gelu
            for dc in range(NCH):
                buf = dc % 2
                dep = [wcast_tok]
                if w2_used[buf] is not None:
                    dep.append(w2_used[buf])
                w2_toks[buf] = em.dma("y", w2sl[buf][:],
                                      w2bf[(e * 8 + dc) * 128:
                                           (e * 8 + dc + 1) * 128, :],
                                      after=dep)
                ps = C.pb[2 + dc % 2]
                fns = []
                for kc in range(32):
                    fns.append(lambda t, kc=kc, buf=buf, ps=ps: t.matmul(
                        ps[:, 0:TBL], w2sl[buf][:, kc * 128:(kc + 1) * 128],
                        hbuf[:, kc * TBL:(kc + 1) * TBL],
                        start=(kc == 0), stop=(kc == 31)))
                tmm = em.ops("t", fns, after=[w2_toks[buf], prev_gelu,
                                              prev_accv])
                w2_used[buf] = tmm
                tv1 = em.op("v", lambda v, ps=ps: v.tensor_mul(
                    tmpm[:, 0:TBL], ps[:, 0:TBL], gbe[:, 0:TBL]),
                    after=[tmm, tgb])
                prev_accv = em.op("v", lambda v, dc=dc: v.tensor_add(
                    acc[dc][:, 0:TBL], acc[dc][:, 0:TBL], tmpm[:, 0:TBL]),
                    after=[tv1])
        prev_blk_acc_read = prev_accv
        res_toks = []
        for dc in range(NCH):
            xi = xfr[dc % 2]
            tx = em.dma("y", xi[:, 0:TBL],
                        x_hbm[128 * dc:128 * (dc + 1),
                              blk * TBL:(blk + 1) * TBL])
            tv = em.op("v", lambda v, dc=dc, xi=xi: v.tensor_add(
                osum[:, 0:TBL], xi[:, 0:TBL], acc[dc][:, 0:TBL]),
                after=[prev_accv, tx] + ([res_toks[-1]] if res_toks else []))
            res_toks.append(em.dma("y", xo_hbm[128 * dc:128 * (dc + 1),
                                               blk * TBL:(blk + 1) * TBL],
                                   osum[:, 0:TBL], after=[tv]))
            prev_store = tv
        prev_accv = prev_store
    return res_toks



ABQ_H0 = [0, 1, 4, 5, 8, 9, 12, 13]
ABQ_H1 = [2, 3, 6, 7, 10, 11, 14, 15]


def host_prep_l0attnb(inputs):
    wq = np.asarray(inputs["L0_attn_b_Wq"])
    wk = np.asarray(inputs["L0_attn_b_Wk"])
    wv = np.asarray(inputs["L0_attn_b_Wv"])
    wo = np.asarray(inputs["L0_attn_b_Wo"])
    bo = np.asarray(inputs["L0_attn_b_bo"])
    tq = np.arange(T, dtype=np.float32)
    hi_q = np.floor(tq / 32.0)
    lo_q = tq - 32.0 * hi_q
    qaux = np.zeros((16, 2, 4, T), np.float32)
    for h in range(16):
        s = 2.0 ** (-(h + 4))
        qaux[h, 0, 0] = -32.0 * s * hi_q
        qaux[h, 0, 1] = 32.0 * s
        qaux[h, 0, 2] = s
        qaux[h, 0, 3] = -s * lo_q
        qaux[h, 1, 0] = 32.0 * s * hi_q
        qaux[h, 1, 1] = -32.0 * s
        qaux[h, 1, 2] = -s
        qaux[h, 1, 3] = s * lo_q
    perm = np.concatenate([np.r_[ABQ_H0[c] * 64:(ABQ_H0[c] + 1) * 64,
                                 ABQ_H1[c] * 64:(ABQ_H1[c] + 1) * 64]
                           for c in range(8)])
    wqp = wq.T[:, perm]
    wop = wo.T[perm, :]
    wql = wqp.reshape(8, 128, 8, 128).transpose(2, 1, 0, 3).reshape(8 * 128, 1024)
    wol = wop.reshape(8, 128, 8, 128).transpose(2, 1, 0, 3).reshape(8 * 128, 1024)
    return {"ab_wql": np.ascontiguousarray(wql),
            "ab_wkT": np.ascontiguousarray(wk.T),
            "ab_wvT": np.ascontiguousarray(wv.T),
            "ab_wol": np.ascontiguousarray(wol),
            "ab_bo": np.ascontiguousarray(bo.reshape(8, 128).T),
            "ab_qaux": qaux.reshape(128, T)}


def emit_l0attnb(C, x_hbm, xo_hbm, W, xln_hbm, dbg=None):
    em = C.em
    TBL = 512
    NB = T // TBL
    nc = C.nc

    def dump(name, src, after):
        if dbg is not None and name in dbg:
            return em.dma("p", dbg[name], src, after=after)
        return None
    wk_sb = C.sb("abwk", [128, NCH * 512], BF16)
    wv_sb = C.sb("abwv", [128, NCH * 512], BF16)
    tok_w = None
    for kc in range(NCH):
        em.dma("p", wk_sb[:, kc * 512:(kc + 1) * 512],
               W["ab_wkT"][kc * 128:(kc + 1) * 128, :])
        tok_w = em.dma("p", wv_sb[:, kc * 512:(kc + 1) * 512],
                       W["ab_wvT"][kc * 128:(kc + 1) * 128, :])
    bo_sb = C.sb("abbo", [128, 8], F32)
    tok_w = em.dma("y", bo_sb[:], W["ab_bo"], after=[tok_w])
    wq2 = [C.sb("abwq2", [128, 1024], BF16) for _ in range(2)]
    wo2 = [C.sb("abwo2", [128, 1024], BF16) for _ in range(2)]

    aux = C.sb("abaux", [68, T], BF16)
    t2 = em.op("p", lambda g: g.iota(aux[0:1, :], [[1, 128], [0, 32]], base=0,
                                     channel_multiplier=0,
                                     allow_small_or_imprecise_dtypes=True))
    t2a = em.dma("p", aux[1:2, :], aux[0:1, :], after=[t2])
    t2b = em.dma("p", aux[65:66, :], aux[0:1, :], after=[t2])
    t3 = em.op("p", lambda g: g.iota(aux[0:1, :], [[0, 128], [1, 32]], base=0,
                                     channel_multiplier=0,
                                     allow_small_or_imprecise_dtypes=True),
               after=[t2a, t2b])
    t3a = em.dma("p", aux[2:3, :], aux[0:1, :], after=[t3])
    t3b = em.dma("p", aux[66:67, :], aux[0:1, :], after=[t3])
    t1 = em.op("v", lambda v: v.memset(aux[0:1, :], 1.0), after=[t3a, t3b])
    em.dma("p", aux[64:65, :], aux[0:1, :], after=[t1])
    em.dma("p", aux[3:4, :], aux[0:1, :], after=[t1])
    tok_aux = em.dma("p", aux[67:68, :], aux[0:1, :], after=[t1])

    xb = [C.sb("bxb", [128, TBL], BF16) for _ in range(NCH)]
    xln = [C.sb("bxln", [128, TBL], BF16) for _ in range(NCH)]
    kbuf = [C.sb("bk", [128, T], BF16) for _ in range(4)]
    vbar = C.sb("bvbar", [128, 32 * 520], BF16)
    qbuf = [C.sb("bq", [128, TBL], BF16) for _ in range(NCH)]
    aout = [C.sb("bao", [128, TBL], BF16) for _ in range(NCH)]
    pbuf = [C.sb("bp", [128, TBL], BF16) for _ in range(2)]
    qa_l = C.sb("bqal", [68, T], BF16)
    qa_u = C.sb("bqau", [68, T], BF16)
    sqs = C.sb("bsq", [128, TBL], BF16)
    kB = C.sb("bkB", [1, 8], F32)
    ksr = C.sb("bksr", [1, 16], F32)
    qB = C.sb("bqB", [1, 16], F32)
    bBr = C.sb("bbBr", [1, 16], F32)
    r11 = C.sb("br11", [1, TBL], F32)
    rb = C.sb("brb", [64, TBL], F32)
    bias_col = C.sb("bbc", [128, 1], F32)
    tmpf = C.sb("btmp", [128, TBL], F32)
    xfr = [C.sb("bxf", [128, TBL], F32) for _ in range(2)]
    osum = C.sb("bosum", [128, TBL], F32)

    tok_vones = em.op("v", lambda v: v.memset(vbar[:, 64::65], 1.0))
    tok_kinit = em.op("v", lambda v: v.memset(kB[:], -3e38), after=[])

    kn_last = tok_kinit
    st_last = None
    va_last = None
    prev_ph = []
    for blk in range(NB):
        xln_toks, _ = emit_ln(C, x_hbm, xb, xln, blk, TBL,
                              after=[tok_w, st_last, kn_last, va_last])
        st_toks = [em.dma("y", xln_hbm[128 * c:128 * (c + 1),
                                       blk * TBL:(blk + 1) * TBL],
                          xln[c][:, 0:TBL], after=[xln_toks[c]])
                   for c in range(NCH)]
        st_last = st_toks[-1]
        for ck in range(4):
            bank = C.pb[ck % 2]
            fns = [lambda t, kc=kc, ck=ck, bank=bank: t.matmul(
                bank[:, 0:TBL], wk_sb[:, kc * 512 + ck * 128:kc * 512 + (ck + 1) * 128],
                xln[kc][:, 0:TBL], start=(kc == 0), stop=(kc == NCH - 1))
                for kc in range(NCH)]
            tmm = em.ops("t", fns, after=xln_toks + [tok_w] + prev_ph)
            tev = em.op("v", lambda v, ck=ck, blk=blk, bank=bank: v.tensor_copy(
                kbuf[ck][:, blk * TBL:(blk + 1) * TBL], bank[:, 0:TBL]),
                after=[tmm])
            tsq = em.op("s", lambda s, ck=ck, blk=blk: s.activation(
                sqs[:, 0:TBL], kbuf[ck][:, blk * TBL:(blk + 1) * TBL],
                AF.Square), after=[tev])
            for half in range(2):
                kvh = 2 * ck + half
                bank2 = C.pb[4]
                tm2 = em.op("t", lambda t, half=half, bank2=bank2: t.matmul(
                    bank2[0:1, 0:TBL], C.ones_col_bf[64 * half:64 * half + 64, :],
                    sqs[64 * half:64 * half + 64, 0:TBL], start=True, stop=True),
                    after=[tsq, kn_last])
                tr = em.op("v", lambda v, bank2=bank2: v.tensor_reduce(
                    r11[:, 0:1], bank2[0:1, 0:TBL], AX.X, ALU.max), after=[tm2])
                kn_last = em.op("v", lambda v, kvh=kvh: v.tensor_max(
                    kB[:, kvh:kvh + 1], kB[:, kvh:kvh + 1], r11[:, 0:1]),
                    after=[tr])
            prev_ph = [tev]
        for ts_ in range(4):
            bank = C.pb[2 + ts_ % 2]
            fns = [lambda t, kc=kc, ts_=ts_, bank=bank: t.matmul(
                bank[:, 0:512], xln[kc][:, ts_ * 128:(ts_ + 1) * 128],
                wv_sb[:, kc * 512:(kc + 1) * 512],
                start=(kc == 0), stop=(kc == NCH - 1)) for kc in range(NCH)]
            tmm = em.ops("t", fns, after=xln_toks + [tok_w] + prev_ph)
            j = blk * 4 + ts_
            fns2 = [lambda v, kvh=kvh, j=j, bank=bank: v.tensor_copy(
                vbar[:, j * 520 + kvh * 65:j * 520 + kvh * 65 + 64],
                bank[:, kvh * 64:(kvh + 1) * 64]) for kvh in range(8)]
            va_last = em.ops("v", fns2, after=[tmm, tok_vones])
            prev_ph = [va_last]
    phaseA = [kn_last, va_last, st_last]
    o = em.op("s", lambda s: s.activation(kB[:], kB[:], AF.Sqrt),
              after=phaseA)
    o1 = em.op("v", lambda v: v.tensor_copy(ksr[:, 0::2], kB[:]), after=[o])
    tok_ksr = em.op("v", lambda v: v.tensor_copy(ksr[:, 1::2], kB[:]),
                    after=[o1])
    dump("ksr", ksr[:], [tok_ksr])
    dump("kb0", kbuf[0][:, 0:512], [phaseA[0]])

    pb_hist = {}
    pbuf_hist = [None, None]
    qa_hist = [None, None]
    qa_used = None
    wq_toks = [None, None]
    wq_used = [None, None]
    wo_toks = [None, None]
    wo_used = [None, None]
    aout_hist = None
    qb_hist = None
    prev_res = None
    for blk in range(NB):
        tq0 = blk * TBL
        xl_toks = [em.dma("p", xln[c][:, 0:TBL],
                          xln_hbm[128 * c:128 * (c + 1), tq0:tq0 + TBL],
                          after=[st_last, prev_res, phaseA[1]] +
                          ([qb_hist] if qb_hist else []))
                   for c in range(NCH)]
        q_toks = []
        for ch in range(NCH):
            buf = ch % 2
            wdep = [wq_used[buf]] if wq_used[buf] else []
            wq_toks[buf] = em.dma("p", wq2[buf][:],
                                  W["ab_wql"][ch * 128:(ch + 1) * 128, :],
                                  after=wdep)
            bank = C.pb[5 + ch % 2]
            fns = [lambda t, kc=kc, buf=buf, bank=bank: t.matmul(
                bank[:, 0:TBL], wq2[buf][:, kc * 128:(kc + 1) * 128],
                xln[kc][:, 0:TBL], start=(kc == 0), stop=(kc == NCH - 1))
                for kc in range(NCH)]
            dep = xl_toks + [wq_toks[buf], pb_hist.get(5 + ch % 2)]
            tmm = em.ops("t", fns, after=dep)
            wq_used[buf] = tmm
            tev = em.op("s", lambda s, ch=ch, bank=bank: s.activation(
                qbuf[ch][:, 0:TBL], bank[:, 0:TBL], AF.Copy, scale=0.125),
                after=[tmm] + ([qb_hist] if qb_hist else []))
            pb_hist[5 + ch % 2] = tev
            q_toks.append(tev)
            tsq = em.op("s", lambda s, ch=ch: s.activation(
                sqs[:, 0:TBL], qbuf[ch][:, 0:TBL], AF.Square), after=[tev])
            for half in range(2):
                h = ABQ_H0[ch] if half == 0 else ABQ_H1[ch]
                bank2 = C.pb[4]
                tm2 = em.op("t", lambda t, half=half, bank2=bank2: t.matmul(
                    bank2[0:1, 0:TBL], C.ones_col_bf[64 * half:64 * half + 64, :],
                    sqs[64 * half:64 * half + 64, 0:TBL], start=True, stop=True),
                    after=[tsq, pb_hist.get(4)])
                tr = em.op("v", lambda v, bank2=bank2: v.tensor_reduce(
                    r11[:, 0:1], bank2[0:1, 0:TBL], AX.X, ALU.max), after=[tm2])
                pb_hist[4] = tr
                q_toks.append(em.op("v", lambda v, h=h: v.tensor_copy(
                    qB[:, h:h + 1], r11[:, 0:1]), after=[tr]))
        o = em.op("s", lambda s: s.activation(qB[:], qB[:], AF.Sqrt),
                  after=q_toks)
        o = em.op("v", lambda v: v.tensor_mul(bBr[:], qB[:], ksr[:]),
                  after=[o, tok_ksr])
        tok_bB = em.op("v", lambda v: v.tensor_scalar_mul(bBr[:], bBr[:], -1.0),
                       after=[o])
        if blk == 0:
            dump("bBr", bBr[:], [tok_bB])
            dump("qb0", qbuf[0][:, 0:TBL], [tok_bB])
        for h in range(16):
            kvh = h // 2
            ck, kvp = divmod(kvh, 2)
            kb64 = 64 * kvp
            if kvp == 0:
                ch = ABQ_H0.index(h)
            else:
                ch = ABQ_H1.index(h)
            qb64 = kb64
            for sgn, qa in ((0, qa_l), (1, qa_u)):
                dep = [qa_used] if qa_used is not None else []
                em.dma("p", qa[0:4, :],
                       W["ab_qaux"][(h * 2 + sgn) * 4:(h * 2 + sgn) * 4 + 4, :],
                       after=dep)
                qa_hist[sgn] = em.dma("p", qa[64:68, :],
                                      W["ab_qaux"][(h * 2 + sgn) * 4:
                                                   (h * 2 + sgn) * 4 + 4, :])
            tb = em.op("t", lambda t, h=h: t.matmul(
                C.pb[4][:, 0:1], C.ones_row_f[:], bBr[0:1, h:h + 1],
                start=True, stop=True), after=[tok_bB, pb_hist.get(4)])
            tbc = em.op("v", lambda v: v.tensor_copy(bias_col[:], C.pb[4][:, 0:1]),
                        after=[tb])
            pb_hist[4] = tbc
            tiles = []
            for jc in range(0, blk * 4):
                tiles.append((jc, 0, False))
            for jc in range(blk * 4, blk * 4 + 4):
                tiles.append((jc, 0, True))
                tiles.append((jc, 1, True))
            for jc in range(blk * 4 + 4, 32):
                tiles.append((jc, 1, False))
            obank = C.pb[2 + h % 2]
            first_dep = pb_hist.get(2 + h % 2)
            nt = len(tiles)
            for i, (jc, sgn, masked) in enumerate(tiles):
                sbank_i = i % 2
                sbank = C.pb[sbank_i]
                qa = qa_l if sgn == 0 else qa_u
                auxb = 64 if kb64 == 0 else 0
                fns = [
                    lambda t, jc=jc, sbank=sbank, ck=ck, kb64=kb64, ch=ch, \
                    qb64=qb64: t.matmul(
                        sbank[:, 0:TBL],
                        kbuf[ck][kb64:kb64 + 64, jc * 128:(jc + 1) * 128],
                        qbuf[ch][qb64:qb64 + 64, 0:TBL],
                        start=True, stop=False),
                    lambda t, jc=jc, sbank=sbank, auxb=auxb, qa=qa, tq0=tq0: \
                    t.matmul(
                        sbank[:, 0:TBL], aux[auxb:auxb + 4, jc * 128:(jc + 1) * 128],
                        qa[auxb:auxb + 4, tq0:tq0 + TBL], start=False, stop=True),
                ]
                tmm = em.ops("t", fns,
                             after=[phaseA[0], phaseA[1], qa_hist[sgn], tok_aux,
                                    q_toks[-1], pb_hist.get(sbank_i)])
                pbi = i % 2
                texp = em.op("s", lambda s, pbi=pbi, sbank=sbank: s.activation(
                    pbuf[pbi][:, 0:TBL], sbank[:, 0:TBL], AF.Exp, bias=bias_col[:]),
                    after=[tmm, tbc] +
                    ([pbuf_hist[pbi]] if pbuf_hist[pbi] else []))
                pb_hist[sbank_i] = texp
                tlast = texp
                if masked:
                    if sgn == 0:
                        base, mult, step = tq0 - jc * 128, -1, 1
                    else:
                        base, mult, step = jc * 128 - tq0 - 1, 1, -1
                    tlast = em.op("p", lambda g, pbi=pbi, base=base, mult=mult,
                                  step=step:
                                  g.affine_select(pbuf[pbi][:, 0:TBL],
                                                  pbuf[pbi][:, 0:TBL],
                                                  [[step, TBL]], ALU.is_ge, 0.0,
                                                  base=base,
                                                  channel_multiplier=mult),
                                  after=[texp])
                tv = em.op("t", lambda t, jc=jc, i=i, pbi=pbi, obank=obank,
                           kvh=kvh, nt=nt:
                           t.matmul(obank[0:65, 0:TBL],
                                    vbar[:, jc * 520 + kvh * 65:
                                         jc * 520 + kvh * 65 + 65],
                                    pbuf[pbi][:, 0:TBL],
                                    start=(i == 0), stop=(i == nt - 1)),
                           after=[tlast] + ([first_dep] if i == 0 else []))
                pbuf_hist[pbi] = tv
                qa_used = tv
                if blk == 0 and h == 0 and i == 0:
                    dump("p00", pbuf[pbi][:, 0:TBL], [tlast])
                    if dbg is not None:
                        tcp = em.op("v", lambda v, sbank=sbank: v.tensor_copy(
                            tmpf[:, 0:TBL], sbank[:, 0:TBL]),
                            after=[tmm, texp])
                        pb_hist[sbank_i] = tcp
                        dump("s00", tmpf[:, 0:TBL], [tcp])
                        dump("aux00", aux[0:68, 0:512], [tok_aux])
                        dump("qa00", qa_l[0:68, 0:TBL], [qa_hist[0]])
            tr = em.op("v", lambda v, obank=obank: v.reciprocal(
                r11[:, 0:TBL], obank[64:65, 0:TBL]), after=[tv])
            if blk == 0 and h == 0:
                dump("rec0", r11[:, 0:TBL], [tr])
            tb2 = em.op("t", lambda t: t.matmul(
                C.pb[4][0:64, 0:TBL], C.ones_row_f[:, 0:64], r11[:, 0:TBL],
                start=True, stop=True), after=[tr, pb_hist.get(4)])
            tb3 = em.op("v", lambda v: v.tensor_copy(rb[:, 0:TBL],
                                                     C.pb[4][0:64, 0:TBL]),
                        after=[tb2])
            pb_hist[4] = tb3
            tao = em.op("v", lambda v, ch=ch, qb64=qb64, obank=obank:
                        v.tensor_mul(aout[ch][qb64:qb64 + 64, 0:TBL],
                                     obank[0:64, 0:TBL], rb[:, 0:TBL]),
                        after=[tb3, aout_hist])
            pb_hist[2 + h % 2] = tao
            if blk == 0 and h == 0:
                dump("ao0", aout[ch][:, 0:TBL], [tao])
            if blk == 0 and h == 15 and dbg is not None and "aoall" in dbg:
                for cc2 in range(NCH):
                    em.dma("p", dbg["aoall"][128 * cc2:128 * (cc2 + 1), :],
                           aout[cc2][:, 0:TBL], after=[tao])
        aout_last = tao
        res_toks = []
        for oc in range(NCH):
            buf = oc % 2
            wdep = [wo_used[buf]] if wo_used[buf] else []
            wo_toks[buf] = em.dma("p", wo2[buf][:],
                                  W["ab_wol"][oc * 128:(oc + 1) * 128, :],
                                  after=wdep)
            bank = C.pb[5 + oc % 2]
            fns = [lambda t, kc=kc, buf=buf, bank=bank: t.matmul(
                bank[:, 0:TBL], wo2[buf][:, kc * 128:(kc + 1) * 128],
                aout[kc][:, 0:TBL], start=(kc == 0), stop=(kc == NCH - 1))
                for kc in range(NCH)]
            tmm = em.ops("t", fns, after=[aout_last, wo_toks[buf],
                                          pb_hist.get(5 + oc % 2)])
            wo_used[buf] = tmm
            qb_hist = tmm
            xi = xfr[oc % 2]
            tx = em.dma("y", xi[:, 0:TBL],
                        x_hbm[128 * oc:128 * (oc + 1), tq0:tq0 + TBL])
            tv2 = em.op("v", lambda v, oc=oc, xi=xi, bank=bank:
                        v.scalar_tensor_tensor(osum[:, 0:TBL], bank[:, 0:TBL],
                                               bo_sb[:, oc:oc + 1],
                                               xi[:, 0:TBL], ALU.add, ALU.add),
                        after=[tmm, tx] + ([res_toks[-1]] if res_toks else []))
            pb_hist[5 + oc % 2] = tv2
            res_toks.append(em.dma("y", xo_hbm[128 * oc:128 * (oc + 1),
                                               tq0:tq0 + TBL],
                                   osum[:, 0:TBL], after=[tv2]))
        aout_hist = qb_hist
        prev_res = res_toks[-1]
    return res_toks



LQ_H0 = [0, 1, 2, 3, 8, 9, 10, 11]
LQ_H1 = [4, 5, 6, 7, 12, 13, 14, 15]


def _rope_tiles(base):
    hd = 64
    inv = 1.0 / base ** (np.arange(0, hd, 2, dtype=np.float32) / hd)
    tt = np.arange(T, dtype=np.float32)
    fr = tt[None, :] * inv[:, None]
    cos = np.cos(fr)
    sin = np.sin(fr)
    cosr = np.repeat(cos, 2, axis=0)
    sinr = np.empty((64, T), np.float32)
    sinr[0::2] = -sin
    sinr[1::2] = sin
    return (np.ascontiguousarray(np.tile(cosr, (2, 1))),
            np.ascontiguousarray(np.tile(sinr, (2, 1))))


def host_prep_l1attna(inputs):
    n1 = np.asarray(inputs["L1_attn_a_norm1_w"])
    n2 = np.asarray(inputs["L1_attn_a_norm2_w"])
    gwq = np.asarray(inputs["L1_attn_a_g_wq"])
    gkd = np.asarray(inputs["L1_attn_a_g_kv_down"])
    gku = np.asarray(inputs["L1_attn_a_g_k_up"])
    gvu = np.asarray(inputs["L1_attn_a_g_v_up"])
    gwo = np.asarray(inputs["L1_attn_a_g_wo"])
    lwq = np.asarray(inputs["L1_attn_a_l_wq"])
    lwk = np.asarray(inputs["L1_attn_a_l_wk"])
    lwv = np.asarray(inputs["L1_attn_a_l_wv"])
    lwo = np.asarray(inputs["L1_attn_a_l_wo"])
    perm = np.concatenate([np.r_[LQ_H0[c] * 64:(LQ_H0[c] + 1) * 64,
                                 LQ_H1[c] * 64:(LQ_H1[c] + 1) * 64]
                           for c in range(8)])

    def stream_layout(w):
        return np.ascontiguousarray(
            w.reshape(8, 128, 8, 128).transpose(2, 1, 0, 3).reshape(1024, 1024))

    cos_g, sin_g = _rope_tiles(1000000.0)
    cos_l, sin_l = _rope_tiles(10000.0)
    return {"la_gwql": stream_layout(gwq[:, perm]),
            "la_gwol": stream_layout(gwo[perm, :]),
            "la_lwql": stream_layout(lwq[:, perm]),
            "la_lwol": stream_layout(lwo[perm, :]),
            "la_gkd": np.ascontiguousarray(gkd),
            "la_gku": np.ascontiguousarray(gku),
            "la_gvu": np.ascontiguousarray(gvu),
            "la_lwk": np.ascontiguousarray(lwk),
            "la_lwv": np.ascontiguousarray(lwv),
            "la_n1": np.ascontiguousarray(n1.reshape(8, 128).T),
            "la_n2": np.ascontiguousarray(n2.reshape(8, 128).T),
            "la_cosg": cos_g, "la_sing": sin_g,
            "la_cosl": cos_l, "la_sinl": sin_l}


def emit_l1attna(C, x_hbm, xo_hbm, W, h0b_hbm, h0f_hbm, h1f_hbm, dbg=None):
    em = C.em
    nc = C.nc

    def dump(name, src, after):
        if dbg is not None and name in dbg:
            return em.dma("p", dbg[name], src, after=after)
        return None
    TBL = 512
    NB = T // TBL
    eps6 = C.sb("laeps6", [128, 1], F32)
    tok_e6 = em.op("v", lambda v: v.memset(eps6[:], 1e-6))
    bias_col = C.sb("labc", [128, 1], F32)
    tok_bc = em.op("v", lambda v: v.memset(bias_col[:], -0.13))

    xb = [C.sb("laxb", [128, TBL], BF16) for _ in range(NCH)]
    xln = [C.sb("laxln", [128, TBL], BF16) for _ in range(NCH)]
    xf32 = [C.sb("laxf3", [128, TBL], F32) for _ in range(2)]
    nb_ = [C.sb("lanb", [128, TBL], BF16) for _ in range(NCH)]
    lat = [C.sb("lalat", [128, TBL], BF16) for _ in range(2)]
    kg = [C.sb("lakg", [128, T], BF16) for _ in range(2)]
    vbar = C.sb("lavb", [128, 32 * 260], BF16)
    qg = [C.sb("laqg", [128, TBL], BF16) for _ in range(NCH)]
    aout = [C.sb("laao", [128, TBL], BF16) for _ in range(NCH)]
    pbuf = [C.sb("lapb", [128, TBL], BF16) for _ in range(2)]
    wstr = [C.sb("law", [128, 1024], BF16) for _ in range(2)]
    cosb = C.sb("lacos", [128, T], F32)
    sinb = C.sb("lasin", [128, T], F32)
    qr = C.sb("laqr", [128, TBL], F32)
    qs = C.sb("laqs", [128, TBL], F32)
    ta_ = C.sb("lata", [128, TBL], F32)
    rot = C.sb("larot", [128, TBL], F32)
    sqs = C.sb("lasq", [128, TBL], BF16)
    nrw = C.sb("lanrw", [1, TBL], F32)
    r11 = C.sb("lar11", [1, TBL], F32)
    rb = C.sb("larb", [64, TBL], F32)
    osum = C.sb("laos", [128, TBL], F32)
    xres = C.sb("laxres", [128, TBL], F32)
    n1c = C.sb("lan1c", [128, 8], F32)
    n2c = C.sb("lan2c", [128, 8], F32)
    gkd_sb = C.sb("lagkd", [128, NCH * 256], BF16)
    gku_sb = C.sb("lagku", [128, 2 * 256], BF16)
    gvu_sb = C.sb("lagvu", [128, 2 * 256], BF16)
    lwk_sb = C.sb("lalwk", [128, NCH * 256], BF16)
    lwv_sb = C.sb("lalwv", [128, NCH * 256], BF16)
    em.dma("y", n1c[:], W["la_n1"])
    tok_w = em.dma("y", n2c[:], W["la_n2"])
    for kc in range(NCH):
        em.dma("p", gkd_sb[:, kc * 256:(kc + 1) * 256],
               W["la_gkd"][kc * 128:(kc + 1) * 128, :])
        em.dma("p", lwk_sb[:, kc * 256:(kc + 1) * 256],
               W["la_lwk"][kc * 128:(kc + 1) * 128, :])
        tok_w = em.dma("p", lwv_sb[:, kc * 256:(kc + 1) * 256],
                       W["la_lwv"][kc * 128:(kc + 1) * 128, :])
    for kc in range(2):
        em.dma("p", gku_sb[:, kc * 256:(kc + 1) * 256],
               W["la_gku"][kc * 128:(kc + 1) * 128, :])
        tok_w = em.dma("p", gvu_sb[:, kc * 256:(kc + 1) * 256],
                       W["la_gvu"][kc * 128:(kc + 1) * 128, :])

    tok_vones = em.op("v", lambda v: v.memset(vbar[:, 64::65], 1.0))
    pb_hist = {}
    wstr_toks = [None, None]
    wstr_used = [None, None]

    st_last = None
    prev_sf = None
    for blk in range(NB):
        xln_toks, _ = emit_ln(C, x_hbm, xb, xln, blk, TBL,
                              after=[tok_w, st_last],
                              f32_hbm=h0f_hbm, f32_bufs=xf32)
        st_toks = [em.dma("y", h0b_hbm[128 * c:128 * (c + 1),
                                       blk * TBL:(blk + 1) * TBL],
                          xln[c][:, 0:TBL], after=[xln_toks[c]])
                   for c in range(NCH)]
        st_last = st_toks[-1]
    p0_done = st_last

    def rms_apply(src_hbm, blk, wcol, extra):
        t0 = blk * TBL
        ld = [em.dma("p", xb[c][:, 0:TBL],
                     src_hbm[128 * c:128 * (c + 1), t0:t0 + TBL],
                     after=extra) for c in range(NCH)]
        lns = C.ln_scratch()
        sq = lns["sq"]
        ps_q = C.pb[7]
        mm_toks = []
        for c in range(NCH):
            tsq = em.op("s", lambda s, c=c: s.activation(
                sq[:, 0:TBL], xb[c][:, 0:TBL], AF.Square),
                after=[ld[c]] + ([mm_toks[-1]] if c else list(extra)))
            mm_toks.append(em.op("t", lambda t, c=c: t.matmul(
                ps_q[0:1, 0:TBL], C.ones_col_bf[:], sq[:, 0:TBL],
                start=(c == 0), stop=(c == NCH - 1)),
                after=[ld[c], tsq, C.t_const, pb_hist.get(7)]))
        r_row = lns["r"][:, 0:TBL]
        tk1 = em.op("v", lambda v: v.tensor_scalar_mul(
            r_row, ps_q[0:1, 0:TBL], 1.0 / DM), after=[mm_toks[-1]])
        tk2 = em.op("s", lambda s: s.activation(r_row, r_row, AF.Sqrt,
                                                bias=eps6[0:1, :]),
                    after=[tk1, tok_e6])
        tk3 = em.op("v", lambda v: v.reciprocal(r_row, r_row), after=[tk2])
        tb1 = em.op("t", lambda t: t.matmul(ps_q[:, 0:TBL], C.ones_row_f[:],
                                            r_row, start=True, stop=True),
                    after=[tk3])
        pb_hist[7] = tb1
        out_toks = []
        for c in range(NCH):
            t1 = em.op("v", lambda v, c=c: v.tensor_mul(
                qr[:, 0:TBL], xb[c][:, 0:TBL], ps_q[:, 0:TBL]),
                after=[tb1] + ([out_toks[-1]] if c else []))
            out_toks.append(em.op("v", lambda v, c=c, wcol=wcol: v.tensor_scalar_mul(
                nb_[c][:, 0:TBL], qr[:, 0:TBL], wcol[:, c:c + 1]),
                after=[t1, tok_w]))
        pb_hist[7] = out_toks[-1]
        return out_toks, ld

    def rope_norm(bank, nparts, dst, dst_off, blk, inv8, extra):
        t0 = blk * TBL
        t1 = em.op("s", lambda s, bank=bank, nparts=nparts: s.copy(
            qr[0:nparts, 0:TBL], bank[0:nparts, 0:TBL]), after=extra)
        mask = [(i ^ 1) for i in range(32)]
        t2 = em.op("v", lambda v, nparts=nparts: v.stream_shuffle(
            qs[0:nparts, 0:TBL], qr[0:nparts, 0:TBL], mask), after=[t1])
        t3 = em.op("v", lambda v, nparts=nparts, t0=t0: v.tensor_mul(
            ta_[0:nparts, 0:TBL], qr[0:nparts, 0:TBL],
            cosb[0:nparts, t0:t0 + TBL]), after=[t1])
        t4 = em.op("v", lambda v, nparts=nparts, t0=t0: v.tensor_mul(
            qs[0:nparts, 0:TBL], qs[0:nparts, 0:TBL],
            sinb[0:nparts, t0:t0 + TBL]), after=[t2])
        t5 = em.op("v", lambda v, nparts=nparts: v.tensor_add(
            rot[0:nparts, 0:TBL], ta_[0:nparts, 0:TBL], qs[0:nparts, 0:TBL]),
            after=[t3, t4])
        t6 = em.op("s", lambda s, nparts=nparts: s.activation(
            sqs[0:nparts, 0:TBL], rot[0:nparts, 0:TBL], AF.Square),
            after=[t5])
        res = t6
        for half in range(nparts // 64):
            b = 64 * half
            t7 = em.op("t", lambda t, b=b: t.matmul(
                C.pb[4][0:1, 0:TBL], C.ones_col_bf[b:b + 64, :],
                sqs[b:b + 64, 0:TBL], start=True, stop=True),
                after=[t6, C.t_const, pb_hist.get(4)])
            t8 = em.op("v", lambda v: v.tensor_scalar_mul(
                nrw[:, 0:TBL], C.pb[4][0:1, 0:TBL], 1.0 / 64.0), after=[t7])
            t9 = em.op("s", lambda s: s.activation(nrw[:, 0:TBL], nrw[:, 0:TBL],
                                                   AF.Sqrt, bias=eps6[0:1, :]),
                       after=[t8, tok_e6])
            t10 = em.op("v", lambda v: v.reciprocal(nrw[:, 0:TBL],
                                                    nrw[:, 0:TBL]), after=[t9])
            if inv8:
                t10 = em.op("v", lambda v: v.tensor_scalar_mul(
                    nrw[:, 0:TBL], nrw[:, 0:TBL], 0.125), after=[t10])
            t11 = em.op("t", lambda t, b=b: t.matmul(
                C.pb[4][b:b + 64, 0:TBL], C.ones_row_f[:, 0:64],
                nrw[:, 0:TBL], start=True, stop=True), after=[t10])
            res = em.op("v", lambda v, b=b, dst=dst, dst_off=dst_off:
                        v.tensor_mul(dst[b:b + 64, dst_off:dst_off + TBL],
                                     rot[b:b + 64, 0:TBL],
                                     C.pb[4][b:b + 64, 0:TBL]), after=[t11])
            pb_hist[4] = res
        return res

    def attn_branch(local, src_hbm, res1_hbm, out_hbm, wcol, wql, wol,
                    kw_sb, kw_cols, vw_sb, cos_hbm, sin_hbm, use_latent,
                    extra0):
        dpre = "ll" if local else "lg"
        """One attention branch; writes out_hbm = res1 + attn_out (+x for
        local, handled by caller via res streams).  Returns last token."""
        tok_cs = em.dma("y", cosb[:], cos_hbm, after=extra0)
        tok_cs = em.dma("y", sinb[:], sin_hbm, after=[tok_cs])
        pbuf_hist = [None, None]
        aout_hist = None
        qb_hist = None
        prev_res = None
        kv_guard = []
        x2_used = None
        for blk in range(NB):
            tq0 = blk * TBL
            nm_toks, _ = rms_apply(src_hbm, blk, wcol,
                                   [tok_cs, prev_res, p0_done] + kv_guard +
                                   ([qb_hist] if qb_hist else []))
            lat_toks = nm_toks
            if use_latent:
                lat_toks = []
                for lc in range(2):
                    bank = C.pb[lc % 2]
                    fns = [lambda t, kc=kc, lc=lc, bank=bank: t.matmul(
                        bank[:, 0:TBL],
                        gkd_sb[:, kc * 256 + lc * 128:kc * 256 + (lc + 1) * 128],
                        nb_[kc][:, 0:TBL], start=(kc == 0), stop=(kc == NCH - 1))
                        for kc in range(NCH)]
                    tmm = em.ops("t", fns, after=nm_toks + [tok_w,
                                                            pb_hist.get(lc % 2)])
                    lat_toks.append(em.op("v", lambda v, lc=lc, bank=bank:
                                          v.tensor_copy(lat[lc][:, 0:TBL],
                                                        bank[:, 0:TBL]),
                                          after=[tmm]))
                    pb_hist[lc % 2] = lat_toks[-1]
                ksrc = lat
                k_nch = 2
                kk_sb = gku_sb
            else:
                ksrc = nb_
                k_nch = NCH
                kk_sb = kw_sb
            for ck in range(2):
                bank = C.pb[2 + ck % 2]
                fns = [lambda t, kc=kc, ck=ck, bank=bank, kk_sb=kk_sb,
                       k_nch=k_nch: t.matmul(
                           bank[:, 0:TBL],
                           kk_sb[:, kc * 256 + ck * 128:kc * 256 + (ck + 1) * 128],
                           ksrc[kc][:, 0:TBL], start=(kc == 0),
                           stop=(kc == k_nch - 1)) for kc in range(k_nch)]
                tmm = em.ops("t", fns, after=(lat_toks if use_latent else nm_toks)
                             + [tok_w, pb_hist.get(2 + ck % 2)])
                tkn = rope_norm(bank, 128, kg[ck], blk * TBL, blk, False,
                                [tmm, tok_cs])
                pb_hist[2 + ck % 2] = tkn
            for ts_ in range(4):
                bank = C.pb[2 + ts_ % 2]
                vsrc = lat if use_latent else nb_
                v_nch = 2 if use_latent else NCH
                vw = gvu_sb if use_latent else vw_sb
                fns = [lambda t, kc=kc, ts_=ts_, bank=bank, vw=vw, vsrc=vsrc,
                       v_nch=v_nch: t.matmul(
                           bank[:, 0:256], vsrc[kc][:, ts_ * 128:(ts_ + 1) * 128],
                           vw[:, kc * 256:(kc + 1) * 256],
                           start=(kc == 0), stop=(kc == v_nch - 1))
                       for kc in range(v_nch)]
                tmm = em.ops("t", fns,
                             after=(lat_toks if use_latent else nm_toks) +
                             [tok_w, pb_hist.get(2 + ts_ % 2)])
                j = blk * 4 + ts_
                fns2 = [lambda v, hh=hh, j=j, bank=bank: v.tensor_copy(
                    vbar[:, j * 260 + hh * 65:j * 260 + hh * 65 + 64],
                    bank[:, hh * 64:(hh + 1) * 64]) for hh in range(4)]
                tvv = em.ops("v", fns2, after=[tmm, tok_vones])
                pb_hist[2 + ts_ % 2] = tvv
            kv_ready = [tkn, tvv]
            if blk == 0:
                for c_ in range(NCH):
                    dump(dpre + "n2_%d" % c_, nb_[c_][:, 0:TBL], nm_toks)
                dump(dpre + "kg0", kg[0][:, 0:TBL], [tkn])
                dump(dpre + "vb0", vbar[0:128, 0:260], [tvv])
            q_last = None
            for ch in range(NCH):
                buf = ch % 2
                wdep = [wstr_used[buf]] if wstr_used[buf] else []
                wstr_toks[buf] = em.dma("p", wstr[buf][:],
                                        wql[ch * 128:(ch + 1) * 128, :],
                                        after=wdep)
                bank = C.pb[ch % 2]
                fns = [lambda t, kc=kc, buf=buf, bank=bank: t.matmul(
                    bank[:, 0:TBL], wstr[buf][:, kc * 128:(kc + 1) * 128],
                    nb_[kc][:, 0:TBL], start=(kc == 0), stop=(kc == NCH - 1))
                    for kc in range(NCH)]
                tmm = em.ops("t", fns, after=nm_toks + [wstr_toks[buf],
                                                        pb_hist.get(ch % 2)])
                wstr_used[buf] = tmm
                q_last = rope_norm(bank, 128, qg[ch], 0, blk, True,
                                   [tmm, tok_cs])
                pb_hist[ch % 2] = q_last
            for h in range(16):
                kvh = h // 4
                ck, kvp = divmod(kvh, 2)
                b64 = 64 * kvp
                if kvp == 0:
                    ch = LQ_H0.index(h)
                else:
                    ch = LQ_H1.index(h)
                jlo = 0 if not local else max(blk * 4 - 2, 0)
                njc = blk * 4 + 4
                obank = C.pb[2 + h % 2]
                first_dep = pb_hist.get(2 + h % 2)
                idxs = list(range(jlo, njc))
                for ii, jc in enumerate(idxs):
                    sbank_i = ii % 2
                    sbank = C.pb[sbank_i]
                    tmm = em.op("t", lambda t, jc=jc, sbank=sbank, ck=ck,
                                b64=b64, ch=ch: t.matmul(
                                    sbank[:, 0:TBL],
                                    kg[ck][b64:b64 + 64, jc * 128:(jc + 1) * 128],
                                    qg[ch][b64:b64 + 64, 0:TBL],
                                    start=True, stop=True),
                                after=[q_last, pb_hist.get(sbank_i)] + kv_ready)
                    pbi = ii % 2
                    texp = em.op("s", lambda s, pbi=pbi, sbank=sbank:
                                 s.activation(pbuf[pbi][:, 0:TBL],
                                              sbank[:, 0:TBL], AF.Exp,
                                              bias=bias_col[:]),
                                 after=[tmm, tok_bc] +
                                 ([pbuf_hist[pbi]] if pbuf_hist[pbi] else []))
                    pb_hist[sbank_i] = texp
                    tlast = texp
                    if jc >= blk * 4:
                        tlast = em.op("p", lambda g, pbi=pbi, base=tq0 - jc * 128:
                                      g.affine_select(pbuf[pbi][:, 0:TBL],
                                                      pbuf[pbi][:, 0:TBL],
                                                      [[1, TBL]], ALU.is_ge,
                                                      0.0, base=base,
                                                      channel_multiplier=-1),
                                      after=[tlast])
                    if local and jc * 128 < tq0 + TBL - 255:
                        tlast = em.op("p", lambda g, pbi=pbi,
                                      base=jc * 128 - tq0 + 255:
                                      g.affine_select(pbuf[pbi][:, 0:TBL],
                                                      pbuf[pbi][:, 0:TBL],
                                                      [[-1, TBL]], ALU.is_ge,
                                                      0.0, base=base,
                                                      channel_multiplier=1),
                                      after=[tlast])
                    tv = em.op("t", lambda t, jc=jc, ii=ii, pbi=pbi,
                               obank=obank, kvh=kvh, n=len(idxs): t.matmul(
                                   obank[0:65, 0:TBL],
                                   vbar[:, jc * 260 + kvh * 65:
                                        jc * 260 + kvh * 65 + 65],
                                   pbuf[pbi][:, 0:TBL],
                                   start=(ii == 0), stop=(ii == n - 1)),
                               after=[tlast] + ([first_dep] if ii == 0 else []))
                    pbuf_hist[pbi] = tv
                kv_guard = [tv]
                tr = em.op("v", lambda v, obank=obank: v.reciprocal(
                    r11[:, 0:TBL], obank[64:65, 0:TBL]), after=[tv])
                tb2 = em.op("t", lambda t: t.matmul(
                    C.pb[4][0:64, 0:TBL], C.ones_row_f[:, 0:64], r11[:, 0:TBL],
                    start=True, stop=True), after=[tr, pb_hist.get(4)])
                tb3 = em.op("v", lambda v: v.tensor_copy(
                    rb[:, 0:TBL], C.pb[4][0:64, 0:TBL]), after=[tb2])
                pb_hist[4] = tb3
                tao = em.op("v", lambda v, ch=ch, b64=b64, obank=obank:
                            v.tensor_mul(aout[ch][b64:b64 + 64, 0:TBL],
                                         obank[0:64, 0:TBL], rb[:, 0:TBL]),
                            after=[tb3, aout_hist])
                pb_hist[2 + h % 2] = tao
                if blk == 0 and h == 15:
                    for c_ in range(NCH):
                        dump(dpre + "ao_%d" % c_, aout[c_][:, 0:TBL], [tao])
                if blk == 0 and h == 0:
                    dump(dpre + "p0", pbuf[pbi][:, 0:TBL], [tlast])
                    dump(dpre + "q0", qg[0][:, 0:TBL], [q_last])
            aout_last = tao
            res_toks = []
            for oc in range(NCH):
                buf = oc % 2
                wdep = [wstr_used[buf]] if wstr_used[buf] else []
                wstr_toks[buf] = em.dma("p", wstr[buf][:],
                                        wol[oc * 128:(oc + 1) * 128, :],
                                        after=wdep)
                bank = C.pb[5 + oc % 2]
                fns = [lambda t, kc=kc, buf=buf, bank=bank: t.matmul(
                    bank[:, 0:TBL], wstr[buf][:, kc * 128:(kc + 1) * 128],
                    aout[kc][:, 0:TBL], start=(kc == 0), stop=(kc == NCH - 1))
                    for kc in range(NCH)]
                tmm = em.ops("t", fns, after=[aout_last, wstr_toks[buf],
                                              pb_hist.get(5 + oc % 2)])
                wstr_used[buf] = tmm
                qb_hist = tmm
                if blk == 0 and oc == 0 and dbg is not None:
                    tcp9 = em.op("v", lambda v, bank=bank: v.tensor_copy(
                        rot[:, 0:TBL], bank[:, 0:TBL]), after=[tmm])
                    pb_hist[5 + oc % 2] = tcp9
                    dump(dpre + "wops", rot[:, 0:TBL], [tcp9])
                    dump(dpre + "wstr0", wstr[0][:, 0:1024], [tcp9])
                xi = xf32[oc % 2]
                dump_resid = (blk == 0 and oc == 0 and dbg is not None)
                tx = em.dma("y", xi[:, 0:TBL],
                            res1_hbm[128 * oc:128 * (oc + 1), tq0:tq0 + TBL])
                if local:
                    tx2 = em.dma("y", xres[:, 0:TBL],
                                 x_hbm[128 * oc:128 * (oc + 1), tq0:tq0 + TBL],
                                 after=[x2_used] if x2_used else [])
                    tv1 = em.op("v", lambda v, xi=xi, bank=bank: v.tensor_add(
                        osum[:, 0:TBL], xi[:, 0:TBL], bank[:, 0:TBL]),
                        after=[tmm, tx] + ([res_toks[-1]] if res_toks else []))
                    tv2 = em.op("v", lambda v: v.tensor_add(
                        osum[:, 0:TBL], osum[:, 0:TBL], xres[:, 0:TBL]),
                        after=[tv1, tx2])
                    x2_used = tv2
                else:
                    tv2 = em.op("v", lambda v, xi=xi, bank=bank: v.tensor_add(
                        osum[:, 0:TBL], xi[:, 0:TBL], bank[:, 0:TBL]),
                        after=[tmm, tx] + ([res_toks[-1]] if res_toks else []))
                pb_hist[5 + oc % 2] = tv2
                if dump_resid:
                    dump(dpre + "xi0", xi[:, 0:TBL], [tv2])
                    dump(dpre + "os0", osum[:, 0:TBL], [tv2])
                    if local:
                        dump(dpre + "qr0", qr[:, 0:TBL], [tv2])
                res_toks.append(em.dma("y", out_hbm[128 * oc:128 * (oc + 1),
                                                    tq0:tq0 + TBL],
                                       osum[:, 0:TBL], after=[tv2]))
            aout_hist = qb_hist
            prev_res = res_toks[-1]
        return res_toks

    res1 = attn_branch(False, h0b_hbm, h0f_hbm, h1f_hbm, n1c,
                       W["la_gwql"], W["la_gwol"], None, None, None,
                       W["la_cosg"], W["la_sing"], True, [p0_done])
    res2 = attn_branch(True, h1f_hbm, h1f_hbm, xo_hbm, n2c,
                       W["la_lwql"], W["la_lwol"], lwk_sb, None, lwv_sb,
                       W["la_cosl"], W["la_sinl"], False, [res1[-1]])
    return res2


def build_l1attna_program(debug=False):
    nc = bass.Bass("TRN2", target_bir_lowering=False)
    x_in = nc.dram_tensor("x_in", [DM, T], F32, kind="ExternalInput").ap()
    x_out = nc.dram_tensor("x_out", [DM, T], F32, kind="ExternalOutput").ap()
    W = {}
    for name, shape in [("la_gwql", [1024, 1024]), ("la_gwol", [1024, 1024]),
                        ("la_lwql", [1024, 1024]), ("la_lwol", [1024, 1024]),
                        ("la_gkd", [1024, 256]), ("la_gku", [256, 256]),
                        ("la_gvu", [256, 256]), ("la_lwk", [1024, 256]),
                        ("la_lwv", [1024, 256]), ("la_n1", [128, 8]),
                        ("la_n2", [128, 8]), ("la_cosg", [128, T]),
                        ("la_sing", [128, T]), ("la_cosl", [128, T]),
                        ("la_sinl", [128, T])]:
        W[name] = nc.dram_tensor(name, shape, F32, kind="ExternalInput").ap()
    kind = "ExternalOutput" if debug else "Internal"
    h0b = nc.dram_tensor("h0b_s", [DM, T], BF16, kind="Internal").ap()
    h0f = nc.dram_tensor("h0f_s", [DM, T], F32, kind=kind).ap()
    h1f = nc.dram_tensor("h1f_s", [DM, T], F32, kind=kind).ap()
    C = Core(nc)
    dbg = None
    if debug:
        dbg = {}
        for pre in ("lg", "ll"):
            for c_ in range(NCH):
                dbg[pre + "n2_%d" % c_] = nc.dram_tensor(
                    "dbg_" + pre + "n2_%d" % c_, [128, 512], F32,
                    kind="ExternalOutput").ap()
                dbg[pre + "ao_%d" % c_] = nc.dram_tensor(
                    "dbg_" + pre + "ao_%d" % c_, [128, 512], F32,
                    kind="ExternalOutput").ap()
            for nm, sh in [("kg0", [128, 512]), ("vb0", [128, 260]),
                           ("p0", [128, 512]), ("q0", [128, 512]),
                           ("wops", [128, 512]), ("wstr0", [128, 1024]),
                           ("xi0", [128, 512]), ("os0", [128, 512]),
                           ("qr0", [128, 512])]:
                dbg[pre + nm] = nc.dram_tensor("dbg_" + pre + nm, sh, F32,
                                               kind="ExternalOutput").ap()
    res = emit_l1attna(C, x_in, x_out, W, h0b, h0f, h1f, dbg=dbg)
    C.em.finalize(res)
    return nc



def host_prep_l1attnb(inputs):
    wqkv = np.asarray(inputs["L1_attn_b_w_qkv"])
    bqkv = np.asarray(inputs["L1_attn_b_b_qkv"])
    wout = np.asarray(inputs["L1_attn_b_w_out"])
    bout = np.asarray(inputs["L1_attn_b_b_out"])

    def stream_layout(w):
        return np.ascontiguousarray(
            w.reshape(8, 128, 8, 128).transpose(2, 1, 0, 3).reshape(1024, 1024))

    return {"lb_wql": stream_layout(wqkv[:, 0:1024]),
            "lb_wkl": stream_layout(wqkv[:, 1024:2048]),
            "lb_wv": np.ascontiguousarray(wqkv[:, 2048:3072]),
            "lb_wol": stream_layout(wout),
            "lb_bq": np.ascontiguousarray(bqkv[0:1024].reshape(8, 128).T),
            "lb_bk": np.ascontiguousarray(bqkv[1024:2048].reshape(8, 128).T),
            "lb_bv": np.ascontiguousarray(bqkv[None, 2048:3072]),
            "lb_bo": np.ascontiguousarray(bout.reshape(8, 128).T)}


def emit_l1attnb(C, x_hbm, xo_hbm, W, xln_hbm, k_hbm, vb_hbm):
    em = C.em
    TBL = 512
    NB = T // TBL
    wv_sb = C.sb("lbwv", [128, NCH * 1024], BF16)
    tok_w = None
    for kc in range(NCH):
        tok_w = em.dma("p", wv_sb[:, kc * 1024:(kc + 1) * 1024],
                       W["lb_wv"][kc * 128:(kc + 1) * 128, :])
    bq_sb = C.sb("lbbq", [128, 8], F32)
    bk_sb = C.sb("lbbk", [128, 8], F32)
    bv_sb = C.sb("lbbv", [1, 1024], BF16)
    bo_sb = C.sb("lbbo", [128, 8], F32)
    em.dma("y", bq_sb[:], W["lb_bq"])
    em.dma("y", bk_sb[:], W["lb_bk"])
    em.dma("p", bv_sb[:], W["lb_bv"])
    tok_w = em.dma("y", bo_sb[:], W["lb_bo"], after=[tok_w])

    xb = [C.sb("lbxb", [128, TBL], BF16) for _ in range(NCH)]
    xln = [C.sb("lbxln", [128, TBL], BF16) for _ in range(NCH)]
    qbuf = [C.sb("lbq", [128, TBL], BF16) for _ in range(NCH)]
    kbl = [C.sb("lbkb", [128, 768], BF16) for _ in range(NCH)]
    vbl = C.sb("lbvb", [128, 6 * 1040], BF16)
    aout = [C.sb("lbao", [128, TBL], BF16) for _ in range(NCH)]
    pbuf = [C.sb("lbp", [128, TBL], BF16) for _ in range(2)]
    wstr = [C.sb("lbw", [128, 1024], BF16) for _ in range(2)]
    sqs = C.sb("lbsq", [128, TBL], BF16)
    kB = C.sb("lbkB", [1, 16], F32)
    qB = C.sb("lbqB", [1, 16], F32)
    bBr = C.sb("lbbBr", [1, 16], F32)
    r11 = C.sb("lbr11", [1, TBL], F32)
    rb = C.sb("lbrb", [64, TBL], F32)
    bias_col = C.sb("lbbc", [128, 1], F32)
    osum = C.sb("lbos", [128, TBL], F32)
    xfr = [C.sb("lbxf", [128, TBL], F32) for _ in range(2)]
    vsc = C.sb("lbvsc", [128, 1040], BF16)

    tok_ki = em.op("v", lambda v: v.memset(kB[:], -3e38))
    tok_vo = em.op("v", lambda v: v.memset(vsc[:, 64::65], 1.0))

    pb_hist = {}
    wstr_toks = [None, None]
    wstr_used = [None, None]

    st_last = None
    kn_last = tok_ki
    ph_last = []
    for blk in range(NB):
        xln_toks, _ = emit_ln(C, x_hbm, xb, xln, blk, TBL,
                              after=[tok_w, st_last, kn_last] + ph_last)
        st_toks = [em.dma("y", xln_hbm[128 * c:128 * (c + 1),
                                       blk * TBL:(blk + 1) * TBL],
                          xln[c][:, 0:TBL], after=[xln_toks[c]])
                   for c in range(NCH)]
        st_last = st_toks[-1]
        for ck in range(NCH):
            buf = ck % 2
            wdep = [wstr_used[buf]] if wstr_used[buf] else []
            wstr_toks[buf] = em.dma("p", wstr[buf][:],
                                    W["lb_wkl"][ck * 128:(ck + 1) * 128, :],
                                    after=wdep)
            bank = C.pb[ck % 2]
            fns = [lambda t, kc=kc, buf=buf, bank=bank: t.matmul(
                bank[:, 0:TBL], wstr[buf][:, kc * 128:(kc + 1) * 128],
                xln[kc][:, 0:TBL], start=(kc == 0), stop=(kc == NCH - 1))
                for kc in range(NCH)]
            tmm = em.ops("t", fns, after=xln_toks + [wstr_toks[buf],
                                                     pb_hist.get(ck % 2)])
            wstr_used[buf] = tmm
            tev = em.op("v", lambda v, ck=ck, bank=bank: v.tensor_scalar_add(
                sqs[:, 0:TBL], bank[:, 0:TBL], bk_sb[:, ck:ck + 1]),
                after=[tmm, tok_w, kn_last])
            pb_hist[ck % 2] = tev
            tst = em.dma("y", k_hbm[128 * ck:128 * (ck + 1),
                                    blk * TBL:(blk + 1) * TBL],
                         sqs[:, 0:TBL], after=[tev])
            tsq = em.op("s", lambda s: s.activation(sqs[:, 0:TBL],
                                                    sqs[:, 0:TBL], AF.Square),
                        after=[tev, tst])
            for half in range(2):
                h = 2 * ck + half
                bank2 = C.pb[4]
                tm2 = em.op("t", lambda t, half=half, bank2=bank2: t.matmul(
                    bank2[0:1, 0:TBL], C.ones_col_bf[64 * half:64 * half + 64, :],
                    sqs[64 * half:64 * half + 64, 0:TBL], start=True, stop=True),
                    after=[tsq, pb_hist.get(4), C.t_const])
                tr = em.op("v", lambda v, bank2=bank2: v.tensor_reduce(
                    r11[:, 0:1], bank2[0:1, 0:TBL], AX.X, ALU.max), after=[tm2])
                pb_hist[4] = tr
                kn_last = em.op("v", lambda v, h=h: v.tensor_max(
                    kB[:, h:h + 1], kB[:, h:h + 1], r11[:, 0:1]), after=[tr])
        for ts_ in range(4):
            for vc in range(2):
                bank = C.pb[2 + (2 * ts_ + vc) % 2]
                fns = [lambda t, kc=kc, ts_=ts_, vc=vc, bank=bank: t.matmul(
                    bank[:, 0:512], xln[kc][:, ts_ * 128:(ts_ + 1) * 128],
                    wv_sb[:, kc * 1024 + vc * 512:kc * 1024 + (vc + 1) * 512],
                    start=(kc == 0), stop=False) for kc in range(NCH)]
                fns.append(lambda t, vc=vc, bank=bank: t.matmul(
                    bank[:, 0:512], C.ones_row_bf[:],
                    bv_sb[:, vc * 512:(vc + 1) * 512], start=False, stop=True))
                tmm = em.ops("t", fns, after=xln_toks + [tok_w, C.t_const] +
                             ph_last)
                fns2 = [lambda v, hh=hh, vc=vc, bank=bank: v.tensor_copy(
                    vsc[:, (8 * vc + hh) * 65:(8 * vc + hh) * 65 + 64],
                    bank[:, hh * 64:(hh + 1) * 64]) for hh in range(8)]
                tvv = em.ops("v", fns2, after=[tmm, tok_vo, ph_last[-1] if ph_last else None])
                j = blk * 4 + ts_
                tvs = em.dma("y", vb_hbm[:, j * 1040:(j + 1) * 1040], vsc[:],
                             after=[tvv])
                ph_last = [tvv, tvs]
    phaseA = [kn_last, st_last] + ph_last
    o = em.op("s", lambda s: s.activation(kB[:], kB[:], AF.Sqrt), after=phaseA)
    tok_ks = o

    pbuf_hist = [None, None]
    aout_hist = None
    prev_res = None
    qb_hist = None
    kv_used = None
    for blk in range(NB):
        tq0 = blk * TBL
        jlo = min(max(blk * 4 - 1, 0), 26)
        xl_toks = [em.dma("p", xln[c][:, 0:TBL],
                          xln_hbm[128 * c:128 * (c + 1), tq0:tq0 + TBL],
                          after=[st_last, prev_res] + phaseA +
                          ([qb_hist] if qb_hist else []))
                   for c in range(NCH)]
        kv_toks = [em.dma("y", kbl[ck][:],
                          k_hbm[128 * ck:128 * (ck + 1),
                                jlo * 128:jlo * 128 + 768],
                          after=[kv_used] + phaseA) for ck in range(NCH)]
        kv_toks.append(em.dma("y", vbl[:],
                              vb_hbm[:, jlo * 1040:jlo * 1040 + 6 * 1040],
                              after=[kv_used] + phaseA))
        q_toks = []
        for ch in range(NCH):
            buf = ch % 2
            wdep = [wstr_used[buf]] if wstr_used[buf] else []
            wstr_toks[buf] = em.dma("p", wstr[buf][:],
                                    W["lb_wql"][ch * 128:(ch + 1) * 128, :],
                                    after=wdep)
            bank = C.pb[5 + ch % 2]
            fns = [lambda t, kc=kc, buf=buf, bank=bank: t.matmul(
                bank[:, 0:TBL], wstr[buf][:, kc * 128:(kc + 1) * 128],
                xln[kc][:, 0:TBL], start=(kc == 0), stop=(kc == NCH - 1))
                for kc in range(NCH)]
            tmm = em.ops("t", fns, after=xl_toks + [wstr_toks[buf],
                                                    pb_hist.get(5 + ch % 2)])
            wstr_used[buf] = tmm
            tev = em.op("v", lambda v, ch=ch, bank=bank: v.tensor_scalar(
                qbuf[ch][:, 0:TBL], bank[:, 0:TBL], bq_sb[:, ch:ch + 1], 0.125,
                ALU.add, ALU.mult), after=[tmm, tok_w] +
                ([qb_hist] if qb_hist else []))
            pb_hist[5 + ch % 2] = tev
            tsq = em.op("s", lambda s, ch=ch: s.activation(
                sqs[:, 0:TBL], qbuf[ch][:, 0:TBL], AF.Square), after=[tev])
            for half in range(2):
                h = 2 * ch + half
                bank2 = C.pb[4]
                tm2 = em.op("t", lambda t, half=half, bank2=bank2: t.matmul(
                    bank2[0:1, 0:TBL], C.ones_col_bf[64 * half:64 * half + 64, :],
                    sqs[64 * half:64 * half + 64, 0:TBL], start=True, stop=True),
                    after=[tsq, pb_hist.get(4)])
                tr = em.op("v", lambda v, bank2=bank2: v.tensor_reduce(
                    r11[:, 0:1], bank2[0:1, 0:TBL], AX.X, ALU.max), after=[tm2])
                pb_hist[4] = tr
                q_toks.append(em.op("v", lambda v, h=h: v.tensor_copy(
                    qB[:, h:h + 1], r11[:, 0:1]), after=[tr]))
        o = em.op("s", lambda s: s.activation(qB[:], qB[:], AF.Sqrt),
                  after=q_toks)
        o = em.op("v", lambda v: v.tensor_mul(bBr[:], qB[:], kB[:]),
                  after=[o, tok_ks])
        tok_bB = em.op("v", lambda v: v.tensor_scalar_mul(bBr[:], bBr[:], -1.0),
                       after=[o])
        for h in range(16):
            ch, hp = divmod(h, 2)
            b64 = 64 * hp
            tb = em.op("t", lambda t, h=h: t.matmul(
                C.pb[4][:, 0:1], C.ones_row_f[:], bBr[0:1, h:h + 1],
                start=True, stop=True), after=[tok_bB, pb_hist.get(4)])
            tbc = em.op("v", lambda v: v.tensor_copy(bias_col[:],
                                                     C.pb[4][:, 0:1]),
                        after=[tb])
            pb_hist[4] = tbc
            obank = C.pb[2 + h % 2]
            first_dep = pb_hist.get(2 + h % 2)
            for i in range(6):
                jc = jlo + i
                sbank_i = i % 2
                sbank = C.pb[sbank_i]
                tmm = em.op("t", lambda t, i=i, sbank=sbank, ch=ch, b64=b64:
                            t.matmul(sbank[:, 0:TBL],
                                     kbl[ch][b64:b64 + 64, i * 128:(i + 1) * 128],
                                     qbuf[ch][b64:b64 + 64, 0:TBL],
                                     start=True, stop=True),
                            after=[q_toks[-1], pb_hist.get(sbank_i)] + kv_toks)
                pbi = i % 2
                texp = em.op("s", lambda s, pbi=pbi, sbank=sbank: s.activation(
                    pbuf[pbi][:, 0:TBL], sbank[:, 0:TBL], AF.Exp,
                    bias=bias_col[:]),
                    after=[tmm, tbc] +
                    ([pbuf_hist[pbi]] if pbuf_hist[pbi] else []))
                pb_hist[sbank_i] = texp
                tm1 = em.op("p", lambda g, pbi=pbi, jc=jc, tq0=tq0:
                            g.affine_select(pbuf[pbi][:, 0:TBL],
                                            pbuf[pbi][:, 0:TBL],
                                            [[1, TBL]], ALU.is_ge, 0.0,
                                            base=tq0 - jc * 128 + 64,
                                            channel_multiplier=-1),
                            after=[texp])
                tlast = em.op("p", lambda g, pbi=pbi, jc=jc, tq0=tq0:
                              g.affine_select(pbuf[pbi][:, 0:TBL],
                                              pbuf[pbi][:, 0:TBL],
                                              [[-1, TBL]], ALU.is_ge, 0.0,
                                              base=jc * 128 - tq0 + 64,
                                              channel_multiplier=1),
                              after=[tm1])
                tv = em.op("t", lambda t, i=i, pbi=pbi, obank=obank, h=h:
                           t.matmul(obank[0:65, 0:TBL],
                                    vbl[:, i * 1040 + h * 65:
                                        i * 1040 + h * 65 + 65],
                                    pbuf[pbi][:, 0:TBL],
                                    start=(i == 0), stop=(i == 5)),
                           after=[tlast] + ([first_dep] if i == 0 else []))
                pbuf_hist[pbi] = tv
                kv_used = tv
            tr = em.op("v", lambda v, obank=obank: v.reciprocal(
                r11[:, 0:TBL], obank[64:65, 0:TBL]), after=[tv])
            tb2 = em.op("t", lambda t: t.matmul(
                C.pb[4][0:64, 0:TBL], C.ones_row_f[:, 0:64], r11[:, 0:TBL],
                start=True, stop=True), after=[tr, pb_hist.get(4)])
            tb3 = em.op("v", lambda v: v.tensor_copy(rb[:, 0:TBL],
                                                     C.pb[4][0:64, 0:TBL]),
                        after=[tb2])
            pb_hist[4] = tb3
            tao = em.op("v", lambda v, ch=ch, b64=b64, obank=obank:
                        v.tensor_mul(aout[ch][b64:b64 + 64, 0:TBL],
                                     obank[0:64, 0:TBL], rb[:, 0:TBL]),
                        after=[tb3, aout_hist])
            pb_hist[2 + h % 2] = tao
        aout_last = tao
        res_toks = []
        for oc in range(NCH):
            buf = oc % 2
            wdep = [wstr_used[buf]] if wstr_used[buf] else []
            wstr_toks[buf] = em.dma("p", wstr[buf][:],
                                    W["lb_wol"][oc * 128:(oc + 1) * 128, :],
                                    after=wdep)
            bank = C.pb[5 + oc % 2]
            fns = [lambda t, kc=kc, buf=buf, bank=bank: t.matmul(
                bank[:, 0:TBL], wstr[buf][:, kc * 128:(kc + 1) * 128],
                aout[kc][:, 0:TBL], start=(kc == 0), stop=(kc == NCH - 1))
                for kc in range(NCH)]
            tmm = em.ops("t", fns, after=[aout_last, wstr_toks[buf],
                                          pb_hist.get(5 + oc % 2)])
            wstr_used[buf] = tmm
            qb_hist = tmm
            xi = xfr[oc % 2]
            tx = em.dma("y", xi[:, 0:TBL],
                        x_hbm[128 * oc:128 * (oc + 1), tq0:tq0 + TBL])
            tv2 = em.op("v", lambda v, oc=oc, xi=xi, bank=bank:
                        v.scalar_tensor_tensor(osum[:, 0:TBL], bank[:, 0:TBL],
                                               bo_sb[:, oc:oc + 1],
                                               xi[:, 0:TBL], ALU.add, ALU.add),
                        after=[tmm, tx] + ([res_toks[-1]] if res_toks else []))
            pb_hist[5 + oc % 2] = tv2
            res_toks.append(em.dma("y", xo_hbm[128 * oc:128 * (oc + 1),
                                               tq0:tq0 + TBL],
                                   osum[:, 0:TBL], after=[tv2]))
        aout_hist = qb_hist
        prev_res = res_toks[-1]
    return res_toks


def build_l1attnb_program():
    nc = bass.Bass("TRN2", target_bir_lowering=False)
    x_in = nc.dram_tensor("x_in", [DM, T], F32, kind="ExternalInput").ap()
    x_out = nc.dram_tensor("x_out", [DM, T], F32, kind="ExternalOutput").ap()
    W = {}
    for name, shape in [("lb_wql", [1024, 1024]), ("lb_wkl", [1024, 1024]),
                        ("lb_wv", [1024, 1024]), ("lb_wol", [1024, 1024]),
                        ("lb_bq", [128, 8]), ("lb_bk", [128, 8]),
                        ("lb_bv", [1, 1024]), ("lb_bo", [128, 8])]:
        W[name] = nc.dram_tensor(name, shape, F32, kind="ExternalInput").ap()
    xln_hbm = nc.dram_tensor("xln_s", [DM, T], BF16, kind="Internal").ap()
    k_hbm = nc.dram_tensor("k_s", [DM, T], BF16, kind="Internal").ap()
    vb_hbm = nc.dram_tensor("vb_s", [128, 32 * 1040], BF16, kind="Internal").ap()
    C = Core(nc)
    res = emit_l1attnb(C, x_in, x_out, W, xln_hbm, k_hbm, vb_hbm)
    C.em.finalize(res)
    return nc



def host_prep_l0attna(inputs):
    wq = np.asarray(inputs["L0_attn_a_Wq"])
    wk = np.asarray(inputs["L0_attn_a_Wk"])
    wv = np.asarray(inputs["L0_attn_a_Wv"])
    wo = np.asarray(inputs["L0_attn_a_Wo"])
    bq = np.asarray(inputs["L0_attn_a_bq"])
    bk = np.asarray(inputs["L0_attn_a_bk"])
    bv = np.asarray(inputs["L0_attn_a_bv"])
    bo = np.asarray(inputs["L0_attn_a_bo"])
    qph = np.asarray(inputs["L0_attn_a_q_phase"]).reshape(256)
    kph = np.asarray(inputs["L0_attn_a_k_phase"]).reshape(256)
    temp = np.asarray(inputs["L0_attn_a_temperature"]).reshape(1, 1)

    def pad_w(w):
        wp = np.zeros((1024, 768), np.float32)
        for h in range(16):
            c, j = divmod(h, 3)
            wp[:, 128 * c + 32 * j:128 * c + 32 * j + 16] = \
                w[:, 16 * h:16 * h + 16]
        return wp.reshape(8, 128, 6, 128).transpose(2, 1, 0, 3).reshape(768, 1024)

    def pad_b(b):
        bp = np.zeros((128, 6), np.float32)
        for h in range(16):
            c, j = divmod(h, 3)
            bp[32 * j:32 * j + 16, c] = b[16 * h:16 * h + 16]
        return bp

    wol = wo.reshape(8, 128, 8, 128).transpose(2, 1, 0, 3).reshape(8 * 128, 1024)
    e3 = np.zeros((128, 3), np.float32)
    e3t = np.zeros((3, 128), np.float32)
    for j in range(3):
        e3[32 * j:32 * j + 16, j] = 1.0
        e3t[j, 32 * j:32 * j + 16] = 1.0
    return {"aa_wqp": np.ascontiguousarray(pad_w(wq)),
            "aa_wkp": np.ascontiguousarray(pad_w(wk)),
            "aa_wv": np.ascontiguousarray(wv),
            "aa_wol": np.ascontiguousarray(wol),
            "aa_bq": pad_b(bq + 0.0), "aa_qph": pad_b(qph),
            "aa_bk": pad_b(bk + 0.0), "aa_kph": pad_b(kph),
            "aa_bv": np.ascontiguousarray(bv[None, :]),
            "aa_bo": np.ascontiguousarray(bo.reshape(8, 128).T),
            "aa_temp": temp, "aa_e3": e3, "aa_e3t": e3t}


def emit_l0attna(C, x_hbm, xo_hbm, W, xln_hbm):
    em = C.em
    TBL = 512
    NB = T // TBL
    NPC = 6
    wv_sb = C.sb("aawv", [128, NCH * 1024], BF16)
    tok_w = None
    for kc in range(NCH):
        tok_w = em.dma("p", wv_sb[:, kc * 1024:(kc + 1) * 1024],
                       W["aa_wv"][kc * 128:(kc + 1) * 128, :])
    bqc = C.sb("aabq", [128, NPC], F32)
    bkc = C.sb("aabk", [128, NPC], F32)
    tmp6 = C.sb("aat6", [128, NPC], F32)
    em.dma("y", bqc[:], W["aa_bq"])
    em.dma("y", bkc[:], W["aa_bk"])
    t_b1 = em.dma("y", tmp6[:], W["aa_qph"])
    t_b2 = em.op("v", lambda v: v.tensor_add(bqc[:], bqc[:], tmp6[:]),
                 after=[t_b1])
    t_b3 = em.dma("y", tmp6[:], W["aa_kph"], after=[t_b2])
    tok_b = em.op("v", lambda v: v.tensor_add(bkc[:], bkc[:], tmp6[:]),
                  after=[t_b3])
    bo_sb = C.sb("aabo", [128, 8], F32)
    bv_sb = C.sb("aabv", [1, 1024], BF16)
    e3_sb = C.sb("aae3", [128, 3], BF16)
    e3t_sb = C.sb("aae3t", [3, 128], F32)
    tok_w = em.dma("y", bo_sb[:], W["aa_bo"], after=[tok_w])
    tok_w = em.dma("p", bv_sb[:], W["aa_bv"], after=[tok_w])
    tok_w = em.dma("p", e3_sb[:], W["aa_e3"], after=[tok_w])
    tok_w = em.dma("y", e3t_sb[:], W["aa_e3t"], after=[tok_w])
    tsc = C.sb("aats", [1, 1], F32)
    it3 = C.sb("aait3", [128, 1], F32)
    bias_col = C.sb("aabc", [128, 1], F32)
    tt0 = em.dma("y", tsc[:], W["aa_temp"])
    tt1 = em.op("v", lambda v: v.reciprocal(tsc[:], tsc[:]), after=[tt0])
    tt2 = em.op("t", lambda t: t.matmul(C.pb[4][:, 0:1], C.ones_row_f[:],
                                        tsc[:], start=True, stop=True),
                after=[tt1, C.t_const])
    tt3 = em.op("v", lambda v: v.tensor_copy(it3[:], C.pb[4][:, 0:1]),
                after=[tt2])
    tt4a = em.op("v", lambda v: v.tensor_scalar_mul(bias_col[:], it3[:], -1.0),
                 after=[tt3])
    tt4 = em.op("v", lambda v: v.tensor_max(bias_col[:], bias_col[:], it3[:]),
                after=[tt4a])
    tt5 = em.op("v", lambda v: v.tensor_scalar_min(bias_col[:], bias_col[:],
                                                   40.0), after=[tt4])
    tok_bias = em.op("v", lambda v: v.tensor_scalar_mul(bias_col[:],
                                                        bias_col[:], -1.0),
                     after=[tt5])

    xb = [C.sb("axb", [128, TBL], BF16) for _ in range(NCH)]
    xln = [C.sb("axln", [128, TBL], BF16) for _ in range(NCH)]
    kt = [C.sb("akt", [128, T], BF16) for _ in range(NPC)]
    vbar = C.sb("avbar", [128, 32 * (16 * 65)], BF16)
    qt = [C.sb("aqt", [128, TBL], BF16) for _ in range(NPC)]
    aout = [C.sb("aao", [128, TBL], BF16) for _ in range(NCH)]
    pbuf = [C.sb("apb", [128, TBL], BF16) for _ in range(2)]
    wstr = [C.sb("awstr", [128, 1024], BF16) for _ in range(2)]
    qraw = C.sb("aqraw", [128, TBL], F32)
    sqs = C.sb("asq", [128, TBL], BF16)
    nrow = C.sb("anr", [3, TBL], F32)
    r11 = C.sb("ar11", [1, TBL], F32)
    rb = C.sb("arb", [64, TBL], F32)
    osum = C.sb("aosum", [128, TBL], F32)
    xfr = [C.sb("axf", [128, TBL], F32) for _ in range(2)]

    tok_vones = em.op("v", lambda v: v.memset(vbar[:, 64::65], 1.0))

    def h_slot(h):
        c, j = divmod(h, 3)
        return c, 32 * j

    def qk_chain(src_w, pc, blk, bias6, dst, dst_off, xln_toks, extra,
                 scale_inv, wbuf, wtoks, wused):
        buf = pc % 2
        wdep = [wused[buf]] if wused[buf] else []
        wtoks[buf] = em.dma("p", wbuf[buf][:],
                            src_w[pc * 128:(pc + 1) * 128, :], after=wdep)
        bank = C.pb[pc % 2]
        fns = [lambda t, kc=kc, buf=buf, bank=bank: t.matmul(
            bank[:, 0:TBL], wbuf[buf][:, kc * 128:(kc + 1) * 128],
            xln[kc][:, 0:TBL], start=(kc == 0), stop=(kc == NCH - 1))
            for kc in range(NCH)]
        tmm = em.ops("t", fns, after=xln_toks + [wtoks[buf]] + extra)
        wused[buf] = tmm
        t1 = em.op("v", lambda v, pc=pc, bank=bank: v.tensor_scalar_add(
            qraw[:, 0:TBL], bank[:, 0:TBL], bias6[:, pc:pc + 1]),
            after=[tmm, tok_b])
        t2 = em.op("s", lambda s: s.activation(sqs[:, 0:TBL], qraw[:, 0:TBL],
                                               AF.Square), after=[t1])
        t3 = em.op("t", lambda t: t.matmul(C.pb[4][0:3, 0:TBL], e3_sb[:],
                                           sqs[:, 0:TBL], start=True,
                                           stop=True),
                   after=[t2, tok_w, pb_hist.get(4)])
        t4 = em.op("s", lambda s: s.activation(nrow[:, 0:TBL],
                                               C.pb[4][0:3, 0:TBL], AF.Sqrt),
                   after=[t3])
        t5 = em.op("v", lambda v: v.tensor_scalar_max(nrow[:, 0:TBL],
                                                      nrow[:, 0:TBL], 1e-12),
                   after=[t4])
        t6 = em.op("v", lambda v: v.reciprocal(nrow[:, 0:TBL], nrow[:, 0:TBL]),
                   after=[t5])
        if scale_inv:
            t6 = em.op("v", lambda v: v.tensor_scalar_mul(
                nrow[:, 0:TBL], nrow[:, 0:TBL], it3[0:3, 0:1]),
                after=[t6, tt3])
        t7 = em.op("t", lambda t: t.matmul(C.pb[5][:, 0:TBL], e3t_sb[:],
                                           nrow[:, 0:TBL], start=True,
                                           stop=True),
                   after=[t6, tok_w, pb_hist.get(5)])
        pb_hist[4] = t7
        t8 = em.op("v", lambda v, dst=dst, dst_off=dst_off: v.tensor_mul(
            dst[:, dst_off:dst_off + TBL], qraw[:, 0:TBL], C.pb[5][:, 0:TBL]),
            after=[t7])
        pb_hist[5] = t8
        pb_hist[pc % 2] = t1
        return t8

    pb_hist = {}
    wstr_toks = [None, None]
    wstr_used = [None, None]
    st_last = None
    ph_last = []
    for blk in range(NB):
        xln_toks, _ = emit_ln(C, x_hbm, xb, xln, blk, TBL,
                              after=[tok_w] + ph_last)
        st_toks = [em.dma("y", xln_hbm[128 * c:128 * (c + 1),
                                       blk * TBL:(blk + 1) * TBL],
                          xln[c][:, 0:TBL], after=[xln_toks[c]])
                   for c in range(NCH)]
        st_last = st_toks[-1]
        for pc in range(NPC):
            tk8 = qk_chain(W["aa_wkp"], pc, blk, bkc, kt[pc], blk * TBL,
                           xln_toks, ph_last, False, wstr, wstr_toks, wstr_used)
            ph_last = [tk8]
        for ts_ in range(4):
            for vc in range(2):
                bank = C.pb[2 + (2 * ts_ + vc) % 2]
                fns = [lambda t, kc=kc, ts_=ts_, vc=vc, bank=bank: t.matmul(
                    bank[:, 0:512], xln[kc][:, ts_ * 128:(ts_ + 1) * 128],
                    wv_sb[:, kc * 1024 + vc * 512:kc * 1024 + (vc + 1) * 512],
                    start=(kc == 0), stop=False) for kc in range(NCH)]
                fns.append(lambda t, vc=vc, bank=bank: t.matmul(
                    bank[:, 0:512], C.ones_row_bf[:],
                    bv_sb[:, vc * 512:(vc + 1) * 512], start=False, stop=True))
                tmm = em.ops("t", fns,
                             after=xln_toks + [tok_w, C.t_const] + ph_last)
                j = blk * 4 + ts_
                fns2 = [lambda v, hh=hh, j=j, vc=vc, bank=bank: v.tensor_copy(
                    vbar[:, j * 1040 + (8 * vc + hh) * 65:
                         j * 1040 + (8 * vc + hh) * 65 + 64],
                    bank[:, hh * 64:(hh + 1) * 64]) for hh in range(8)]
                tvv = em.ops("v", fns2, after=[tmm, tok_vones])
                ph_last = [tvv]
    phaseA = ph_last + [st_last]

    pbuf_hist = [None, None]
    aout_hist = None
    prev_res = None
    qb_hist = None
    for blk in range(NB):
        tq0 = blk * TBL
        xl_toks = [em.dma("p", xln[c][:, 0:TBL],
                          xln_hbm[128 * c:128 * (c + 1), tq0:tq0 + TBL],
                          after=[st_last, prev_res] + phaseA +
                          ([qb_hist] if qb_hist else []))
                   for c in range(NCH)]
        q_toks = []
        for pc in range(NPC):
            t8 = qk_chain(W["aa_wqp"], pc, blk, bqc, qt[pc], 0,
                          xl_toks, phaseA + ([qb_hist] if qb_hist else []),
                          True, wstr, wstr_toks, wstr_used)
            q_toks.append(t8)
        for h in range(16):
            hc, hb = h_slot(h)
            nj = blk * 4 + 4
            obank = C.pb[2 + h % 2]
            first_dep = pb_hist.get(2 + h % 2)
            for jc in range(nj):
                masked = jc >= blk * 4
                sbank_i = jc % 2
                sbank = C.pb[sbank_i]
                tmm = em.op("t", lambda t, jc=jc, sbank=sbank, hc=hc, hb=hb:
                            t.matmul(sbank[:, 0:TBL],
                                     kt[hc][hb:hb + 16, jc * 128:(jc + 1) * 128],
                                     qt[hc][hb:hb + 16, 0:TBL],
                                     start=True, stop=True),
                            after=[q_toks[hc % NPC] if False else q_toks[-1],
                                   pb_hist.get(sbank_i)] + phaseA)
                pbi = jc % 2
                texp = em.op("s", lambda s, pbi=pbi, sbank=sbank: s.activation(
                    pbuf[pbi][:, 0:TBL], sbank[:, 0:TBL], AF.Exp,
                    bias=bias_col[:]),
                    after=[tmm, tok_bias] +
                    ([pbuf_hist[pbi]] if pbuf_hist[pbi] else []))
                pb_hist[sbank_i] = texp
                tlast = texp
                if masked:
                    base = tq0 - jc * 128
                    tlast = em.op("p", lambda g, pbi=pbi, base=base:
                                  g.affine_select(pbuf[pbi][:, 0:TBL],
                                                  pbuf[pbi][:, 0:TBL],
                                                  [[1, TBL]], ALU.is_ge, 0.0,
                                                  base=base,
                                                  channel_multiplier=-1),
                                  after=[texp])
                tv = em.op("t", lambda t, jc=jc, pbi=pbi, obank=obank, h=h,
                           nj=nj: t.matmul(
                               obank[0:65, 0:TBL],
                               vbar[:, jc * 1040 + h * 65:jc * 1040 + h * 65 + 65],
                               pbuf[pbi][:, 0:TBL],
                               start=(jc == 0), stop=(jc == nj - 1)),
                           after=[tlast] + ([first_dep] if jc == 0 else []))
                pbuf_hist[pbi] = tv
            tr = em.op("v", lambda v, obank=obank: v.reciprocal(
                r11[:, 0:TBL], obank[64:65, 0:TBL]), after=[tv])
            tb2 = em.op("t", lambda t: t.matmul(
                C.pb[4][0:64, 0:TBL], C.ones_row_f[:, 0:64], r11[:, 0:TBL],
                start=True, stop=True), after=[tr, pb_hist.get(4)])
            tb3 = em.op("v", lambda v: v.tensor_copy(rb[:, 0:TBL],
                                                     C.pb[4][0:64, 0:TBL]),
                        after=[tb2])
            pb_hist[4] = tb3
            tao = em.op("v", lambda v, h=h, obank=obank: v.tensor_mul(
                aout[h // 2][64 * (h % 2):64 * (h % 2) + 64, 0:TBL],
                obank[0:64, 0:TBL], rb[:, 0:TBL]),
                after=[tb3, aout_hist])
            pb_hist[2 + h % 2] = tao
        aout_last = tao
        res_toks = []
        for oc in range(NCH):
            buf = oc % 2
            wdep = [wstr_used[buf]] if wstr_used[buf] else []
            wstr_toks[buf] = em.dma("p", wstr[buf][:],
                                    W["aa_wol"][oc * 128:(oc + 1) * 128, :],
                                    after=wdep)
            bank = C.pb[5 + oc % 2]
            fns = [lambda t, kc=kc, buf=buf, bank=bank: t.matmul(
                bank[:, 0:TBL], wstr[buf][:, kc * 128:(kc + 1) * 128],
                aout[kc][:, 0:TBL], start=(kc == 0), stop=(kc == NCH - 1))
                for kc in range(NCH)]
            tmm = em.ops("t", fns, after=[aout_last, wstr_toks[buf],
                                          pb_hist.get(5 + oc % 2)])
            wstr_used[buf] = tmm
            qb_hist = tmm
            xi = xfr[oc % 2]
            tx = em.dma("y", xi[:, 0:TBL],
                        x_hbm[128 * oc:128 * (oc + 1), tq0:tq0 + TBL])
            tv2 = em.op("v", lambda v, oc=oc, xi=xi, bank=bank:
                        v.scalar_tensor_tensor(osum[:, 0:TBL], bank[:, 0:TBL],
                                               bo_sb[:, oc:oc + 1],
                                               xi[:, 0:TBL], ALU.add, ALU.add),
                        after=[tmm, tx] + ([res_toks[-1]] if res_toks else []))
            pb_hist[5 + oc % 2] = tv2
            res_toks.append(em.dma("y", xo_hbm[128 * oc:128 * (oc + 1),
                                               tq0:tq0 + TBL],
                                   osum[:, 0:TBL], after=[tv2]))
        aout_hist = qb_hist
        prev_res = res_toks[-1]
    return res_toks


def build_l0attna_program():
    nc = bass.Bass("TRN2", target_bir_lowering=False)
    x_in = nc.dram_tensor("x_in", [DM, T], F32, kind="ExternalInput").ap()
    x_out = nc.dram_tensor("x_out", [DM, T], F32, kind="ExternalOutput").ap()
    W = {}
    for name, shape in [("aa_wqp", [768, 1024]), ("aa_wkp", [768, 1024]),
                        ("aa_wv", [1024, 1024]), ("aa_wol", [1024, 1024]),
                        ("aa_bq", [128, 6]), ("aa_qph", [128, 6]),
                        ("aa_bk", [128, 6]), ("aa_kph", [128, 6]),
                        ("aa_bv", [1, 1024]), ("aa_bo", [128, 8]),
                        ("aa_temp", [1, 1]), ("aa_e3", [128, 3]),
                        ("aa_e3t", [3, 128])]:
        W[name] = nc.dram_tensor(name, shape, F32, kind="ExternalInput").ap()
    xln_hbm = nc.dram_tensor("xln_s", [DM, T], BF16, kind="Internal").ap()
    C = Core(nc)
    res = emit_l0attna(C, x_in, x_out, W, xln_hbm)
    C.em.finalize(res)
    return nc


def build_l0attnb_program(debug=False):
    nc = bass.Bass("TRN2", target_bir_lowering=False)
    x_in = nc.dram_tensor("x_in", [DM, T], F32, kind="ExternalInput").ap()
    x_out = nc.dram_tensor("x_out", [DM, T], F32, kind="ExternalOutput").ap()
    W = {}
    for name, shape in [("ab_wql", [1024, 1024]), ("ab_wkT", [1024, 512]),
                        ("ab_wvT", [1024, 512]), ("ab_wol", [1024, 1024]),
                        ("ab_bo", [128, 8]), ("ab_qaux", [128, T])]:
        W[name] = nc.dram_tensor(name, shape, F32, kind="ExternalInput").ap()
    xln_hbm = nc.dram_tensor("xln_s", [DM, T], BF16, kind="Internal").ap()
    C = Core(nc)
    dbg = None
    if debug:
        dbg = {}
        for name, shape in [("ksr", [1, 16]), ("kb0", [128, 512]),
                            ("bBr", [1, 16]), ("qb0", [128, 512]),
                            ("p00", [128, 512]), ("rec0", [1, 512]),
                            ("ao0", [128, 512]), ("s00", [128, 512]),
                            ("aoall", [1024, 512]),
                            ("aux00", [68, 512]), ("qa00", [68, 512])]:
            dbg[name] = nc.dram_tensor("dbg_" + name, shape, F32,
                                       kind="ExternalOutput").ap()
    res = emit_l0attnb(C, x_in, x_out, W, xln_hbm, dbg=dbg)
    C.em.finalize(res)
    return nc


def emit_cast_hbm(C, src_f32, dst_bf16, rows, cols, tag):
    em = C.em
    buf = [C.sb(tag, [128, cols], BF16) for _ in range(2)]
    tok_st = [None, None]
    for rc in range(rows // 128):
        b = rc % 2
        tl = em.dma("p", buf[b][:], src_f32[rc * 128:(rc + 1) * 128, :],
                    after=[tok_st[b]] if tok_st[b] else [])
        tok_st[b] = em.dma("y", dst_bf16[rc * 128:(rc + 1) * 128, :], buf[b][:],
                           after=[tl])
    return tok_st[(rows // 128 - 1) % 2]



def build_l1ffn_program():
    nc = bass.Bass("TRN2", target_bir_lowering=False)
    x_in = nc.dram_tensor("x_in", [DM, T], F32, kind="ExternalInput").ap()
    x_out = nc.dram_tensor("x_out", [DM, T], F32, kind="ExternalOutput").ap()
    W = {
        "l1f_bd1": nc.dram_tensor("l1f_bd1", [128, 4 * 8 * 128], F32,
                                  kind="ExternalInput").ap(),
        "l1f_bd2": nc.dram_tensor("l1f_bd2", [128, 4 * 32 * 128], F32,
                                  kind="ExternalInput").ap(),
        "l1f_cpT": nc.dram_tensor("l1f_cpT", [16384, 1024], F32,
                                  kind="ExternalInput").ap(),
    }
    cpbf = nc.dram_tensor("cpbf", [16384, 1024], BF16, kind="Internal").ap()
    C = Core(nc)
    tok_cast = emit_cast_hbm(C, W["l1f_cpT"], cpbf, 16384, 1024, "cpcast")
    res = emit_l1ffn(C, x_in, x_out, W, cpbf, tok_cast)
    C.em.finalize(res)
    return nc


def build_l0ffn_program():
    nc = bass.Bass("TRN2", target_bir_lowering=False)
    x_in = nc.dram_tensor("x_in", [DM, T], F32, kind="ExternalInput").ap()
    x_out = nc.dram_tensor("x_out", [DM, T], F32, kind="ExternalOutput").ap()
    W = {}
    for name, shape in [("ff_w1l", [8 * 32 * 128, 8 * 128]),
                        ("ff_w2l", [8 * 8 * 128, 32 * 128]),
                        ("ff_b1l", [128, 8 * 32]), ("ff_b2", [8, 1024]),
                        ("ff_rl", [1024, 12]), ("ff_rlb", [1, 12]),
                        ("ff_oh", [8, 8 * 128])]:
        W[name] = nc.dram_tensor(name, shape, F32, kind="ExternalInput").ap()
    w1bf = nc.dram_tensor("w1bf", [8 * 32 * 128, 8 * 128], BF16,
                          kind="Internal").ap()
    w2bf = nc.dram_tensor("w2bf", [8 * 8 * 128, 32 * 128], BF16,
                          kind="Internal").ap()
    xln_hbm = nc.dram_tensor("xln_s", [DM, T], BF16, kind="Internal").ap()
    C = Core(nc)
    t1 = emit_cast_hbm(C, W["ff_w1l"], w1bf, 8 * 32 * 128, 8 * 128, "w1cast")
    t2 = emit_cast_hbm(C, W["ff_w2l"], w2bf, 8 * 8 * 128, 32 * 128, "w2cast")
    res = emit_l0ffn(C, x_in, x_out, W, w1bf, w2bf, t2, xln_hbm)
    C.em.finalize(res)
    return nc


SUBLAYERS = ["L0_attn_a", "L0_ffn_1", "L0_attn_b", "L0_ffn_2",
             "L1_attn_a", "L1_ffn_1", "L1_attn_b", "L1_ffn_2"]

_BUILDERS = {
    "L0_attn_a": (build_l0attna_program, host_prep_l0attna),
    "L0_ffn": (build_l0ffn_program, host_prep_l0ffn),
    "L0_attn_b": (build_l0attnb_program, host_prep_l0attnb),
    "L1_attn_a": (build_l1attna_program, host_prep_l1attna),
    "L1_ffn": (build_l1ffn_program, host_prep_l1ffn),
    "L1_attn_b": (build_l1attnb_program, host_prep_l1attnb),
}


def kernel(**inputs):
    x = np.asarray(inputs["L0_attn_a_x"], np.float32)
    xT = [np.ascontiguousarray(x[b].T) for b in range(N_CORES)]
    progs = {}
    weights = {}
    for name in SUBLAYERS:
        base = name[:-2] if name.endswith(("_1", "_2")) else name
        if base not in progs:
            build, prep = _BUILDERS[base]
            progs[base] = build()
            weights[base] = prep(inputs)
        nc = progs[base]
        W = weights[base]
        in_maps = [{"x_in": xT[b], **W} for b in range(N_CORES)]
        res = run_bass_kernel_spmd(nc, in_maps, core_ids=list(range(N_CORES)))
        xT = [np.asarray(r["x_out"]) for r in res.results]
    out = np.stack([np.ascontiguousarray(xb.T) for xb in xT])
    return out.astype(np.float32)


# version 02
# speedup vs baseline: 1.0096x; 1.0096x over previous
import numpy as np
import concourse.bass as bass
import concourse.mybir as mybir
from concourse.bass_utils import run_bass_kernel_spmd

F32 = mybir.dt.float32
BF16 = mybir.dt.bfloat16
AF = mybir.ActivationFunctionType
ALU = mybir.AluOpType
AX = mybir.AxisListType

T = 4096
DM = 1024
NCH = 8
N_CORES = 8

ENGINES = ("t", "v", "s", "p", "y")


class Em:

    def __init__(self, nc):
        self.nc = nc
        self.q = {e: [] for e in ENGINES}
        self.sem = {e: nc.alloc_semaphore("sem_" + e) for e in ENGINES}
        self.cnt = {e: 0 for e in ENGINES}
        self.hw = {e: {} for e in ENGINES}
        self.bufsem = {}
        self.free_sems = []
        self.phase = []

    def _waits(self, e, after):
        out = []
        for tok in list(after) + self.phase:
            if tok is None:
                continue
            sem, val = tok
            key = id(sem)
            if self.hw[e].get(key, -1) >= val:
                continue
            self.hw[e][key] = val
            out.append((sem, val))
        return out

    def op(self, e, fn, after=()):
        return self.ops(e, [fn], after)

    def ops(self, e, fns, after=()):
        waits = self._waits(e, after)
        self.cnt[e] += 1
        val = self.cnt[e]
        sem = self.sem[e]

        def emit(eng, fns=fns, waits=waits, sem=sem):
            for s, v in waits:
                eng.wait_ge(s, v)
            inst = None
            for fn in fns:
                inst = fn(eng)
            inst.then_inc(sem, 1)

        self.q[e].append(emit)
        return (sem, val)

    def dma(self, e, out, in_, after=()):
        waits = self._waits(e, after)
        name = out.tensor.name
        ent = self.bufsem.get(name)
        if ent is None:
            if self.free_sems:
                sem0, cnt0 = self.free_sems.pop()
                ent = [sem0, cnt0]
            else:
                ent = [self.nc.alloc_semaphore("dsem%d" % len(self.bufsem)), 0]
            self.bufsem[name] = ent
        ent[1] += 16
        val = ent[1]
        sem = ent[0]

        def emit(eng, out=out, in_=in_, waits=waits, sem=sem):
            for s, v in waits:
                eng.wait_ge(s, v)
            eng.dma_start(out=out, in_=in_).then_inc(sem, 16)

        self.q[e].append(emit)
        return (sem, val)

    def set_phase(self):
        toks = [(self.sem[e], self.cnt[e]) for e in ENGINES if self.cnt[e]]
        toks += [(ent[0], ent[1]) for ent in self.bufsem.values()]
        self.phase = toks

    def recycle_bufsems(self, names):
        for n in names:
            ent = self.bufsem.pop(n, None)
            if ent is not None:
                self.free_sems.append((ent[0], ent[1]))

    def finalize(self, final_tokens):
        nc = self.nc
        with nc.Block() as block:
            def run(eng, e):
                for emit in self.q[e]:
                    emit(eng)

            @block.tensor
            def _(eng):
                run(eng, "t")

            @block.vector
            def _(eng):
                run(eng, "v")

            @block.scalar
            def _(eng):
                run(eng, "s")

            @block.gpsimd
            def _(eng):
                run(eng, "p")

            @block.sync
            def _(eng):
                run(eng, "y")
                for s, v in final_tokens:
                    eng.wait_ge(s, v)


class Core:

    def __init__(self, nc):
        self.nc = nc
        self.em = Em(nc)
        self.uid = 0
        self.pb = [nc.alloc_psum_tensor("pbank%d" % i, [128, 512], F32)
                   for i in range(8)]
        self.ones_col_bf = nc.alloc_sbuf_tensor("ones_col_bf", [128, 1], BF16)
        self.ones_row_bf = nc.alloc_sbuf_tensor("ones_row_bf", [1, 128], BF16)
        self.ones_col_f = nc.alloc_sbuf_tensor("ones_col_f", [128, 1], F32)
        self.ones_row_f = nc.alloc_sbuf_tensor("ones_row_f", [1, 128], F32)
        self.eps_col = nc.alloc_sbuf_tensor("eps_col", [128, 1], F32)
        em = self.em
        self.t_const = em.ops("v", [
            lambda v: v.memset(self.ones_col_bf[:], 1.0),
            lambda v: v.memset(self.ones_row_bf[:], 1.0),
            lambda v: v.memset(self.ones_col_f[:], 1.0),
            lambda v: v.memset(self.ones_row_f[:], 1.0),
            lambda v: v.memset(self.eps_col[:], 1e-5),
        ])
        self.ln_scratch()

    def name(self, base):
        self.uid += 1
        return "%s_%d" % (base, self.uid)

    def sb(self, base, shape, dtype=F32):
        nm = self.name(base)
        if getattr(self, "_scope", None) is not None:
            h = self._scope.enter_context(self.nc.sbuf_tensor(nm, shape, dtype))
            self._scope_names.append(nm)
            return h
        return self.nc.alloc_sbuf_tensor(nm, shape, dtype)

    def begin_scope(self):
        import contextlib
        self._scope = contextlib.ExitStack()
        self._scope_names = []

    def end_scope(self):
        names = self._scope_names
        self._scope.close()
        self._scope = None
        self._scope_names = []
        self.em.recycle_bufsems(names)

    def ln_scratch(self):
        if not hasattr(self, "_lns"):
            self._lns = {
                "sq": self.sb("lnx_sq", [128, 512], BF16),
                "m": self.sb("lnx_m", [1, 512], F32),
                "q": self.sb("lnx_q", [1, 512], F32),
                "v": self.sb("lnx_v", [1, 512], F32),
                "r": self.sb("lnx_r", [1, 512], F32),
                "s": self.sb("lnx_s", [1, 512], F32),
                "t": self.sb("lnx_t", [128, 512], F32),
            }
        return self._lns


def emit_ln(C, x_hbm, xb, xln, blk, tb, after=(), f32_hbm=None, f32_bufs=None):
    em = C.em
    t0 = tb * blk
    xb_toks = [em.dma("p", xb[c][:, 0:tb],
                      x_hbm[128 * c:128 * (c + 1), t0:t0 + tb],
                      after=after) for c in range(NCH)]
    lns = C.ln_scratch()
    sq = lns["sq"]
    ps_s = C.pb[6]
    ps_q = C.pb[7]
    mm_toks = []
    for c in range(NCH):
        tsq = em.op("s", lambda s, c=c: s.activation(sq[:, 0:tb], xb[c][:, 0:tb],
                                                     AF.Square),
                    after=[xb_toks[c]] + ([mm_toks[-1]] if c else list(after)))
        mm_toks.append(em.ops("t", [
            lambda t, c=c: t.matmul(ps_s[0:1, 0:tb], C.ones_col_bf[:],
                                    xb[c][:, 0:tb], start=(c == 0),
                                    stop=(c == NCH - 1)),
            lambda t, c=c: t.matmul(ps_q[0:1, 0:tb], C.ones_col_bf[:],
                                    sq[:, 0:tb], start=(c == 0),
                                    stop=(c == NCH - 1)),
        ], after=[xb_toks[c], tsq, C.t_const]))
    m_row = lns["m"][:, 0:tb]
    q_row = lns["q"][:, 0:tb]
    v_row = lns["v"][:, 0:tb]
    r_row = lns["r"][:, 0:tb]
    s_row = lns["s"][:, 0:tb]
    tk = mm_toks[-1]
    tk1 = em.op("v", lambda v: v.tensor_scalar_mul(m_row, ps_s[0:1, 0:tb],
                                                   1.0 / DM), after=[tk])
    tk2 = em.op("v", lambda v: v.tensor_scalar_mul(q_row, ps_q[0:1, 0:tb],
                                                   1.0 / DM), after=[tk])
    tk3 = em.op("v", lambda v: v.tensor_mul(v_row, m_row, m_row),
                after=[tk1])
    tk4 = em.op("v", lambda v: v.tensor_sub(v_row, q_row, v_row),
                after=[tk2, tk3])
    tk5 = em.op("s", lambda s: s.activation(v_row, v_row, AF.Sqrt,
                                            bias=C.eps_col[0:1, :]),
                after=[tk4, C.t_const])
    tk6 = em.op("v", lambda v: v.reciprocal(r_row, v_row), after=[tk5])
    tk7 = em.op("v", lambda v: v.tensor_mul(s_row, m_row, r_row),
                after=[tk6])
    tb1 = em.op("t", lambda t: t.matmul(ps_s[:, 0:tb], C.ones_row_f[:],
                                        r_row, start=True, stop=True),
                after=[tk6])
    tb2 = em.op("t", lambda t: t.matmul(ps_q[:, 0:tb], C.ones_row_f[:],
                                        s_row, start=True, stop=True),
                after=[tk7])
    tmp = lns["t"]
    out_toks = []
    f32_st = [None, None]
    for c in range(NCH):
        ta = em.op("v", lambda v, c=c: v.tensor_mul(tmp[:, 0:tb], xb[c][:, 0:tb],
                                                    ps_s[:, 0:tb]),
                   after=[tb1] + ([out_toks[-1]] if c else []))
        out_toks.append(em.op("v", lambda v, c=c: v.tensor_sub(
            xln[c][:, 0:tb], tmp[:, 0:tb], ps_q[:, 0:tb]), after=[tb2, ta]))
        if f32_hbm is not None:
            fb = f32_bufs[c % 2]
            t32 = em.op("v", lambda v, c=c, fb=fb: v.tensor_sub(
                fb[:, 0:tb], tmp[:, 0:tb], ps_q[:, 0:tb]),
                after=[tb2, ta] + ([f32_st[c % 2]] if f32_st[c % 2] else []))
            f32_st[c % 2] = em.dma("y", f32_hbm[128 * c:128 * (c + 1),
                                                t0:t0 + tb],
                                   fb[:, 0:tb], after=[t32])
            out_toks[-1] = t32
    return out_toks, xb_toks



def host_prep_l1ffn(inputs):
    rm = np.asarray(inputs["L1_ffn_rail_maker"])
    up = np.asarray(inputs["L1_ffn_rail_up_proj"])
    cp = np.asarray(inputs["L1_ffn_w_cproj"])
    bd1 = np.zeros((4, 8, 128, 128), np.float32)
    for h in range(4):
        for r in range(64):
            c, j = divmod(r, 8)
            bd1[h, c, 16 * j:16 * j + 16, 16 * j:16 * j + 16] = rm[h, r]
    bd2 = np.zeros((4, 32, 128, 128), np.float32)
    for h in range(4):
        for cz in range(32):
            r0 = 2 * cz
            base = 32 * (cz % 4)
            bd2[h, cz, base:base + 16, 0:64] = up[h, r0].T
            bd2[h, cz, base + 16:base + 32, 64:128] = up[h, r0 + 1].T
    return {"l1f_bd1": np.ascontiguousarray(
                bd1.transpose(2, 0, 1, 3).reshape(128, 4 * 8 * 128)),
            "l1f_bd2": np.ascontiguousarray(
                bd2.transpose(2, 0, 1, 3).reshape(128, 4 * 32 * 128)),
            "l1f_cpT": np.ascontiguousarray(cp.T)}


def emit_l1ffn(C, x_hbm, xo_hbm, W, cpbf_hbm, cpbf_ready):
    em = C.em
    G = 256
    NG = T // G
    bd1_sb = C.sb("bd1", [128, 4 * 8 * 128], BF16)
    bd2_sb = C.sb("bd2", [128, 4 * 32 * 128], BF16)
    tok_bd1 = em.dma("p", bd1_sb[:], W["l1f_bd1"])
    tok_bd = em.dma("p", bd2_sb[:], W["l1f_bd2"], after=[tok_bd1])

    def bd1_t(h, c):
        i = h * 8 + c
        return bd1_sb[:, i * 128:(i + 1) * 128]

    def bd2_t(h, cz):
        i = h * 32 + cz
        return bd2_sb[:, i * 128:(i + 1) * 128]

    xb = [C.sb("xb", [128, G], BF16) for _ in range(NCH)]
    xf = [C.sb("xf", [128, G], F32) for _ in range(2)]
    xln = [C.sb("xln", [128, G], BF16) for _ in range(NCH)]
    y_sb = [C.sb("yrail", [128, G], BF16) for _ in range(NCH)]
    hbuf = C.sb("hbuf", [128, 128 * G], BF16)
    cp_sb = [C.sb("cpslice", [128, 1024], BF16) for _ in range(2)]
    osum = C.sb("osum", [128, G], F32)

    prev_cproj = cpbf_ready
    prev_resid = [None] * NCH
    prev_res_v = None
    cp_toks = [None, None]
    mm_last = None
    for g in range(NG):
        xln_toks, xb_toks = emit_ln(C, x_hbm, xb, xln, g, G,
                                    after=[tok_bd, prev_cproj, prev_res_v])
        gel_toks = []
        for h in range(4):
            y_toks = []
            for c in range(NCH):
                bank = C.pb[c % 2]
                tm = em.op("t", lambda t, h=h, c=c, bank=bank: t.matmul(
                    bank[:, 0:G], bd1_t(h, c), xln[c][:, 0:G],
                    start=True, stop=True),
                    after=[xln_toks[c], tok_bd] + ([y_toks[-2]] if len(y_toks) >= 2 else []))
                y_toks.append(em.op("v", lambda v, c=c, bank=bank: v.tensor_copy(
                    y_sb[c][:, 0:G], bank[:, 0:G]), after=[tm]))
            for cz in range(32):
                cj = cz // 4
                bank = C.pb[2 + cz % 2]
                tm = em.op("t", lambda t, h=h, cz=cz, cj=cj, bank=bank: t.matmul(
                    bank[:, 0:G], bd2_t(h, cz), y_sb[cj][:, 0:G],
                    start=True, stop=True),
                    after=[y_toks[cj]] + ([gel_toks[-2]] if len(gel_toks) >= 2 else []))
                slot = h * 32 + cz
                gel_toks.append(em.op("s", lambda s, slot=slot, bank=bank: s.activation(
                    hbuf[:, slot * G:(slot + 1) * G], bank[:, 0:G], AF.Gelu),
                    after=[tm, prev_cproj]))
        tok_h = gel_toks[-1]
        for kc in range(128):
            buf = kc % 2
            dep = [tok_bd]
            if cp_toks[buf] is not None:
                dep.append(mm_last)
            cp_toks[buf] = em.dma("y", cp_sb[buf][:],
                                  cpbf_hbm[kc * 128:(kc + 1) * 128, :],
                                  after=dep + [cpbf_ready])
            fns = []
            for oc in range(NCH):
                fns.append(lambda t, kc=kc, oc=oc, buf=buf: t.matmul(
                    C.pb[oc][:, 0:G], cp_sb[buf][:, oc * 128:(oc + 1) * 128],
                    hbuf[:, kc * G:(kc + 1) * G], start=(kc == 0),
                    stop=(kc == 127)))
            mm_last = em.ops("t", fns,
                             after=[cp_toks[buf], tok_h] +
                             ([prev_resid[-1]] if kc == 0 and prev_resid[-1] else []))
        res_toks = []
        for oc in range(NCH):
            xi = xf[oc % 2]
            tx = em.dma("y", xi[:, 0:G],
                        x_hbm[128 * oc:128 * (oc + 1), g * G:(g + 1) * G],
                        after=[res_toks[-2]] if len(res_toks) >= 2 else [])
            tv = em.op("v", lambda v, oc=oc, xi=xi: v.tensor_add(
                osum[:, 0:G], xi[:, 0:G], C.pb[oc][:, 0:G]),
                after=[mm_last, tx] + ([res_toks[-1]] if res_toks else []))
            res_toks.append(em.dma("y", xo_hbm[128 * oc:128 * (oc + 1),
                                               g * G:(g + 1) * G],
                                   osum[:, 0:G], after=[tv]))
            prev_res_v = tv
        prev_cproj = mm_last
        prev_resid = res_toks
    return res_toks



def host_prep_l0ffn(inputs):
    w1 = np.asarray(inputs["L0_ffn_w1"])
    w2 = np.asarray(inputs["L0_ffn_w2"])
    b1 = np.asarray(inputs["L0_ffn_b1"])
    b2 = np.asarray(inputs["L0_ffn_b2"])
    rw = np.asarray(inputs["L0_ffn_router_w"])
    rb = np.asarray(inputs["L0_ffn_router_b"])
    lw = np.asarray(inputs["L0_ffn_lang_w"])
    lb = np.asarray(inputs["L0_ffn_lang_b"])
    w1l = w1.reshape(8, 8, 128, 32, 128).transpose(0, 3, 2, 1, 4).reshape(
        8 * 32 * 128, 8 * 128)
    w2l = w2.reshape(8, 32, 128, 8, 128).transpose(0, 3, 2, 1, 4).reshape(
        8 * 8 * 128, 32 * 128)
    b1l = b1.reshape(8, 32, 128).transpose(2, 0, 1).reshape(128, 8 * 32)
    rl = np.concatenate([rw.T, lw.T], axis=1)
    rlb = np.concatenate([rb, lb])[None, :]
    oh = np.zeros((8, 8 * 128), np.float32)
    for e in range(8):
        oh[e, e * 128:(e + 1) * 128] = 1.0
    return {"ff_w1l": np.ascontiguousarray(w1l), "ff_w2l": np.ascontiguousarray(w2l),
            "ff_b1l": np.ascontiguousarray(b1l), "ff_b2": b2.astype(np.float32),
            "ff_rl": np.ascontiguousarray(rl), "ff_rlb": rlb.astype(np.float32),
            "ff_oh": oh}


def emit_l0ffn(C, x_hbm, xo_hbm, W, w1bf, w2bf, wcast_tok, xln_hbm):
    em = C.em
    TBL = 512
    NB = T // TBL
    rl_sb = C.sb("rl", [128, 8 * 12], BF16)
    rlb_sb = C.sb("rlb", [1, 12], BF16)
    b1_sb = C.sb("b1c", [128, 8 * 32], F32)
    b2_sb = C.sb("b2r", [8, 1024], F32)
    oh_sb = C.sb("ohr", [8, 8 * 128], F32)
    g_all = C.sb("gall", [8, T], F32)
    tok_w = [em.dma("p", rl_sb[:, kc * 12:(kc + 1) * 12],
                    W["ff_rl"][kc * 128:(kc + 1) * 128, :]) for kc in range(NCH)]
    tok_w.append(em.dma("p", rlb_sb[:], W["ff_rlb"]))
    tok_w.append(em.dma("y", b1_sb[:], W["ff_b1l"]))
    tok_w.append(em.dma("y", b2_sb[:], W["ff_b2"]))
    tok_w.append(em.dma("y", oh_sb[:], W["ff_oh"]))
    tok_w = tok_w[-1]

    xb = [C.sb("fxb", [128, TBL], BF16) for _ in range(NCH)]
    xln = [C.sb("fxln", [128, TBL], BF16) for _ in range(NCH)]
    ident_f = C.sb("identf", [128, 128], F32)
    iota_mp = C.sb("iotamp", [128, 128], F32)
    t_id0 = em.op("p", lambda g: g.iota(iota_mp[:], [[1, 128]], base=0,
                                        channel_multiplier=-1,
                                        allow_small_or_imprecise_dtypes=True))
    t_id = em.op("v", lambda v: v.tensor_single_scalar(
        ident_f[:], iota_mp[:], 0.0, ALU.is_equal), after=[t_id0])

    rt = C.sb("rt", [128, 12], F32)
    le = C.sb("rle", [128, 4], F32)
    ews = C.sb("rew", [128, 8], F32)
    rq = C.sb("rrq", [128, 8], F32)
    re_ = C.sb("rre", [128, 8], F32)
    rp = C.sb("rrp", [128, 8], F32)
    gt = C.sb("rgt", [128, 8], F32)
    c1 = C.sb("rc1", [128, 1], F32)
    c2 = C.sb("rc2", [128, 1], F32)
    c3 = C.sb("rc3", [128, 1], F32)

    last_route = None
    prev_st = None
    for blk in range(NB):
        xln_toks, _ = emit_ln(C, x_hbm, xb, xln, blk, TBL,
                              after=[tok_w, last_route, prev_st])
        st_toks = [em.dma("y", xln_hbm[128 * c:128 * (c + 1),
                                       blk * TBL:(blk + 1) * TBL],
                          xln[c][:, 0:TBL], after=[xln_toks[c]])
                   for c in range(NCH)]
        for ts_ in range(4):
            ps = C.pb[ts_ % 2]
            fns = []
            for kc in range(NCH):
                fns.append(lambda t, kc=kc, ts_=ts_, ps=ps: t.matmul(
                    ps[0:128, 0:12], xln[kc][:, ts_ * 128:(ts_ + 1) * 128],
                    rl_sb[:, kc * 12:(kc + 1) * 12],
                    start=(kc == 0), stop=False))
            fns.append(lambda t, ts_=ts_, ps=ps: t.matmul(
                ps[0:128, 0:12], C.ones_row_bf[:], rlb_sb[:],
                start=False, stop=True))
            tmm = em.ops("t", fns, after=xln_toks + [tok_w, C.t_const, last_route])
            o = em.op("v", lambda v, ps=ps: v.tensor_copy(rt[:], ps[0:128, 0:12]),
                      after=[tmm])
            o = em.op("v", lambda v: v.tensor_reduce(c1[:], rt[:, 8:12], AX.X,
                                                     ALU.max), after=[o])
            o = em.op("v", lambda v: v.tensor_scalar_mul(c1[:], c1[:], -1.0),
                      after=[o])
            o = em.op("s", lambda s: s.activation(le[:], rt[:, 8:12], AF.Exp,
                                                  bias=c1[:]), after=[o])
            o = em.op("v", lambda v: v.tensor_reduce(c2[:], le[:], AX.X, ALU.add),
                      after=[o])
            o = em.op("v", lambda v: v.reciprocal(c2[:], c2[:]), after=[o])
            o = em.op("v", lambda v: v.tensor_scalar_mul(le[:], le[:], c2[:]),
                      after=[o])
            fns = []
            for e in range(8):
                l = e % 4
                if l < 3:
                    fns.append(lambda v, e=e, l=l: v.tensor_add(
                        ews[:, e:e + 1], le[:, l:l + 1], le[:, l + 1:l + 2]))
                else:
                    fns.append(lambda v, e=e, l=l: v.tensor_copy(
                        ews[:, e:e + 1], le[:, l:l + 1]))
            o = em.ops("v", fns, after=[o])
            o = em.op("v", lambda v: v.scalar_tensor_tensor(
                rq[:], ews[:], 0.1, rt[:, 0:8], ALU.mult, ALU.add), after=[o])
            o = em.op("v", lambda v: v.tensor_reduce(c1[:], rq[:], AX.X, ALU.max),
                      after=[o])
            o = em.op("v", lambda v: v.tensor_scalar_mul(c1[:], c1[:], -1.0),
                      after=[o])
            o = em.op("s", lambda s: s.activation(re_[:], rq[:], AF.Exp,
                                                  bias=c1[:]), after=[o])
            o = em.op("v", lambda v: v.tensor_reduce(c2[:], re_[:], AX.X, ALU.add),
                      after=[o])
            o = em.op("v", lambda v: v.reciprocal(c2[:], c2[:]), after=[o])
            o = em.op("v", lambda v: v.tensor_scalar_mul(rp[:], re_[:], c2[:]),
                      after=[o])
            o = em.op("v", lambda v: v.tensor_reduce(c1[:], rp[:], AX.X, ALU.max),
                      after=[o])
            o = em.op("v", lambda v: v.tensor_single_scalar(gt[:], rp[:], c1[:],
                                                            ALU.is_equal),
                      after=[o])
            o = em.op("v", lambda v: v.scalar_tensor_tensor(
                gt[:], gt[:], -2.0, rp[:], ALU.mult, ALU.add), after=[o])
            o = em.op("v", lambda v: v.tensor_reduce(c2[:], gt[:], AX.X, ALU.max),
                      after=[o])
            o = em.op("v", lambda v: v.tensor_add(c3[:], c1[:], c2[:]), after=[o])
            o = em.op("v", lambda v: v.reciprocal(c3[:], c3[:]), after=[o])
            o = em.op("v", lambda v: v.tensor_single_scalar(gt[:], rp[:], c2[:],
                                                            ALU.is_ge), after=[o])
            o = em.op("v", lambda v: v.tensor_mul(gt[:], gt[:], rp[:]), after=[o])
            o = em.op("v", lambda v: v.tensor_scalar_mul(gt[:], gt[:], c3[:]),
                      after=[o])
            pst = C.pb[2 + ts_ % 2]
            o2 = em.op("t", lambda t, pst=pst: t.transpose(
                pst[0:8, 0:128], gt[:], ident_f[:]), after=[o, t_id])
            last_route = em.op("v", lambda v, blk=blk, ts_=ts_, pst=pst:
                               v.tensor_copy(g_all[0:8, blk * TBL + ts_ * 128:
                                                   blk * TBL + (ts_ + 1) * 128],
                                             pst[0:8, 0:128]), after=[o2])
        prev_st = st_toks[-1]
    route_done = last_route
    xlnst_done = st_toks[-1]

    w1sl = [C.sb("w1sl", [128, 8 * 128], BF16) for _ in range(2)]
    w2sl = [C.sb("w2sl", [128, 32 * 128], BF16) for _ in range(2)]
    hbuf = C.sb("fhbuf", [128, 32 * TBL], BF16)
    gbe = C.sb("gbe", [128, TBL], BF16)
    acc = [C.sb("facc", [128, TBL], F32) for _ in range(NCH)]
    tmpm = C.sb("ftmp", [128, TBL], F32)
    xfr = [C.sb("fxf", [128, TBL], F32) for _ in range(2)]
    osum = C.sb("fosum", [128, TBL], F32)

    w1_toks = [None, None]
    w2_toks = [None, None]
    w1_used = [None, None]
    w2_used = [None, None]
    xl_toks = [None] * NCH
    prev_blk_acc_read = None
    prev_store = None
    prev_gelu = None
    prev_accv = None
    for blk in range(NB):
        xl_toks = [em.dma("p", xln[c][:, 0:TBL],
                          xln_hbm[128 * c:128 * (c + 1),
                                  blk * TBL:(blk + 1) * TBL],
                          after=[xlnst_done, route_done, prev_gelu, prev_accv])
                   for c in range(NCH)]
        for dc in range(NCH):
            ps = C.pb[4 + dc % 2]
            tm = em.op("t", lambda t, dc=dc, blk=blk, ps=ps: t.matmul(
                ps[:, 0:TBL], b2_sb[0:8, dc * 128:(dc + 1) * 128],
                g_all[0:8, blk * TBL:(blk + 1) * TBL], start=True, stop=True),
                after=[route_done, tok_w, prev_accv])
            prev_accv = em.op("v", lambda v, dc=dc, ps=ps: v.tensor_copy(
                acc[dc][:, 0:TBL], ps[:, 0:TBL]), after=[tm, prev_store])
        for e in range(8):
            psg = C.pb[6]
            tg = em.op("t", lambda t, e=e, blk=blk: t.matmul(
                psg[:, 0:TBL], oh_sb[0:8, e * 128:(e + 1) * 128],
                g_all[0:8, blk * TBL:(blk + 1) * TBL], start=True, stop=True),
                after=[route_done, tok_w, prev_accv])
            tgb = em.op("v", lambda v: v.tensor_copy(gbe[:, 0:TBL],
                                                     psg[:, 0:TBL]), after=[tg])
            gelu_hist = [None, None]
            for hc in range(32):
                buf = hc % 2
                dep = [wcast_tok]
                if w1_used[buf] is not None:
                    dep.append(w1_used[buf])
                w1_toks[buf] = em.dma("y", w1sl[buf][:],
                                      w1bf[(e * 32 + hc) * 128:
                                           (e * 32 + hc + 1) * 128, :],
                                      after=dep)
                ps = C.pb[hc % 2]
                fns = []
                for kc in range(NCH):
                    fns.append(lambda t, kc=kc, buf=buf, ps=ps: t.matmul(
                        ps[:, 0:TBL], w1sl[buf][:, kc * 128:(kc + 1) * 128],
                        xln[kc][:, 0:TBL], start=(kc == 0), stop=(kc == NCH - 1)))
                tmm = em.ops("t", fns, after=[w1_toks[buf]] + xl_toks +
                             [gelu_hist[buf] if gelu_hist[buf] is not None
                              else prev_gelu])
                w1_used[buf] = tmm
                prev_gelu = em.op("s", lambda s, hc=hc, e=e, ps=ps: s.activation(
                    hbuf[:, hc * TBL:(hc + 1) * TBL], ps[:, 0:TBL], AF.Gelu,
                    bias=b1_sb[:, e * 32 + hc:e * 32 + hc + 1]),
                    after=[tmm, tok_w])
                gelu_hist[buf] = prev_gelu
            for dc in range(NCH):
                buf = dc % 2
                dep = [wcast_tok]
                if w2_used[buf] is not None:
                    dep.append(w2_used[buf])
                w2_toks[buf] = em.dma("y", w2sl[buf][:],
                                      w2bf[(e * 8 + dc) * 128:
                                           (e * 8 + dc + 1) * 128, :],
                                      after=dep)
                ps = C.pb[2 + dc % 2]
                fns = []
                for kc in range(32):
                    fns.append(lambda t, kc=kc, buf=buf, ps=ps: t.matmul(
                        ps[:, 0:TBL], w2sl[buf][:, kc * 128:(kc + 1) * 128],
                        hbuf[:, kc * TBL:(kc + 1) * TBL],
                        start=(kc == 0), stop=(kc == 31)))
                tmm = em.ops("t", fns, after=[w2_toks[buf], prev_gelu,
                                              prev_accv])
                w2_used[buf] = tmm
                tv1 = em.op("v", lambda v, ps=ps: v.tensor_mul(
                    tmpm[:, 0:TBL], ps[:, 0:TBL], gbe[:, 0:TBL]),
                    after=[tmm, tgb])
                prev_accv = em.op("v", lambda v, dc=dc: v.tensor_add(
                    acc[dc][:, 0:TBL], acc[dc][:, 0:TBL], tmpm[:, 0:TBL]),
                    after=[tv1])
        prev_blk_acc_read = prev_accv
        res_toks = []
        for dc in range(NCH):
            xi = xfr[dc % 2]
            tx = em.dma("y", xi[:, 0:TBL],
                        x_hbm[128 * dc:128 * (dc + 1),
                              blk * TBL:(blk + 1) * TBL])
            tv = em.op("v", lambda v, dc=dc, xi=xi: v.tensor_add(
                osum[:, 0:TBL], xi[:, 0:TBL], acc[dc][:, 0:TBL]),
                after=[prev_accv, tx] + ([res_toks[-1]] if res_toks else []))
            res_toks.append(em.dma("y", xo_hbm[128 * dc:128 * (dc + 1),
                                               blk * TBL:(blk + 1) * TBL],
                                   osum[:, 0:TBL], after=[tv]))
            prev_store = tv
        prev_accv = prev_store
    return res_toks



ABQ_H0 = [0, 1, 4, 5, 8, 9, 12, 13]
ABQ_H1 = [2, 3, 6, 7, 10, 11, 14, 15]


def host_prep_l0attnb(inputs):
    wq = np.asarray(inputs["L0_attn_b_Wq"])
    wk = np.asarray(inputs["L0_attn_b_Wk"])
    wv = np.asarray(inputs["L0_attn_b_Wv"])
    wo = np.asarray(inputs["L0_attn_b_Wo"])
    bo = np.asarray(inputs["L0_attn_b_bo"])
    tq = np.arange(T, dtype=np.float32)
    hi_q = np.floor(tq / 32.0)
    lo_q = tq - 32.0 * hi_q
    qaux = np.zeros((16, 2, 4, T), np.float32)
    for h in range(16):
        s = 2.0 ** (-(h + 4))
        qaux[h, 0, 0] = -32.0 * s * hi_q
        qaux[h, 0, 1] = 32.0 * s
        qaux[h, 0, 2] = s
        qaux[h, 0, 3] = -s * lo_q
        qaux[h, 1, 0] = 32.0 * s * hi_q
        qaux[h, 1, 1] = -32.0 * s
        qaux[h, 1, 2] = -s
        qaux[h, 1, 3] = s * lo_q
    perm = np.concatenate([np.r_[ABQ_H0[c] * 64:(ABQ_H0[c] + 1) * 64,
                                 ABQ_H1[c] * 64:(ABQ_H1[c] + 1) * 64]
                           for c in range(8)])
    wqp = wq.T[:, perm]
    wop = wo.T[perm, :]
    wql = wqp.reshape(8, 128, 8, 128).transpose(2, 1, 0, 3).reshape(8 * 128, 1024)
    wol = wop.reshape(8, 128, 8, 128).transpose(2, 1, 0, 3).reshape(8 * 128, 1024)
    return {"ab_wql": np.ascontiguousarray(wql),
            "ab_wkT": np.ascontiguousarray(wk.T),
            "ab_wvT": np.ascontiguousarray(wv.T),
            "ab_wol": np.ascontiguousarray(wol),
            "ab_bo": np.ascontiguousarray(bo.reshape(8, 128).T),
            "ab_qaux": qaux.reshape(128, T)}


def emit_l0attnb(C, x_hbm, xo_hbm, W, xln_hbm, dbg=None):
    em = C.em
    TBL = 512
    NB = T // TBL
    nc = C.nc

    def dump(name, src, after):
        if dbg is not None and name in dbg:
            return em.dma("p", dbg[name], src, after=after)
        return None
    wk_sb = C.sb("abwk", [128, NCH * 512], BF16)
    wv_sb = C.sb("abwv", [128, NCH * 512], BF16)
    tok_w = None
    for kc in range(NCH):
        em.dma("p", wk_sb[:, kc * 512:(kc + 1) * 512],
               W["ab_wkT"][kc * 128:(kc + 1) * 128, :])
        tok_w = em.dma("p", wv_sb[:, kc * 512:(kc + 1) * 512],
                       W["ab_wvT"][kc * 128:(kc + 1) * 128, :])
    bo_sb = C.sb("abbo", [128, 8], F32)
    tok_w = em.dma("y", bo_sb[:], W["ab_bo"], after=[tok_w])
    wq2 = [C.sb("abwq2", [128, 1024], BF16) for _ in range(2)]
    wo2 = [C.sb("abwo2", [128, 1024], BF16) for _ in range(2)]

    aux = C.sb("abaux", [68, T], BF16)
    t2 = em.op("p", lambda g: g.iota(aux[0:1, :], [[1, 128], [0, 32]], base=0,
                                     channel_multiplier=0,
                                     allow_small_or_imprecise_dtypes=True))
    t2a = em.dma("p", aux[1:2, :], aux[0:1, :], after=[t2])
    t2b = em.dma("p", aux[65:66, :], aux[0:1, :], after=[t2])
    t3 = em.op("p", lambda g: g.iota(aux[0:1, :], [[0, 128], [1, 32]], base=0,
                                     channel_multiplier=0,
                                     allow_small_or_imprecise_dtypes=True),
               after=[t2a, t2b])
    t3a = em.dma("p", aux[2:3, :], aux[0:1, :], after=[t3])
    t3b = em.dma("p", aux[66:67, :], aux[0:1, :], after=[t3])
    t1 = em.op("v", lambda v: v.memset(aux[0:1, :], 1.0), after=[t3a, t3b])
    em.dma("p", aux[64:65, :], aux[0:1, :], after=[t1])
    em.dma("p", aux[3:4, :], aux[0:1, :], after=[t1])
    tok_aux = em.dma("p", aux[67:68, :], aux[0:1, :], after=[t1])

    xb = [C.sb("bxb", [128, TBL], BF16) for _ in range(NCH)]
    xln = [C.sb("bxln", [128, TBL], BF16) for _ in range(NCH)]
    kbuf = [C.sb("bk", [128, T], BF16) for _ in range(4)]
    vbar = C.sb("bvbar", [128, 32 * 520], BF16)
    qbuf = [C.sb("bq", [128, TBL], BF16) for _ in range(NCH)]
    aout = [C.sb("bao", [128, TBL], BF16) for _ in range(NCH)]
    pbuf = [C.sb("bp", [128, TBL], BF16) for _ in range(2)]
    qa_l = C.sb("bqal", [68, T], BF16)
    qa_u = C.sb("bqau", [68, T], BF16)
    sqs = C.sb("bsq", [128, TBL], BF16)
    kB = C.sb("bkB", [1, 8], F32)
    ksr = C.sb("bksr", [1, 16], F32)
    qB = C.sb("bqB", [1, 16], F32)
    bBr = C.sb("bbBr", [1, 16], F32)
    r11 = C.sb("br11", [1, TBL], F32)
    rb = C.sb("brb", [64, TBL], F32)
    bias_col = C.sb("bbc", [128, 1], F32)
    tmpf = C.sb("btmp", [128, TBL], F32)
    xfr = [C.sb("bxf", [128, TBL], F32) for _ in range(2)]
    osum = C.sb("bosum", [128, TBL], F32)

    tok_vones = em.op("v", lambda v: v.memset(vbar[:, 64::65], 1.0))
    tok_kinit = em.op("v", lambda v: v.memset(kB[:], -3e38), after=[])

    kn_last = tok_kinit
    st_last = None
    va_last = None
    prev_ph = []
    for blk in range(NB):
        xln_toks, _ = emit_ln(C, x_hbm, xb, xln, blk, TBL,
                              after=[tok_w, st_last, kn_last, va_last])
        st_toks = [em.dma("y", xln_hbm[128 * c:128 * (c + 1),
                                       blk * TBL:(blk + 1) * TBL],
                          xln[c][:, 0:TBL], after=[xln_toks[c]])
                   for c in range(NCH)]
        st_last = st_toks[-1]
        for ck in range(4):
            bank = C.pb[ck % 2]
            fns = [lambda t, kc=kc, ck=ck, bank=bank: t.matmul(
                bank[:, 0:TBL], wk_sb[:, kc * 512 + ck * 128:kc * 512 + (ck + 1) * 128],
                xln[kc][:, 0:TBL], start=(kc == 0), stop=(kc == NCH - 1))
                for kc in range(NCH)]
            tmm = em.ops("t", fns, after=xln_toks + [tok_w] + prev_ph)
            tev = em.op("v", lambda v, ck=ck, blk=blk, bank=bank: v.tensor_copy(
                kbuf[ck][:, blk * TBL:(blk + 1) * TBL], bank[:, 0:TBL]),
                after=[tmm])
            tsq = em.op("s", lambda s, ck=ck, blk=blk: s.activation(
                sqs[:, 0:TBL], kbuf[ck][:, blk * TBL:(blk + 1) * TBL],
                AF.Square), after=[tev])
            for half in range(2):
                kvh = 2 * ck + half
                bank2 = C.pb[4]
                tm2 = em.op("t", lambda t, half=half, bank2=bank2: t.matmul(
                    bank2[0:1, 0:TBL], C.ones_col_bf[64 * half:64 * half + 64, :],
                    sqs[64 * half:64 * half + 64, 0:TBL], start=True, stop=True),
                    after=[tsq, kn_last])
                tr = em.op("v", lambda v, bank2=bank2: v.tensor_reduce(
                    r11[:, 0:1], bank2[0:1, 0:TBL], AX.X, ALU.max), after=[tm2])
                kn_last = em.op("v", lambda v, kvh=kvh: v.tensor_max(
                    kB[:, kvh:kvh + 1], kB[:, kvh:kvh + 1], r11[:, 0:1]),
                    after=[tr])
            prev_ph = [tev]
        for ts_ in range(4):
            bank = C.pb[2 + ts_ % 2]
            fns = [lambda t, kc=kc, ts_=ts_, bank=bank: t.matmul(
                bank[:, 0:512], xln[kc][:, ts_ * 128:(ts_ + 1) * 128],
                wv_sb[:, kc * 512:(kc + 1) * 512],
                start=(kc == 0), stop=(kc == NCH - 1)) for kc in range(NCH)]
            tmm = em.ops("t", fns, after=xln_toks + [tok_w] + prev_ph)
            j = blk * 4 + ts_
            fns2 = [lambda v, kvh=kvh, j=j, bank=bank: v.tensor_copy(
                vbar[:, j * 520 + kvh * 65:j * 520 + kvh * 65 + 64],
                bank[:, kvh * 64:(kvh + 1) * 64]) for kvh in range(8)]
            va_last = em.ops("v", fns2, after=[tmm, tok_vones])
            prev_ph = [va_last]
    phaseA = [kn_last, va_last, st_last]
    o = em.op("s", lambda s: s.activation(kB[:], kB[:], AF.Sqrt),
              after=phaseA)
    o1 = em.op("v", lambda v: v.tensor_copy(ksr[:, 0::2], kB[:]), after=[o])
    tok_ksr = em.op("v", lambda v: v.tensor_copy(ksr[:, 1::2], kB[:]),
                    after=[o1])
    dump("ksr", ksr[:], [tok_ksr])
    dump("kb0", kbuf[0][:, 0:512], [phaseA[0]])

    pb_hist = {}
    pbuf_hist = [None, None]
    qa_hist = [None, None]
    qa_used = None
    wq_toks = [None, None]
    wq_used = [None, None]
    wo_toks = [None, None]
    wo_used = [None, None]
    aout_hist = None
    qb_hist = None
    prev_res = None
    for blk in range(NB):
        tq0 = blk * TBL
        xl_toks = [em.dma("p", xln[c][:, 0:TBL],
                          xln_hbm[128 * c:128 * (c + 1), tq0:tq0 + TBL],
                          after=[st_last, prev_res, phaseA[1]] +
                          ([qb_hist] if qb_hist else []))
                   for c in range(NCH)]
        q_toks = []
        for ch in range(NCH):
            buf = ch % 2
            wdep = [wq_used[buf]] if wq_used[buf] else []
            wq_toks[buf] = em.dma("p", wq2[buf][:],
                                  W["ab_wql"][ch * 128:(ch + 1) * 128, :],
                                  after=wdep)
            bank = C.pb[5 + ch % 2]
            fns = [lambda t, kc=kc, buf=buf, bank=bank: t.matmul(
                bank[:, 0:TBL], wq2[buf][:, kc * 128:(kc + 1) * 128],
                xln[kc][:, 0:TBL], start=(kc == 0), stop=(kc == NCH - 1))
                for kc in range(NCH)]
            dep = xl_toks + [wq_toks[buf], pb_hist.get(5 + ch % 2)]
            tmm = em.ops("t", fns, after=dep)
            wq_used[buf] = tmm
            tev = em.op("s", lambda s, ch=ch, bank=bank: s.activation(
                qbuf[ch][:, 0:TBL], bank[:, 0:TBL], AF.Copy, scale=0.125),
                after=[tmm] + ([qb_hist] if qb_hist else []))
            pb_hist[5 + ch % 2] = tev
            q_toks.append(tev)
            tsq = em.op("s", lambda s, ch=ch: s.activation(
                sqs[:, 0:TBL], qbuf[ch][:, 0:TBL], AF.Square), after=[tev])
            for half in range(2):
                h = ABQ_H0[ch] if half == 0 else ABQ_H1[ch]
                bank2 = C.pb[4]
                tm2 = em.op("t", lambda t, half=half, bank2=bank2: t.matmul(
                    bank2[0:1, 0:TBL], C.ones_col_bf[64 * half:64 * half + 64, :],
                    sqs[64 * half:64 * half + 64, 0:TBL], start=True, stop=True),
                    after=[tsq, pb_hist.get(4)])
                tr = em.op("v", lambda v, bank2=bank2: v.tensor_reduce(
                    r11[:, 0:1], bank2[0:1, 0:TBL], AX.X, ALU.max), after=[tm2])
                pb_hist[4] = tr
                q_toks.append(em.op("v", lambda v, h=h: v.tensor_copy(
                    qB[:, h:h + 1], r11[:, 0:1]), after=[tr]))
        o = em.op("s", lambda s: s.activation(qB[:], qB[:], AF.Sqrt),
                  after=q_toks)
        o = em.op("v", lambda v: v.tensor_mul(bBr[:], qB[:], ksr[:]),
                  after=[o, tok_ksr])
        tok_bB = em.op("v", lambda v: v.tensor_scalar_mul(bBr[:], bBr[:], -1.0),
                       after=[o])
        if blk == 0:
            dump("bBr", bBr[:], [tok_bB])
            dump("qb0", qbuf[0][:, 0:TBL], [tok_bB])
        for h in range(16):
            kvh = h // 2
            ck, kvp = divmod(kvh, 2)
            kb64 = 64 * kvp
            if kvp == 0:
                ch = ABQ_H0.index(h)
            else:
                ch = ABQ_H1.index(h)
            qb64 = kb64
            for sgn, qa in ((0, qa_l), (1, qa_u)):
                dep = [qa_used] if qa_used is not None else []
                em.dma("p", qa[0:4, :],
                       W["ab_qaux"][(h * 2 + sgn) * 4:(h * 2 + sgn) * 4 + 4, :],
                       after=dep)
                qa_hist[sgn] = em.dma("p", qa[64:68, :],
                                      W["ab_qaux"][(h * 2 + sgn) * 4:
                                                   (h * 2 + sgn) * 4 + 4, :])
            tb = em.op("t", lambda t, h=h: t.matmul(
                C.pb[4][:, 0:1], C.ones_row_f[:], bBr[0:1, h:h + 1],
                start=True, stop=True), after=[tok_bB, pb_hist.get(4)])
            tbc = em.op("v", lambda v: v.tensor_copy(bias_col[:], C.pb[4][:, 0:1]),
                        after=[tb])
            pb_hist[4] = tbc
            tiles = []
            for jc in range(0, blk * 4):
                tiles.append((jc, 0, False))
            for jc in range(blk * 4, blk * 4 + 4):
                tiles.append((jc, 0, True))
                tiles.append((jc, 1, True))
            for jc in range(blk * 4 + 4, 32):
                tiles.append((jc, 1, False))
            obank = C.pb[2 + h % 2]
            first_dep = pb_hist.get(2 + h % 2)
            nt = len(tiles)
            for i, (jc, sgn, masked) in enumerate(tiles):
                sbank_i = i % 2
                sbank = C.pb[sbank_i]
                qa = qa_l if sgn == 0 else qa_u
                auxb = 64 if kb64 == 0 else 0
                fns = [
                    lambda t, jc=jc, sbank=sbank, ck=ck, kb64=kb64, ch=ch, \
                    qb64=qb64: t.matmul(
                        sbank[:, 0:TBL],
                        kbuf[ck][kb64:kb64 + 64, jc * 128:(jc + 1) * 128],
                        qbuf[ch][qb64:qb64 + 64, 0:TBL],
                        start=True, stop=False),
                    lambda t, jc=jc, sbank=sbank, auxb=auxb, qa=qa, tq0=tq0: \
                    t.matmul(
                        sbank[:, 0:TBL], aux[auxb:auxb + 4, jc * 128:(jc + 1) * 128],
                        qa[auxb:auxb + 4, tq0:tq0 + TBL], start=False, stop=True),
                ]
                tmm = em.ops("t", fns,
                             after=[phaseA[0], phaseA[1], qa_hist[sgn], tok_aux,
                                    q_toks[-1], pb_hist.get(sbank_i)])
                pbi = i % 2
                texp = em.op("s", lambda s, pbi=pbi, sbank=sbank: s.activation(
                    pbuf[pbi][:, 0:TBL], sbank[:, 0:TBL], AF.Exp, bias=bias_col[:]),
                    after=[tmm, tbc] +
                    ([pbuf_hist[pbi]] if pbuf_hist[pbi] else []))
                pb_hist[sbank_i] = texp
                tlast = texp
                if masked:
                    if sgn == 0:
                        base, mult, step = tq0 - jc * 128, -1, 1
                    else:
                        base, mult, step = jc * 128 - tq0 - 1, 1, -1
                    tlast = em.op("p", lambda g, pbi=pbi, base=base, mult=mult,
                                  step=step:
                                  g.affine_select(pbuf[pbi][:, 0:TBL],
                                                  pbuf[pbi][:, 0:TBL],
                                                  [[step, TBL]], ALU.is_ge, 0.0,
                                                  base=base,
                                                  channel_multiplier=mult),
                                  after=[texp])
                tv = em.op("t", lambda t, jc=jc, i=i, pbi=pbi, obank=obank,
                           kvh=kvh, nt=nt:
                           t.matmul(obank[0:65, 0:TBL],
                                    vbar[:, jc * 520 + kvh * 65:
                                         jc * 520 + kvh * 65 + 65],
                                    pbuf[pbi][:, 0:TBL],
                                    start=(i == 0), stop=(i == nt - 1)),
                           after=[tlast] + ([first_dep] if i == 0 else []))
                pbuf_hist[pbi] = tv
                qa_used = tv
                if blk == 0 and h == 0 and i == 0:
                    dump("p00", pbuf[pbi][:, 0:TBL], [tlast])
                    if dbg is not None:
                        tcp = em.op("v", lambda v, sbank=sbank: v.tensor_copy(
                            tmpf[:, 0:TBL], sbank[:, 0:TBL]),
                            after=[tmm, texp])
                        pb_hist[sbank_i] = tcp
                        dump("s00", tmpf[:, 0:TBL], [tcp])
                        dump("aux00", aux[0:68, 0:512], [tok_aux])
                        dump("qa00", qa_l[0:68, 0:TBL], [qa_hist[0]])
            tr = em.op("v", lambda v, obank=obank: v.reciprocal(
                r11[:, 0:TBL], obank[64:65, 0:TBL]), after=[tv])
            if blk == 0 and h == 0:
                dump("rec0", r11[:, 0:TBL], [tr])
            tb2 = em.op("t", lambda t: t.matmul(
                C.pb[4][0:64, 0:TBL], C.ones_row_f[:, 0:64], r11[:, 0:TBL],
                start=True, stop=True), after=[tr, pb_hist.get(4)])
            tb3 = em.op("v", lambda v: v.tensor_copy(rb[:, 0:TBL],
                                                     C.pb[4][0:64, 0:TBL]),
                        after=[tb2])
            pb_hist[4] = tb3
            tao = em.op("v", lambda v, ch=ch, qb64=qb64, obank=obank:
                        v.tensor_mul(aout[ch][qb64:qb64 + 64, 0:TBL],
                                     obank[0:64, 0:TBL], rb[:, 0:TBL]),
                        after=[tb3, aout_hist])
            pb_hist[2 + h % 2] = tao
            if blk == 0 and h == 0:
                dump("ao0", aout[ch][:, 0:TBL], [tao])
            if blk == 0 and h == 15 and dbg is not None and "aoall" in dbg:
                for cc2 in range(NCH):
                    em.dma("p", dbg["aoall"][128 * cc2:128 * (cc2 + 1), :],
                           aout[cc2][:, 0:TBL], after=[tao])
        aout_last = tao
        res_toks = []
        for oc in range(NCH):
            buf = oc % 2
            wdep = [wo_used[buf]] if wo_used[buf] else []
            wo_toks[buf] = em.dma("p", wo2[buf][:],
                                  W["ab_wol"][oc * 128:(oc + 1) * 128, :],
                                  after=wdep)
            bank = C.pb[5 + oc % 2]
            fns = [lambda t, kc=kc, buf=buf, bank=bank: t.matmul(
                bank[:, 0:TBL], wo2[buf][:, kc * 128:(kc + 1) * 128],
                aout[kc][:, 0:TBL], start=(kc == 0), stop=(kc == NCH - 1))
                for kc in range(NCH)]
            tmm = em.ops("t", fns, after=[aout_last, wo_toks[buf],
                                          pb_hist.get(5 + oc % 2)])
            wo_used[buf] = tmm
            qb_hist = tmm
            xi = xfr[oc % 2]
            tx = em.dma("y", xi[:, 0:TBL],
                        x_hbm[128 * oc:128 * (oc + 1), tq0:tq0 + TBL])
            tv2 = em.op("v", lambda v, oc=oc, xi=xi, bank=bank:
                        v.scalar_tensor_tensor(osum[:, 0:TBL], bank[:, 0:TBL],
                                               bo_sb[:, oc:oc + 1],
                                               xi[:, 0:TBL], ALU.add, ALU.add),
                        after=[tmm, tx] + ([res_toks[-1]] if res_toks else []))
            pb_hist[5 + oc % 2] = tv2
            res_toks.append(em.dma("y", xo_hbm[128 * oc:128 * (oc + 1),
                                               tq0:tq0 + TBL],
                                   osum[:, 0:TBL], after=[tv2]))
        aout_hist = qb_hist
        prev_res = res_toks[-1]
    return res_toks



LQ_H0 = [0, 1, 2, 3, 8, 9, 10, 11]
LQ_H1 = [4, 5, 6, 7, 12, 13, 14, 15]


def _rope_tiles(base):
    hd = 64
    inv = 1.0 / base ** (np.arange(0, hd, 2, dtype=np.float32) / hd)
    tt = np.arange(T, dtype=np.float32)
    fr = tt[None, :] * inv[:, None]
    cos = np.cos(fr)
    sin = np.sin(fr)
    cosr = np.repeat(cos, 2, axis=0)
    sinr = np.empty((64, T), np.float32)
    sinr[0::2] = -sin
    sinr[1::2] = sin
    return (np.ascontiguousarray(np.tile(cosr, (2, 1))),
            np.ascontiguousarray(np.tile(sinr, (2, 1))))


def host_prep_l1attna(inputs):
    n1 = np.asarray(inputs["L1_attn_a_norm1_w"])
    n2 = np.asarray(inputs["L1_attn_a_norm2_w"])
    gwq = np.asarray(inputs["L1_attn_a_g_wq"])
    gkd = np.asarray(inputs["L1_attn_a_g_kv_down"])
    gku = np.asarray(inputs["L1_attn_a_g_k_up"])
    gvu = np.asarray(inputs["L1_attn_a_g_v_up"])
    gwo = np.asarray(inputs["L1_attn_a_g_wo"])
    lwq = np.asarray(inputs["L1_attn_a_l_wq"])
    lwk = np.asarray(inputs["L1_attn_a_l_wk"])
    lwv = np.asarray(inputs["L1_attn_a_l_wv"])
    lwo = np.asarray(inputs["L1_attn_a_l_wo"])
    perm = np.concatenate([np.r_[LQ_H0[c] * 64:(LQ_H0[c] + 1) * 64,
                                 LQ_H1[c] * 64:(LQ_H1[c] + 1) * 64]
                           for c in range(8)])

    def stream_layout(w):
        return np.ascontiguousarray(
            w.reshape(8, 128, 8, 128).transpose(2, 1, 0, 3).reshape(1024, 1024))

    cos_g, sin_g = _rope_tiles(1000000.0)
    cos_l, sin_l = _rope_tiles(10000.0)
    return {"la_gwql": stream_layout(gwq[:, perm]),
            "la_gwol": stream_layout(gwo[perm, :]),
            "la_lwql": stream_layout(lwq[:, perm]),
            "la_lwol": stream_layout(lwo[perm, :]),
            "la_gkd": np.ascontiguousarray(gkd),
            "la_gku": np.ascontiguousarray(gku),
            "la_gvu": np.ascontiguousarray(gvu),
            "la_lwk": np.ascontiguousarray(lwk),
            "la_lwv": np.ascontiguousarray(lwv),
            "la_n1": np.ascontiguousarray(n1.reshape(8, 128).T),
            "la_n2": np.ascontiguousarray(n2.reshape(8, 128).T),
            "la_cosg": cos_g, "la_sing": sin_g,
            "la_cosl": cos_l, "la_sinl": sin_l}


def emit_l1attna(C, x_hbm, xo_hbm, W, h0b_hbm, h0f_hbm, h1f_hbm, dbg=None):
    em = C.em
    nc = C.nc

    def dump(name, src, after):
        if dbg is not None and name in dbg:
            return em.dma("p", dbg[name], src, after=after)
        return None
    TBL = 512
    NB = T // TBL
    eps6 = C.sb("laeps6", [128, 1], F32)
    tok_e6 = em.op("v", lambda v: v.memset(eps6[:], 1e-6))
    bias_col = C.sb("labc", [128, 1], F32)
    tok_bc = em.op("v", lambda v: v.memset(bias_col[:], -0.13))

    xb = [C.sb("laxb", [128, TBL], BF16) for _ in range(NCH)]
    xln = [C.sb("laxln", [128, TBL], BF16) for _ in range(NCH)]
    xf32 = [C.sb("laxf3", [128, TBL], F32) for _ in range(2)]
    nb_ = [C.sb("lanb", [128, TBL], BF16) for _ in range(NCH)]
    lat = [C.sb("lalat", [128, TBL], BF16) for _ in range(2)]
    kg = [C.sb("lakg", [128, T], BF16) for _ in range(2)]
    vbar = C.sb("lavb", [128, 32 * 260], BF16)
    qg = [C.sb("laqg", [128, TBL], BF16) for _ in range(NCH)]
    aout = [C.sb("laao", [128, TBL], BF16) for _ in range(NCH)]
    pbuf = [C.sb("lapb", [128, TBL], BF16) for _ in range(2)]
    wstr = [C.sb("law", [128, 1024], BF16) for _ in range(2)]
    cosb = C.sb("lacos", [128, T], F32)
    sinb = C.sb("lasin", [128, T], F32)
    qr = C.sb("laqr", [128, TBL], F32)
    qs = C.sb("laqs", [128, TBL], F32)
    ta_ = C.sb("lata", [128, TBL], F32)
    rot = C.sb("larot", [128, TBL], F32)
    sqs = C.sb("lasq", [128, TBL], BF16)
    nrw = C.sb("lanrw", [1, TBL], F32)
    r11 = C.sb("lar11", [1, TBL], F32)
    rb = C.sb("larb", [64, TBL], F32)
    osum = C.sb("laos", [128, TBL], F32)
    xres = C.sb("laxres", [128, TBL], F32)
    n1c = C.sb("lan1c", [128, 8], F32)
    n2c = C.sb("lan2c", [128, 8], F32)
    gkd_sb = C.sb("lagkd", [128, NCH * 256], BF16)
    gku_sb = C.sb("lagku", [128, 2 * 256], BF16)
    gvu_sb = C.sb("lagvu", [128, 2 * 256], BF16)
    lwk_sb = C.sb("lalwk", [128, NCH * 256], BF16)
    lwv_sb = C.sb("lalwv", [128, NCH * 256], BF16)
    em.dma("y", n1c[:], W["la_n1"])
    tok_w = em.dma("y", n2c[:], W["la_n2"])
    for kc in range(NCH):
        em.dma("p", gkd_sb[:, kc * 256:(kc + 1) * 256],
               W["la_gkd"][kc * 128:(kc + 1) * 128, :])
        em.dma("p", lwk_sb[:, kc * 256:(kc + 1) * 256],
               W["la_lwk"][kc * 128:(kc + 1) * 128, :])
        tok_w = em.dma("p", lwv_sb[:, kc * 256:(kc + 1) * 256],
                       W["la_lwv"][kc * 128:(kc + 1) * 128, :])
    for kc in range(2):
        em.dma("p", gku_sb[:, kc * 256:(kc + 1) * 256],
               W["la_gku"][kc * 128:(kc + 1) * 128, :])
        tok_w = em.dma("p", gvu_sb[:, kc * 256:(kc + 1) * 256],
                       W["la_gvu"][kc * 128:(kc + 1) * 128, :])

    tok_vones = em.op("v", lambda v: v.memset(vbar[:, 64::65], 1.0))
    pb_hist = {}
    wstr_toks = [None, None]
    wstr_used = [None, None]

    st_last = None
    prev_sf = None
    for blk in range(NB):
        xln_toks, _ = emit_ln(C, x_hbm, xb, xln, blk, TBL,
                              after=[tok_w, st_last],
                              f32_hbm=h0f_hbm, f32_bufs=xf32)
        st_toks = [em.dma("y", h0b_hbm[128 * c:128 * (c + 1),
                                       blk * TBL:(blk + 1) * TBL],
                          xln[c][:, 0:TBL], after=[xln_toks[c]])
                   for c in range(NCH)]
        st_last = st_toks[-1]
    p0_done = st_last

    def rms_apply(src_hbm, blk, wcol, extra):
        t0 = blk * TBL
        ld = [em.dma("p", xb[c][:, 0:TBL],
                     src_hbm[128 * c:128 * (c + 1), t0:t0 + TBL],
                     after=extra) for c in range(NCH)]
        lns = C.ln_scratch()
        sq = lns["sq"]
        ps_q = C.pb[7]
        mm_toks = []
        for c in range(NCH):
            tsq = em.op("s", lambda s, c=c: s.activation(
                sq[:, 0:TBL], xb[c][:, 0:TBL], AF.Square),
                after=[ld[c]] + ([mm_toks[-1]] if c else list(extra)))
            mm_toks.append(em.op("t", lambda t, c=c: t.matmul(
                ps_q[0:1, 0:TBL], C.ones_col_bf[:], sq[:, 0:TBL],
                start=(c == 0), stop=(c == NCH - 1)),
                after=[ld[c], tsq, C.t_const, pb_hist.get(7)]))
        r_row = lns["r"][:, 0:TBL]
        tk1 = em.op("v", lambda v: v.tensor_scalar_mul(
            r_row, ps_q[0:1, 0:TBL], 1.0 / DM), after=[mm_toks[-1]])
        tk2 = em.op("s", lambda s: s.activation(r_row, r_row, AF.Sqrt,
                                                bias=eps6[0:1, :]),
                    after=[tk1, tok_e6])
        tk3 = em.op("v", lambda v: v.reciprocal(r_row, r_row), after=[tk2])
        tb1 = em.op("t", lambda t: t.matmul(ps_q[:, 0:TBL], C.ones_row_f[:],
                                            r_row, start=True, stop=True),
                    after=[tk3])
        pb_hist[7] = tb1
        out_toks = []
        for c in range(NCH):
            t1 = em.op("v", lambda v, c=c: v.tensor_mul(
                qr[:, 0:TBL], xb[c][:, 0:TBL], ps_q[:, 0:TBL]),
                after=[tb1] + ([out_toks[-1]] if c else []))
            out_toks.append(em.op("v", lambda v, c=c, wcol=wcol: v.tensor_scalar_mul(
                nb_[c][:, 0:TBL], qr[:, 0:TBL], wcol[:, c:c + 1]),
                after=[t1, tok_w]))
        pb_hist[7] = out_toks[-1]
        return out_toks, ld

    def rope_norm(bank, nparts, dst, dst_off, blk, inv8, extra):
        t0 = blk * TBL
        t1 = em.op("s", lambda s, bank=bank, nparts=nparts: s.copy(
            qr[0:nparts, 0:TBL], bank[0:nparts, 0:TBL]), after=extra)
        mask = [(i ^ 1) for i in range(32)]
        t2 = em.op("v", lambda v, nparts=nparts: v.stream_shuffle(
            qs[0:nparts, 0:TBL], qr[0:nparts, 0:TBL], mask), after=[t1])
        t3 = em.op("v", lambda v, nparts=nparts, t0=t0: v.tensor_mul(
            ta_[0:nparts, 0:TBL], qr[0:nparts, 0:TBL],
            cosb[0:nparts, t0:t0 + TBL]), after=[t1])
        t4 = em.op("v", lambda v, nparts=nparts, t0=t0: v.tensor_mul(
            qs[0:nparts, 0:TBL], qs[0:nparts, 0:TBL],
            sinb[0:nparts, t0:t0 + TBL]), after=[t2])
        t5 = em.op("v", lambda v, nparts=nparts: v.tensor_add(
            rot[0:nparts, 0:TBL], ta_[0:nparts, 0:TBL], qs[0:nparts, 0:TBL]),
            after=[t3, t4])
        t6 = em.op("s", lambda s, nparts=nparts: s.activation(
            sqs[0:nparts, 0:TBL], rot[0:nparts, 0:TBL], AF.Square),
            after=[t5])
        res = t6
        for half in range(nparts // 64):
            b = 64 * half
            t7 = em.op("t", lambda t, b=b: t.matmul(
                C.pb[4][0:1, 0:TBL], C.ones_col_bf[b:b + 64, :],
                sqs[b:b + 64, 0:TBL], start=True, stop=True),
                after=[t6, C.t_const, pb_hist.get(4)])
            t8 = em.op("v", lambda v: v.tensor_scalar_mul(
                nrw[:, 0:TBL], C.pb[4][0:1, 0:TBL], 1.0 / 64.0), after=[t7])
            t9 = em.op("s", lambda s: s.activation(nrw[:, 0:TBL], nrw[:, 0:TBL],
                                                   AF.Sqrt, bias=eps6[0:1, :]),
                       after=[t8, tok_e6])
            t10 = em.op("v", lambda v: v.reciprocal(nrw[:, 0:TBL],
                                                    nrw[:, 0:TBL]), after=[t9])
            if inv8:
                t10 = em.op("v", lambda v: v.tensor_scalar_mul(
                    nrw[:, 0:TBL], nrw[:, 0:TBL], 0.125), after=[t10])
            t11 = em.op("t", lambda t, b=b: t.matmul(
                C.pb[4][b:b + 64, 0:TBL], C.ones_row_f[:, 0:64],
                nrw[:, 0:TBL], start=True, stop=True), after=[t10])
            res = em.op("v", lambda v, b=b, dst=dst, dst_off=dst_off:
                        v.tensor_mul(dst[b:b + 64, dst_off:dst_off + TBL],
                                     rot[b:b + 64, 0:TBL],
                                     C.pb[4][b:b + 64, 0:TBL]), after=[t11])
            pb_hist[4] = res
        return res

    def attn_branch(local, src_hbm, res1_hbm, out_hbm, wcol, wql, wol,
                    kw_sb, kw_cols, vw_sb, cos_hbm, sin_hbm, use_latent,
                    extra0):
        dpre = "ll" if local else "lg"
        """One attention branch; writes out_hbm = res1 + attn_out (+x for
        local, handled by caller via res streams).  Returns last token."""
        tok_cs = em.dma("y", cosb[:], cos_hbm, after=extra0)
        tok_cs = em.dma("y", sinb[:], sin_hbm, after=[tok_cs])
        pbuf_hist = [None, None]
        aout_hist = None
        qb_hist = None
        prev_res = None
        kv_guard = []
        x2_used = None
        for blk in range(NB):
            tq0 = blk * TBL
            nm_toks, _ = rms_apply(src_hbm, blk, wcol,
                                   [tok_cs, prev_res, p0_done] + kv_guard +
                                   ([qb_hist] if qb_hist else []))
            lat_toks = nm_toks
            if use_latent:
                lat_toks = []
                for lc in range(2):
                    bank = C.pb[lc % 2]
                    fns = [lambda t, kc=kc, lc=lc, bank=bank: t.matmul(
                        bank[:, 0:TBL],
                        gkd_sb[:, kc * 256 + lc * 128:kc * 256 + (lc + 1) * 128],
                        nb_[kc][:, 0:TBL], start=(kc == 0), stop=(kc == NCH - 1))
                        for kc in range(NCH)]
                    tmm = em.ops("t", fns, after=nm_toks + [tok_w,
                                                            pb_hist.get(lc % 2)])
                    lat_toks.append(em.op("v", lambda v, lc=lc, bank=bank:
                                          v.tensor_copy(lat[lc][:, 0:TBL],
                                                        bank[:, 0:TBL]),
                                          after=[tmm]))
                    pb_hist[lc % 2] = lat_toks[-1]
                ksrc = lat
                k_nch = 2
                kk_sb = gku_sb
            else:
                ksrc = nb_
                k_nch = NCH
                kk_sb = kw_sb
            for ck in range(2):
                bank = C.pb[2 + ck % 2]
                fns = [lambda t, kc=kc, ck=ck, bank=bank, kk_sb=kk_sb,
                       k_nch=k_nch: t.matmul(
                           bank[:, 0:TBL],
                           kk_sb[:, kc * 256 + ck * 128:kc * 256 + (ck + 1) * 128],
                           ksrc[kc][:, 0:TBL], start=(kc == 0),
                           stop=(kc == k_nch - 1)) for kc in range(k_nch)]
                tmm = em.ops("t", fns, after=(lat_toks if use_latent else nm_toks)
                             + [tok_w, pb_hist.get(2 + ck % 2)])
                tkn = rope_norm(bank, 128, kg[ck], blk * TBL, blk, False,
                                [tmm, tok_cs])
                pb_hist[2 + ck % 2] = tkn
            for ts_ in range(4):
                bank = C.pb[2 + ts_ % 2]
                vsrc = lat if use_latent else nb_
                v_nch = 2 if use_latent else NCH
                vw = gvu_sb if use_latent else vw_sb
                fns = [lambda t, kc=kc, ts_=ts_, bank=bank, vw=vw, vsrc=vsrc,
                       v_nch=v_nch: t.matmul(
                           bank[:, 0:256], vsrc[kc][:, ts_ * 128:(ts_ + 1) * 128],
                           vw[:, kc * 256:(kc + 1) * 256],
                           start=(kc == 0), stop=(kc == v_nch - 1))
                       for kc in range(v_nch)]
                tmm = em.ops("t", fns,
                             after=(lat_toks if use_latent else nm_toks) +
                             [tok_w, pb_hist.get(2 + ts_ % 2)])
                j = blk * 4 + ts_
                fns2 = [lambda v, hh=hh, j=j, bank=bank: v.tensor_copy(
                    vbar[:, j * 260 + hh * 65:j * 260 + hh * 65 + 64],
                    bank[:, hh * 64:(hh + 1) * 64]) for hh in range(4)]
                tvv = em.ops("v", fns2, after=[tmm, tok_vones])
                pb_hist[2 + ts_ % 2] = tvv
            kv_ready = [tkn, tvv]
            if blk == 0:
                for c_ in range(NCH):
                    dump(dpre + "n2_%d" % c_, nb_[c_][:, 0:TBL], nm_toks)
                dump(dpre + "kg0", kg[0][:, 0:TBL], [tkn])
                dump(dpre + "vb0", vbar[0:128, 0:260], [tvv])
            q_last = None
            for ch in range(NCH):
                buf = ch % 2
                wdep = [wstr_used[buf]] if wstr_used[buf] else []
                wstr_toks[buf] = em.dma("p", wstr[buf][:],
                                        wql[ch * 128:(ch + 1) * 128, :],
                                        after=wdep)
                bank = C.pb[ch % 2]
                fns = [lambda t, kc=kc, buf=buf, bank=bank: t.matmul(
                    bank[:, 0:TBL], wstr[buf][:, kc * 128:(kc + 1) * 128],
                    nb_[kc][:, 0:TBL], start=(kc == 0), stop=(kc == NCH - 1))
                    for kc in range(NCH)]
                tmm = em.ops("t", fns, after=nm_toks + [wstr_toks[buf],
                                                        pb_hist.get(ch % 2)])
                wstr_used[buf] = tmm
                q_last = rope_norm(bank, 128, qg[ch], 0, blk, True,
                                   [tmm, tok_cs])
                pb_hist[ch % 2] = q_last
            for h in range(16):
                kvh = h // 4
                ck, kvp = divmod(kvh, 2)
                b64 = 64 * kvp
                if kvp == 0:
                    ch = LQ_H0.index(h)
                else:
                    ch = LQ_H1.index(h)
                jlo = 0 if not local else max(blk * 4 - 2, 0)
                njc = blk * 4 + 4
                obank = C.pb[2 + h % 2]
                first_dep = pb_hist.get(2 + h % 2)
                idxs = list(range(jlo, njc))
                for ii, jc in enumerate(idxs):
                    sbank_i = ii % 2
                    sbank = C.pb[sbank_i]
                    tmm = em.op("t", lambda t, jc=jc, sbank=sbank, ck=ck,
                                b64=b64, ch=ch: t.matmul(
                                    sbank[:, 0:TBL],
                                    kg[ck][b64:b64 + 64, jc * 128:(jc + 1) * 128],
                                    qg[ch][b64:b64 + 64, 0:TBL],
                                    start=True, stop=True),
                                after=[q_last, pb_hist.get(sbank_i)] + kv_ready)
                    pbi = ii % 2
                    texp = em.op("s", lambda s, pbi=pbi, sbank=sbank:
                                 s.activation(pbuf[pbi][:, 0:TBL],
                                              sbank[:, 0:TBL], AF.Exp,
                                              bias=bias_col[:]),
                                 after=[tmm, tok_bc] +
                                 ([pbuf_hist[pbi]] if pbuf_hist[pbi] else []))
                    pb_hist[sbank_i] = texp
                    tlast = texp
                    if jc >= blk * 4:
                        tlast = em.op("p", lambda g, pbi=pbi, base=tq0 - jc * 128:
                                      g.affine_select(pbuf[pbi][:, 0:TBL],
                                                      pbuf[pbi][:, 0:TBL],
                                                      [[1, TBL]], ALU.is_ge,
                                                      0.0, base=base,
                                                      channel_multiplier=-1),
                                      after=[tlast])
                    if local and jc * 128 < tq0 + TBL - 255:
                        tlast = em.op("p", lambda g, pbi=pbi,
                                      base=jc * 128 - tq0 + 255:
                                      g.affine_select(pbuf[pbi][:, 0:TBL],
                                                      pbuf[pbi][:, 0:TBL],
                                                      [[-1, TBL]], ALU.is_ge,
                                                      0.0, base=base,
                                                      channel_multiplier=1),
                                      after=[tlast])
                    tv = em.op("t", lambda t, jc=jc, ii=ii, pbi=pbi,
                               obank=obank, kvh=kvh, n=len(idxs): t.matmul(
                                   obank[0:65, 0:TBL],
                                   vbar[:, jc * 260 + kvh * 65:
                                        jc * 260 + kvh * 65 + 65],
                                   pbuf[pbi][:, 0:TBL],
                                   start=(ii == 0), stop=(ii == n - 1)),
                               after=[tlast] + ([first_dep] if ii == 0 else []))
                    pbuf_hist[pbi] = tv
                kv_guard = [tv]
                tr = em.op("v", lambda v, obank=obank: v.reciprocal(
                    r11[:, 0:TBL], obank[64:65, 0:TBL]), after=[tv])
                tb2 = em.op("t", lambda t: t.matmul(
                    C.pb[4][0:64, 0:TBL], C.ones_row_f[:, 0:64], r11[:, 0:TBL],
                    start=True, stop=True), after=[tr, pb_hist.get(4)])
                tb3 = em.op("v", lambda v: v.tensor_copy(
                    rb[:, 0:TBL], C.pb[4][0:64, 0:TBL]), after=[tb2])
                pb_hist[4] = tb3
                tao = em.op("v", lambda v, ch=ch, b64=b64, obank=obank:
                            v.tensor_mul(aout[ch][b64:b64 + 64, 0:TBL],
                                         obank[0:64, 0:TBL], rb[:, 0:TBL]),
                            after=[tb3, aout_hist])
                pb_hist[2 + h % 2] = tao
                if blk == 0 and h == 15:
                    for c_ in range(NCH):
                        dump(dpre + "ao_%d" % c_, aout[c_][:, 0:TBL], [tao])
                if blk == 0 and h == 0:
                    dump(dpre + "p0", pbuf[pbi][:, 0:TBL], [tlast])
                    dump(dpre + "q0", qg[0][:, 0:TBL], [q_last])
            aout_last = tao
            res_toks = []
            for oc in range(NCH):
                buf = oc % 2
                wdep = [wstr_used[buf]] if wstr_used[buf] else []
                wstr_toks[buf] = em.dma("p", wstr[buf][:],
                                        wol[oc * 128:(oc + 1) * 128, :],
                                        after=wdep)
                bank = C.pb[5 + oc % 2]
                fns = [lambda t, kc=kc, buf=buf, bank=bank: t.matmul(
                    bank[:, 0:TBL], wstr[buf][:, kc * 128:(kc + 1) * 128],
                    aout[kc][:, 0:TBL], start=(kc == 0), stop=(kc == NCH - 1))
                    for kc in range(NCH)]
                tmm = em.ops("t", fns, after=[aout_last, wstr_toks[buf],
                                              pb_hist.get(5 + oc % 2)])
                wstr_used[buf] = tmm
                qb_hist = tmm
                if blk == 0 and oc == 0 and dbg is not None:
                    tcp9 = em.op("v", lambda v, bank=bank: v.tensor_copy(
                        rot[:, 0:TBL], bank[:, 0:TBL]), after=[tmm])
                    pb_hist[5 + oc % 2] = tcp9
                    dump(dpre + "wops", rot[:, 0:TBL], [tcp9])
                    dump(dpre + "wstr0", wstr[0][:, 0:1024], [tcp9])
                xi = xf32[oc % 2]
                dump_resid = (blk == 0 and oc == 0 and dbg is not None)
                tx = em.dma("y", xi[:, 0:TBL],
                            res1_hbm[128 * oc:128 * (oc + 1), tq0:tq0 + TBL])
                if local:
                    tx2 = em.dma("y", xres[:, 0:TBL],
                                 x_hbm[128 * oc:128 * (oc + 1), tq0:tq0 + TBL],
                                 after=[x2_used] if x2_used else [])
                    tv1 = em.op("v", lambda v, xi=xi, bank=bank: v.tensor_add(
                        osum[:, 0:TBL], xi[:, 0:TBL], bank[:, 0:TBL]),
                        after=[tmm, tx] + ([res_toks[-1]] if res_toks else []))
                    tv2 = em.op("v", lambda v: v.tensor_add(
                        osum[:, 0:TBL], osum[:, 0:TBL], xres[:, 0:TBL]),
                        after=[tv1, tx2])
                    x2_used = tv2
                else:
                    tv2 = em.op("v", lambda v, xi=xi, bank=bank: v.tensor_add(
                        osum[:, 0:TBL], xi[:, 0:TBL], bank[:, 0:TBL]),
                        after=[tmm, tx] + ([res_toks[-1]] if res_toks else []))
                pb_hist[5 + oc % 2] = tv2
                if dump_resid:
                    dump(dpre + "xi0", xi[:, 0:TBL], [tv2])
                    dump(dpre + "os0", osum[:, 0:TBL], [tv2])
                    if local:
                        dump(dpre + "qr0", qr[:, 0:TBL], [tv2])
                res_toks.append(em.dma("y", out_hbm[128 * oc:128 * (oc + 1),
                                                    tq0:tq0 + TBL],
                                       osum[:, 0:TBL], after=[tv2]))
            aout_hist = qb_hist
            prev_res = res_toks[-1]
        return res_toks

    res1 = attn_branch(False, h0b_hbm, h0f_hbm, h1f_hbm, n1c,
                       W["la_gwql"], W["la_gwol"], None, None, None,
                       W["la_cosg"], W["la_sing"], True, [p0_done])
    res2 = attn_branch(True, h1f_hbm, h1f_hbm, xo_hbm, n2c,
                       W["la_lwql"], W["la_lwol"], lwk_sb, None, lwv_sb,
                       W["la_cosl"], W["la_sinl"], False, [res1[-1]])
    return res2


def build_l1attna_program(debug=False):
    nc = bass.Bass("TRN2", target_bir_lowering=False)
    x_in = nc.dram_tensor("x_in", [DM, T], F32, kind="ExternalInput").ap()
    x_out = nc.dram_tensor("x_out", [DM, T], F32, kind="ExternalOutput").ap()
    W = {}
    for name, shape in [("la_gwql", [1024, 1024]), ("la_gwol", [1024, 1024]),
                        ("la_lwql", [1024, 1024]), ("la_lwol", [1024, 1024]),
                        ("la_gkd", [1024, 256]), ("la_gku", [256, 256]),
                        ("la_gvu", [256, 256]), ("la_lwk", [1024, 256]),
                        ("la_lwv", [1024, 256]), ("la_n1", [128, 8]),
                        ("la_n2", [128, 8]), ("la_cosg", [128, T]),
                        ("la_sing", [128, T]), ("la_cosl", [128, T]),
                        ("la_sinl", [128, T])]:
        W[name] = nc.dram_tensor(name, shape, F32, kind="ExternalInput").ap()
    kind = "ExternalOutput" if debug else "Internal"
    h0b = nc.dram_tensor("h0b_s", [DM, T], BF16, kind="Internal").ap()
    h0f = nc.dram_tensor("h0f_s", [DM, T], F32, kind=kind).ap()
    h1f = nc.dram_tensor("h1f_s", [DM, T], F32, kind=kind).ap()
    C = Core(nc)
    dbg = None
    if debug:
        dbg = {}
        for pre in ("lg", "ll"):
            for c_ in range(NCH):
                dbg[pre + "n2_%d" % c_] = nc.dram_tensor(
                    "dbg_" + pre + "n2_%d" % c_, [128, 512], F32,
                    kind="ExternalOutput").ap()
                dbg[pre + "ao_%d" % c_] = nc.dram_tensor(
                    "dbg_" + pre + "ao_%d" % c_, [128, 512], F32,
                    kind="ExternalOutput").ap()
            for nm, sh in [("kg0", [128, 512]), ("vb0", [128, 260]),
                           ("p0", [128, 512]), ("q0", [128, 512]),
                           ("wops", [128, 512]), ("wstr0", [128, 1024]),
                           ("xi0", [128, 512]), ("os0", [128, 512]),
                           ("qr0", [128, 512])]:
                dbg[pre + nm] = nc.dram_tensor("dbg_" + pre + nm, sh, F32,
                                               kind="ExternalOutput").ap()
    res = emit_l1attna(C, x_in, x_out, W, h0b, h0f, h1f, dbg=dbg)
    C.em.finalize(res)
    return nc



def host_prep_l1attnb(inputs):
    wqkv = np.asarray(inputs["L1_attn_b_w_qkv"])
    bqkv = np.asarray(inputs["L1_attn_b_b_qkv"])
    wout = np.asarray(inputs["L1_attn_b_w_out"])
    bout = np.asarray(inputs["L1_attn_b_b_out"])

    def stream_layout(w):
        return np.ascontiguousarray(
            w.reshape(8, 128, 8, 128).transpose(2, 1, 0, 3).reshape(1024, 1024))

    return {"lb_wql": stream_layout(wqkv[:, 0:1024]),
            "lb_wkl": stream_layout(wqkv[:, 1024:2048]),
            "lb_wv": np.ascontiguousarray(wqkv[:, 2048:3072]),
            "lb_wol": stream_layout(wout),
            "lb_bq": np.ascontiguousarray(bqkv[0:1024].reshape(8, 128).T),
            "lb_bk": np.ascontiguousarray(bqkv[1024:2048].reshape(8, 128).T),
            "lb_bv": np.ascontiguousarray(bqkv[None, 2048:3072]),
            "lb_bo": np.ascontiguousarray(bout.reshape(8, 128).T)}


def emit_l1attnb(C, x_hbm, xo_hbm, W, xln_hbm, k_hbm, vb_hbm):
    em = C.em
    TBL = 512
    NB = T // TBL
    wv_sb = C.sb("lbwv", [128, NCH * 1024], BF16)
    tok_w = None
    for kc in range(NCH):
        tok_w = em.dma("p", wv_sb[:, kc * 1024:(kc + 1) * 1024],
                       W["lb_wv"][kc * 128:(kc + 1) * 128, :])
    bq_sb = C.sb("lbbq", [128, 8], F32)
    bk_sb = C.sb("lbbk", [128, 8], F32)
    bv_sb = C.sb("lbbv", [1, 1024], BF16)
    bo_sb = C.sb("lbbo", [128, 8], F32)
    em.dma("y", bq_sb[:], W["lb_bq"])
    em.dma("y", bk_sb[:], W["lb_bk"])
    em.dma("p", bv_sb[:], W["lb_bv"])
    tok_w = em.dma("y", bo_sb[:], W["lb_bo"], after=[tok_w])

    xb = [C.sb("lbxb", [128, TBL], BF16) for _ in range(NCH)]
    xln = [C.sb("lbxln", [128, TBL], BF16) for _ in range(NCH)]
    qbuf = [C.sb("lbq", [128, TBL], BF16) for _ in range(NCH)]
    kbl = [C.sb("lbkb", [128, 768], BF16) for _ in range(NCH)]
    vbl = C.sb("lbvb", [128, 6 * 1040], BF16)
    aout = [C.sb("lbao", [128, TBL], BF16) for _ in range(NCH)]
    pbuf = [C.sb("lbp", [128, TBL], BF16) for _ in range(2)]
    wstr = [C.sb("lbw", [128, 1024], BF16) for _ in range(2)]
    sqs = C.sb("lbsq", [128, TBL], BF16)
    kB = C.sb("lbkB", [1, 16], F32)
    qB = C.sb("lbqB", [1, 16], F32)
    bBr = C.sb("lbbBr", [1, 16], F32)
    r11 = C.sb("lbr11", [1, TBL], F32)
    rb = C.sb("lbrb", [64, TBL], F32)
    bias_col = C.sb("lbbc", [128, 1], F32)
    osum = C.sb("lbos", [128, TBL], F32)
    xfr = [C.sb("lbxf", [128, TBL], F32) for _ in range(2)]
    vsc = C.sb("lbvsc", [128, 1040], BF16)

    tok_ki = em.op("v", lambda v: v.memset(kB[:], -3e38))
    tok_vo = em.op("v", lambda v: v.memset(vsc[:, 64::65], 1.0))

    pb_hist = {}
    wstr_toks = [None, None]
    wstr_used = [None, None]

    st_last = None
    kn_last = tok_ki
    ph_last = []
    for blk in range(NB):
        xln_toks, _ = emit_ln(C, x_hbm, xb, xln, blk, TBL,
                              after=[tok_w, st_last, kn_last] + ph_last)
        st_toks = [em.dma("y", xln_hbm[128 * c:128 * (c + 1),
                                       blk * TBL:(blk + 1) * TBL],
                          xln[c][:, 0:TBL], after=[xln_toks[c]])
                   for c in range(NCH)]
        st_last = st_toks[-1]
        for ck in range(NCH):
            buf = ck % 2
            wdep = [wstr_used[buf]] if wstr_used[buf] else []
            wstr_toks[buf] = em.dma("p", wstr[buf][:],
                                    W["lb_wkl"][ck * 128:(ck + 1) * 128, :],
                                    after=wdep)
            bank = C.pb[ck % 2]
            fns = [lambda t, kc=kc, buf=buf, bank=bank: t.matmul(
                bank[:, 0:TBL], wstr[buf][:, kc * 128:(kc + 1) * 128],
                xln[kc][:, 0:TBL], start=(kc == 0), stop=(kc == NCH - 1))
                for kc in range(NCH)]
            tmm = em.ops("t", fns, after=xln_toks + [wstr_toks[buf],
                                                     pb_hist.get(ck % 2)])
            wstr_used[buf] = tmm
            tev = em.op("v", lambda v, ck=ck, bank=bank: v.tensor_scalar_add(
                sqs[:, 0:TBL], bank[:, 0:TBL], bk_sb[:, ck:ck + 1]),
                after=[tmm, tok_w, kn_last])
            pb_hist[ck % 2] = tev
            tst = em.dma("y", k_hbm[128 * ck:128 * (ck + 1),
                                    blk * TBL:(blk + 1) * TBL],
                         sqs[:, 0:TBL], after=[tev])
            tsq = em.op("s", lambda s: s.activation(sqs[:, 0:TBL],
                                                    sqs[:, 0:TBL], AF.Square),
                        after=[tev, tst])
            for half in range(2):
                h = 2 * ck + half
                bank2 = C.pb[4]
                tm2 = em.op("t", lambda t, half=half, bank2=bank2: t.matmul(
                    bank2[0:1, 0:TBL], C.ones_col_bf[64 * half:64 * half + 64, :],
                    sqs[64 * half:64 * half + 64, 0:TBL], start=True, stop=True),
                    after=[tsq, pb_hist.get(4), C.t_const])
                tr = em.op("v", lambda v, bank2=bank2: v.tensor_reduce(
                    r11[:, 0:1], bank2[0:1, 0:TBL], AX.X, ALU.max), after=[tm2])
                pb_hist[4] = tr
                kn_last = em.op("v", lambda v, h=h: v.tensor_max(
                    kB[:, h:h + 1], kB[:, h:h + 1], r11[:, 0:1]), after=[tr])
        for ts_ in range(4):
            for vc in range(2):
                bank = C.pb[2 + (2 * ts_ + vc) % 2]
                fns = [lambda t, kc=kc, ts_=ts_, vc=vc, bank=bank: t.matmul(
                    bank[:, 0:512], xln[kc][:, ts_ * 128:(ts_ + 1) * 128],
                    wv_sb[:, kc * 1024 + vc * 512:kc * 1024 + (vc + 1) * 512],
                    start=(kc == 0), stop=False) for kc in range(NCH)]
                fns.append(lambda t, vc=vc, bank=bank: t.matmul(
                    bank[:, 0:512], C.ones_row_bf[:],
                    bv_sb[:, vc * 512:(vc + 1) * 512], start=False, stop=True))
                tmm = em.ops("t", fns, after=xln_toks + [tok_w, C.t_const] +
                             ph_last)
                fns2 = [lambda v, hh=hh, vc=vc, bank=bank: v.tensor_copy(
                    vsc[:, (8 * vc + hh) * 65:(8 * vc + hh) * 65 + 64],
                    bank[:, hh * 64:(hh + 1) * 64]) for hh in range(8)]
                tvv = em.ops("v", fns2, after=[tmm, tok_vo, ph_last[-1] if ph_last else None])
                j = blk * 4 + ts_
                tvs = em.dma("y", vb_hbm[:, j * 1040:(j + 1) * 1040], vsc[:],
                             after=[tvv])
                ph_last = [tvv, tvs]
    phaseA = [kn_last, st_last] + ph_last
    o = em.op("s", lambda s: s.activation(kB[:], kB[:], AF.Sqrt), after=phaseA)
    tok_ks = o

    pbuf_hist = [None, None]
    aout_hist = None
    prev_res = None
    qb_hist = None
    kv_used = None
    for blk in range(NB):
        tq0 = blk * TBL
        jlo = min(max(blk * 4 - 1, 0), 26)
        xl_toks = [em.dma("p", xln[c][:, 0:TBL],
                          xln_hbm[128 * c:128 * (c + 1), tq0:tq0 + TBL],
                          after=[st_last, prev_res] + phaseA +
                          ([qb_hist] if qb_hist else []))
                   for c in range(NCH)]
        kv_toks = [em.dma("y", kbl[ck][:],
                          k_hbm[128 * ck:128 * (ck + 1),
                                jlo * 128:jlo * 128 + 768],
                          after=[kv_used] + phaseA) for ck in range(NCH)]
        kv_toks.append(em.dma("y", vbl[:],
                              vb_hbm[:, jlo * 1040:jlo * 1040 + 6 * 1040],
                              after=[kv_used] + phaseA))
        q_toks = []
        for ch in range(NCH):
            buf = ch % 2
            wdep = [wstr_used[buf]] if wstr_used[buf] else []
            wstr_toks[buf] = em.dma("p", wstr[buf][:],
                                    W["lb_wql"][ch * 128:(ch + 1) * 128, :],
                                    after=wdep)
            bank = C.pb[5 + ch % 2]
            fns = [lambda t, kc=kc, buf=buf, bank=bank: t.matmul(
                bank[:, 0:TBL], wstr[buf][:, kc * 128:(kc + 1) * 128],
                xln[kc][:, 0:TBL], start=(kc == 0), stop=(kc == NCH - 1))
                for kc in range(NCH)]
            tmm = em.ops("t", fns, after=xl_toks + [wstr_toks[buf],
                                                    pb_hist.get(5 + ch % 2)])
            wstr_used[buf] = tmm
            tev = em.op("v", lambda v, ch=ch, bank=bank: v.tensor_scalar(
                qbuf[ch][:, 0:TBL], bank[:, 0:TBL], bq_sb[:, ch:ch + 1], 0.125,
                ALU.add, ALU.mult), after=[tmm, tok_w] +
                ([qb_hist] if qb_hist else []))
            pb_hist[5 + ch % 2] = tev
            tsq = em.op("s", lambda s, ch=ch: s.activation(
                sqs[:, 0:TBL], qbuf[ch][:, 0:TBL], AF.Square), after=[tev])
            for half in range(2):
                h = 2 * ch + half
                bank2 = C.pb[4]
                tm2 = em.op("t", lambda t, half=half, bank2=bank2: t.matmul(
                    bank2[0:1, 0:TBL], C.ones_col_bf[64 * half:64 * half + 64, :],
                    sqs[64 * half:64 * half + 64, 0:TBL], start=True, stop=True),
                    after=[tsq, pb_hist.get(4)])
                tr = em.op("v", lambda v, bank2=bank2: v.tensor_reduce(
                    r11[:, 0:1], bank2[0:1, 0:TBL], AX.X, ALU.max), after=[tm2])
                pb_hist[4] = tr
                q_toks.append(em.op("v", lambda v, h=h: v.tensor_copy(
                    qB[:, h:h + 1], r11[:, 0:1]), after=[tr]))
        o = em.op("s", lambda s: s.activation(qB[:], qB[:], AF.Sqrt),
                  after=q_toks)
        o = em.op("v", lambda v: v.tensor_mul(bBr[:], qB[:], kB[:]),
                  after=[o, tok_ks])
        tok_bB = em.op("v", lambda v: v.tensor_scalar_mul(bBr[:], bBr[:], -1.0),
                       after=[o])
        for h in range(16):
            ch, hp = divmod(h, 2)
            b64 = 64 * hp
            tb = em.op("t", lambda t, h=h: t.matmul(
                C.pb[4][:, 0:1], C.ones_row_f[:], bBr[0:1, h:h + 1],
                start=True, stop=True), after=[tok_bB, pb_hist.get(4)])
            tbc = em.op("v", lambda v: v.tensor_copy(bias_col[:],
                                                     C.pb[4][:, 0:1]),
                        after=[tb])
            pb_hist[4] = tbc
            obank = C.pb[2 + h % 2]
            first_dep = pb_hist.get(2 + h % 2)
            for i in range(6):
                jc = jlo + i
                sbank_i = i % 2
                sbank = C.pb[sbank_i]
                tmm = em.op("t", lambda t, i=i, sbank=sbank, ch=ch, b64=b64:
                            t.matmul(sbank[:, 0:TBL],
                                     kbl[ch][b64:b64 + 64, i * 128:(i + 1) * 128],
                                     qbuf[ch][b64:b64 + 64, 0:TBL],
                                     start=True, stop=True),
                            after=[q_toks[-1], pb_hist.get(sbank_i)] + kv_toks)
                pbi = i % 2
                texp = em.op("s", lambda s, pbi=pbi, sbank=sbank: s.activation(
                    pbuf[pbi][:, 0:TBL], sbank[:, 0:TBL], AF.Exp,
                    bias=bias_col[:]),
                    after=[tmm, tbc] +
                    ([pbuf_hist[pbi]] if pbuf_hist[pbi] else []))
                pb_hist[sbank_i] = texp
                tm1 = em.op("p", lambda g, pbi=pbi, jc=jc, tq0=tq0:
                            g.affine_select(pbuf[pbi][:, 0:TBL],
                                            pbuf[pbi][:, 0:TBL],
                                            [[1, TBL]], ALU.is_ge, 0.0,
                                            base=tq0 - jc * 128 + 64,
                                            channel_multiplier=-1),
                            after=[texp])
                tlast = em.op("p", lambda g, pbi=pbi, jc=jc, tq0=tq0:
                              g.affine_select(pbuf[pbi][:, 0:TBL],
                                              pbuf[pbi][:, 0:TBL],
                                              [[-1, TBL]], ALU.is_ge, 0.0,
                                              base=jc * 128 - tq0 + 64,
                                              channel_multiplier=1),
                              after=[tm1])
                tv = em.op("t", lambda t, i=i, pbi=pbi, obank=obank, h=h:
                           t.matmul(obank[0:65, 0:TBL],
                                    vbl[:, i * 1040 + h * 65:
                                        i * 1040 + h * 65 + 65],
                                    pbuf[pbi][:, 0:TBL],
                                    start=(i == 0), stop=(i == 5)),
                           after=[tlast] + ([first_dep] if i == 0 else []))
                pbuf_hist[pbi] = tv
                kv_used = tv
            tr = em.op("v", lambda v, obank=obank: v.reciprocal(
                r11[:, 0:TBL], obank[64:65, 0:TBL]), after=[tv])
            tb2 = em.op("t", lambda t: t.matmul(
                C.pb[4][0:64, 0:TBL], C.ones_row_f[:, 0:64], r11[:, 0:TBL],
                start=True, stop=True), after=[tr, pb_hist.get(4)])
            tb3 = em.op("v", lambda v: v.tensor_copy(rb[:, 0:TBL],
                                                     C.pb[4][0:64, 0:TBL]),
                        after=[tb2])
            pb_hist[4] = tb3
            tao = em.op("v", lambda v, ch=ch, b64=b64, obank=obank:
                        v.tensor_mul(aout[ch][b64:b64 + 64, 0:TBL],
                                     obank[0:64, 0:TBL], rb[:, 0:TBL]),
                        after=[tb3, aout_hist])
            pb_hist[2 + h % 2] = tao
        aout_last = tao
        res_toks = []
        for oc in range(NCH):
            buf = oc % 2
            wdep = [wstr_used[buf]] if wstr_used[buf] else []
            wstr_toks[buf] = em.dma("p", wstr[buf][:],
                                    W["lb_wol"][oc * 128:(oc + 1) * 128, :],
                                    after=wdep)
            bank = C.pb[5 + oc % 2]
            fns = [lambda t, kc=kc, buf=buf, bank=bank: t.matmul(
                bank[:, 0:TBL], wstr[buf][:, kc * 128:(kc + 1) * 128],
                aout[kc][:, 0:TBL], start=(kc == 0), stop=(kc == NCH - 1))
                for kc in range(NCH)]
            tmm = em.ops("t", fns, after=[aout_last, wstr_toks[buf],
                                          pb_hist.get(5 + oc % 2)])
            wstr_used[buf] = tmm
            qb_hist = tmm
            xi = xfr[oc % 2]
            tx = em.dma("y", xi[:, 0:TBL],
                        x_hbm[128 * oc:128 * (oc + 1), tq0:tq0 + TBL])
            tv2 = em.op("v", lambda v, oc=oc, xi=xi, bank=bank:
                        v.scalar_tensor_tensor(osum[:, 0:TBL], bank[:, 0:TBL],
                                               bo_sb[:, oc:oc + 1],
                                               xi[:, 0:TBL], ALU.add, ALU.add),
                        after=[tmm, tx] + ([res_toks[-1]] if res_toks else []))
            pb_hist[5 + oc % 2] = tv2
            res_toks.append(em.dma("y", xo_hbm[128 * oc:128 * (oc + 1),
                                               tq0:tq0 + TBL],
                                   osum[:, 0:TBL], after=[tv2]))
        aout_hist = qb_hist
        prev_res = res_toks[-1]
    return res_toks


def build_l1attnb_program():
    nc = bass.Bass("TRN2", target_bir_lowering=False)
    x_in = nc.dram_tensor("x_in", [DM, T], F32, kind="ExternalInput").ap()
    x_out = nc.dram_tensor("x_out", [DM, T], F32, kind="ExternalOutput").ap()
    W = {}
    for name, shape in [("lb_wql", [1024, 1024]), ("lb_wkl", [1024, 1024]),
                        ("lb_wv", [1024, 1024]), ("lb_wol", [1024, 1024]),
                        ("lb_bq", [128, 8]), ("lb_bk", [128, 8]),
                        ("lb_bv", [1, 1024]), ("lb_bo", [128, 8])]:
        W[name] = nc.dram_tensor(name, shape, F32, kind="ExternalInput").ap()
    xln_hbm = nc.dram_tensor("xln_s", [DM, T], BF16, kind="Internal").ap()
    k_hbm = nc.dram_tensor("k_s", [DM, T], BF16, kind="Internal").ap()
    vb_hbm = nc.dram_tensor("vb_s", [128, 32 * 1040], BF16, kind="Internal").ap()
    C = Core(nc)
    res = emit_l1attnb(C, x_in, x_out, W, xln_hbm, k_hbm, vb_hbm)
    C.em.finalize(res)
    return nc



def host_prep_l0attna(inputs):
    wq = np.asarray(inputs["L0_attn_a_Wq"])
    wk = np.asarray(inputs["L0_attn_a_Wk"])
    wv = np.asarray(inputs["L0_attn_a_Wv"])
    wo = np.asarray(inputs["L0_attn_a_Wo"])
    bq = np.asarray(inputs["L0_attn_a_bq"])
    bk = np.asarray(inputs["L0_attn_a_bk"])
    bv = np.asarray(inputs["L0_attn_a_bv"])
    bo = np.asarray(inputs["L0_attn_a_bo"])
    qph = np.asarray(inputs["L0_attn_a_q_phase"]).reshape(256)
    kph = np.asarray(inputs["L0_attn_a_k_phase"]).reshape(256)
    temp = np.asarray(inputs["L0_attn_a_temperature"]).reshape(1, 1)

    def pad_w(w):
        wp = np.zeros((1024, 768), np.float32)
        for h in range(16):
            c, j = divmod(h, 3)
            wp[:, 128 * c + 32 * j:128 * c + 32 * j + 16] = \
                w[:, 16 * h:16 * h + 16]
        return wp.reshape(8, 128, 6, 128).transpose(2, 1, 0, 3).reshape(768, 1024)

    def pad_b(b):
        bp = np.zeros((128, 6), np.float32)
        for h in range(16):
            c, j = divmod(h, 3)
            bp[32 * j:32 * j + 16, c] = b[16 * h:16 * h + 16]
        return bp

    wol = wo.reshape(8, 128, 8, 128).transpose(2, 1, 0, 3).reshape(8 * 128, 1024)
    e3 = np.zeros((128, 3), np.float32)
    e3t = np.zeros((3, 128), np.float32)
    for j in range(3):
        e3[32 * j:32 * j + 16, j] = 1.0
        e3t[j, 32 * j:32 * j + 16] = 1.0
    return {"aa_wqp": np.ascontiguousarray(pad_w(wq)),
            "aa_wkp": np.ascontiguousarray(pad_w(wk)),
            "aa_wv": np.ascontiguousarray(wv),
            "aa_wol": np.ascontiguousarray(wol),
            "aa_bq": pad_b(bq + 0.0), "aa_qph": pad_b(qph),
            "aa_bk": pad_b(bk + 0.0), "aa_kph": pad_b(kph),
            "aa_bv": np.ascontiguousarray(bv[None, :]),
            "aa_bo": np.ascontiguousarray(bo.reshape(8, 128).T),
            "aa_temp": temp, "aa_e3": e3, "aa_e3t": e3t}


def emit_l0attna(C, x_hbm, xo_hbm, W, xln_hbm):
    em = C.em
    TBL = 512
    NB = T // TBL
    NPC = 6
    wv_sb = C.sb("aawv", [128, NCH * 1024], BF16)
    tok_w = None
    for kc in range(NCH):
        tok_w = em.dma("p", wv_sb[:, kc * 1024:(kc + 1) * 1024],
                       W["aa_wv"][kc * 128:(kc + 1) * 128, :])
    bqc = C.sb("aabq", [128, NPC], F32)
    bkc = C.sb("aabk", [128, NPC], F32)
    tmp6 = C.sb("aat6", [128, NPC], F32)
    em.dma("y", bqc[:], W["aa_bq"])
    em.dma("y", bkc[:], W["aa_bk"])
    t_b1 = em.dma("y", tmp6[:], W["aa_qph"])
    t_b2 = em.op("v", lambda v: v.tensor_add(bqc[:], bqc[:], tmp6[:]),
                 after=[t_b1])
    t_b3 = em.dma("y", tmp6[:], W["aa_kph"], after=[t_b2])
    tok_b = em.op("v", lambda v: v.tensor_add(bkc[:], bkc[:], tmp6[:]),
                  after=[t_b3])
    bo_sb = C.sb("aabo", [128, 8], F32)
    bv_sb = C.sb("aabv", [1, 1024], BF16)
    e3_sb = C.sb("aae3", [128, 3], BF16)
    e3t_sb = C.sb("aae3t", [3, 128], F32)
    tok_w = em.dma("y", bo_sb[:], W["aa_bo"], after=[tok_w])
    tok_w = em.dma("p", bv_sb[:], W["aa_bv"], after=[tok_w])
    tok_w = em.dma("p", e3_sb[:], W["aa_e3"], after=[tok_w])
    tok_w = em.dma("y", e3t_sb[:], W["aa_e3t"], after=[tok_w])
    tsc = C.sb("aats", [1, 1], F32)
    it3 = C.sb("aait3", [128, 1], F32)
    bias_col = C.sb("aabc", [128, 1], F32)
    tt0 = em.dma("y", tsc[:], W["aa_temp"])
    tt1 = em.op("v", lambda v: v.reciprocal(tsc[:], tsc[:]), after=[tt0])
    tt2 = em.op("t", lambda t: t.matmul(C.pb[4][:, 0:1], C.ones_row_f[:],
                                        tsc[:], start=True, stop=True),
                after=[tt1, C.t_const])
    tt3 = em.op("v", lambda v: v.tensor_copy(it3[:], C.pb[4][:, 0:1]),
                after=[tt2])
    tt4a = em.op("v", lambda v: v.tensor_scalar_mul(bias_col[:], it3[:], -1.0),
                 after=[tt3])
    tt4 = em.op("v", lambda v: v.tensor_max(bias_col[:], bias_col[:], it3[:]),
                after=[tt4a])
    tt5 = em.op("v", lambda v: v.tensor_scalar_min(bias_col[:], bias_col[:],
                                                   40.0), after=[tt4])
    tok_bias = em.op("v", lambda v: v.tensor_scalar_mul(bias_col[:],
                                                        bias_col[:], -1.0),
                     after=[tt5])

    xb = [C.sb("axb", [128, TBL], BF16) for _ in range(NCH)]
    xln = [C.sb("axln", [128, TBL], BF16) for _ in range(NCH)]
    kt = [C.sb("akt", [128, T], BF16) for _ in range(NPC)]
    vbar = C.sb("avbar", [128, 32 * (16 * 65)], BF16)
    qt = [C.sb("aqt", [128, TBL], BF16) for _ in range(NPC)]
    aout = [C.sb("aao", [128, TBL], BF16) for _ in range(NCH)]
    pbuf = [C.sb("apb", [128, TBL], BF16) for _ in range(2)]
    wstr = [C.sb("awstr", [128, 1024], BF16) for _ in range(2)]
    qraw = C.sb("aqraw", [128, TBL], F32)
    sqs = C.sb("asq", [128, TBL], BF16)
    nrow = C.sb("anr", [3, TBL], F32)
    r11 = C.sb("ar11", [1, TBL], F32)
    rb = C.sb("arb", [64, TBL], F32)
    osum = C.sb("aosum", [128, TBL], F32)
    xfr = [C.sb("axf", [128, TBL], F32) for _ in range(2)]

    tok_vones = em.op("v", lambda v: v.memset(vbar[:, 64::65], 1.0))

    def h_slot(h):
        c, j = divmod(h, 3)
        return c, 32 * j

    def qk_chain(src_w, pc, blk, bias6, dst, dst_off, xln_toks, extra,
                 scale_inv, wbuf, wtoks, wused):
        buf = pc % 2
        wdep = [wused[buf]] if wused[buf] else []
        wtoks[buf] = em.dma("p", wbuf[buf][:],
                            src_w[pc * 128:(pc + 1) * 128, :], after=wdep)
        bank = C.pb[pc % 2]
        fns = [lambda t, kc=kc, buf=buf, bank=bank: t.matmul(
            bank[:, 0:TBL], wbuf[buf][:, kc * 128:(kc + 1) * 128],
            xln[kc][:, 0:TBL], start=(kc == 0), stop=(kc == NCH - 1))
            for kc in range(NCH)]
        tmm = em.ops("t", fns, after=xln_toks + [wtoks[buf]] + extra)
        wused[buf] = tmm
        t1 = em.op("v", lambda v, pc=pc, bank=bank: v.tensor_scalar_add(
            qraw[:, 0:TBL], bank[:, 0:TBL], bias6[:, pc:pc + 1]),
            after=[tmm, tok_b])
        t2 = em.op("s", lambda s: s.activation(sqs[:, 0:TBL], qraw[:, 0:TBL],
                                               AF.Square), after=[t1])
        t3 = em.op("t", lambda t: t.matmul(C.pb[4][0:3, 0:TBL], e3_sb[:],
                                           sqs[:, 0:TBL], start=True,
                                           stop=True),
                   after=[t2, tok_w, pb_hist.get(4)])
        t4 = em.op("s", lambda s: s.activation(nrow[:, 0:TBL],
                                               C.pb[4][0:3, 0:TBL], AF.Sqrt),
                   after=[t3])
        t5 = em.op("v", lambda v: v.tensor_scalar_max(nrow[:, 0:TBL],
                                                      nrow[:, 0:TBL], 1e-12),
                   after=[t4])
        t6 = em.op("v", lambda v: v.reciprocal(nrow[:, 0:TBL], nrow[:, 0:TBL]),
                   after=[t5])
        if scale_inv:
            t6 = em.op("v", lambda v: v.tensor_scalar_mul(
                nrow[:, 0:TBL], nrow[:, 0:TBL], it3[0:3, 0:1]),
                after=[t6, tt3])
        t7 = em.op("t", lambda t: t.matmul(C.pb[5][:, 0:TBL], e3t_sb[:],
                                           nrow[:, 0:TBL], start=True,
                                           stop=True),
                   after=[t6, tok_w, pb_hist.get(5)])
        pb_hist[4] = t7
        t8 = em.op("v", lambda v, dst=dst, dst_off=dst_off: v.tensor_mul(
            dst[:, dst_off:dst_off + TBL], qraw[:, 0:TBL], C.pb[5][:, 0:TBL]),
            after=[t7])
        pb_hist[5] = t8
        pb_hist[pc % 2] = t1
        return t8

    pb_hist = {}
    wstr_toks = [None, None]
    wstr_used = [None, None]
    st_last = None
    ph_last = []
    for blk in range(NB):
        xln_toks, _ = emit_ln(C, x_hbm, xb, xln, blk, TBL,
                              after=[tok_w] + ph_last)
        st_toks = [em.dma("y", xln_hbm[128 * c:128 * (c + 1),
                                       blk * TBL:(blk + 1) * TBL],
                          xln[c][:, 0:TBL], after=[xln_toks[c]])
                   for c in range(NCH)]
        st_last = st_toks[-1]
        for pc in range(NPC):
            tk8 = qk_chain(W["aa_wkp"], pc, blk, bkc, kt[pc], blk * TBL,
                           xln_toks, ph_last, False, wstr, wstr_toks, wstr_used)
            ph_last = [tk8]
        for ts_ in range(4):
            for vc in range(2):
                bank = C.pb[2 + (2 * ts_ + vc) % 2]
                fns = [lambda t, kc=kc, ts_=ts_, vc=vc, bank=bank: t.matmul(
                    bank[:, 0:512], xln[kc][:, ts_ * 128:(ts_ + 1) * 128],
                    wv_sb[:, kc * 1024 + vc * 512:kc * 1024 + (vc + 1) * 512],
                    start=(kc == 0), stop=False) for kc in range(NCH)]
                fns.append(lambda t, vc=vc, bank=bank: t.matmul(
                    bank[:, 0:512], C.ones_row_bf[:],
                    bv_sb[:, vc * 512:(vc + 1) * 512], start=False, stop=True))
                tmm = em.ops("t", fns,
                             after=xln_toks + [tok_w, C.t_const] + ph_last)
                j = blk * 4 + ts_
                fns2 = [lambda v, hh=hh, j=j, vc=vc, bank=bank: v.tensor_copy(
                    vbar[:, j * 1040 + (8 * vc + hh) * 65:
                         j * 1040 + (8 * vc + hh) * 65 + 64],
                    bank[:, hh * 64:(hh + 1) * 64]) for hh in range(8)]
                tvv = em.ops("v", fns2, after=[tmm, tok_vones])
                ph_last = [tvv]
    phaseA = ph_last + [st_last]

    pbuf_hist = [None, None]
    aout_hist = None
    prev_res = None
    qb_hist = None
    for blk in range(NB):
        tq0 = blk * TBL
        xl_toks = [em.dma("p", xln[c][:, 0:TBL],
                          xln_hbm[128 * c:128 * (c + 1), tq0:tq0 + TBL],
                          after=[st_last, prev_res] + phaseA +
                          ([qb_hist] if qb_hist else []))
                   for c in range(NCH)]
        q_toks = []
        for pc in range(NPC):
            t8 = qk_chain(W["aa_wqp"], pc, blk, bqc, qt[pc], 0,
                          xl_toks, phaseA + ([qb_hist] if qb_hist else []),
                          True, wstr, wstr_toks, wstr_used)
            q_toks.append(t8)
        for h in range(16):
            hc, hb = h_slot(h)
            nj = blk * 4 + 4
            obank = C.pb[2 + h % 2]
            first_dep = pb_hist.get(2 + h % 2)
            for jc in range(nj):
                masked = jc >= blk * 4
                sbank_i = jc % 2
                sbank = C.pb[sbank_i]
                tmm = em.op("t", lambda t, jc=jc, sbank=sbank, hc=hc, hb=hb:
                            t.matmul(sbank[:, 0:TBL],
                                     kt[hc][hb:hb + 16, jc * 128:(jc + 1) * 128],
                                     qt[hc][hb:hb + 16, 0:TBL],
                                     start=True, stop=True),
                            after=[q_toks[hc % NPC] if False else q_toks[-1],
                                   pb_hist.get(sbank_i)] + phaseA)
                pbi = jc % 2
                texp = em.op("s", lambda s, pbi=pbi, sbank=sbank: s.activation(
                    pbuf[pbi][:, 0:TBL], sbank[:, 0:TBL], AF.Exp,
                    bias=bias_col[:]),
                    after=[tmm, tok_bias] +
                    ([pbuf_hist[pbi]] if pbuf_hist[pbi] else []))
                pb_hist[sbank_i] = texp
                tlast = texp
                if masked:
                    base = tq0 - jc * 128
                    tlast = em.op("p", lambda g, pbi=pbi, base=base:
                                  g.affine_select(pbuf[pbi][:, 0:TBL],
                                                  pbuf[pbi][:, 0:TBL],
                                                  [[1, TBL]], ALU.is_ge, 0.0,
                                                  base=base,
                                                  channel_multiplier=-1),
                                  after=[texp])
                tv = em.op("t", lambda t, jc=jc, pbi=pbi, obank=obank, h=h,
                           nj=nj: t.matmul(
                               obank[0:65, 0:TBL],
                               vbar[:, jc * 1040 + h * 65:jc * 1040 + h * 65 + 65],
                               pbuf[pbi][:, 0:TBL],
                               start=(jc == 0), stop=(jc == nj - 1)),
                           after=[tlast] + ([first_dep] if jc == 0 else []))
                pbuf_hist[pbi] = tv
            tr = em.op("v", lambda v, obank=obank: v.reciprocal(
                r11[:, 0:TBL], obank[64:65, 0:TBL]), after=[tv])
            tb2 = em.op("t", lambda t: t.matmul(
                C.pb[4][0:64, 0:TBL], C.ones_row_f[:, 0:64], r11[:, 0:TBL],
                start=True, stop=True), after=[tr, pb_hist.get(4)])
            tb3 = em.op("v", lambda v: v.tensor_copy(rb[:, 0:TBL],
                                                     C.pb[4][0:64, 0:TBL]),
                        after=[tb2])
            pb_hist[4] = tb3
            tao = em.op("v", lambda v, h=h, obank=obank: v.tensor_mul(
                aout[h // 2][64 * (h % 2):64 * (h % 2) + 64, 0:TBL],
                obank[0:64, 0:TBL], rb[:, 0:TBL]),
                after=[tb3, aout_hist])
            pb_hist[2 + h % 2] = tao
        aout_last = tao
        res_toks = []
        for oc in range(NCH):
            buf = oc % 2
            wdep = [wstr_used[buf]] if wstr_used[buf] else []
            wstr_toks[buf] = em.dma("p", wstr[buf][:],
                                    W["aa_wol"][oc * 128:(oc + 1) * 128, :],
                                    after=wdep)
            bank = C.pb[5 + oc % 2]
            fns = [lambda t, kc=kc, buf=buf, bank=bank: t.matmul(
                bank[:, 0:TBL], wstr[buf][:, kc * 128:(kc + 1) * 128],
                aout[kc][:, 0:TBL], start=(kc == 0), stop=(kc == NCH - 1))
                for kc in range(NCH)]
            tmm = em.ops("t", fns, after=[aout_last, wstr_toks[buf],
                                          pb_hist.get(5 + oc % 2)])
            wstr_used[buf] = tmm
            qb_hist = tmm
            xi = xfr[oc % 2]
            tx = em.dma("y", xi[:, 0:TBL],
                        x_hbm[128 * oc:128 * (oc + 1), tq0:tq0 + TBL])
            tv2 = em.op("v", lambda v, oc=oc, xi=xi, bank=bank:
                        v.scalar_tensor_tensor(osum[:, 0:TBL], bank[:, 0:TBL],
                                               bo_sb[:, oc:oc + 1],
                                               xi[:, 0:TBL], ALU.add, ALU.add),
                        after=[tmm, tx] + ([res_toks[-1]] if res_toks else []))
            pb_hist[5 + oc % 2] = tv2
            res_toks.append(em.dma("y", xo_hbm[128 * oc:128 * (oc + 1),
                                               tq0:tq0 + TBL],
                                   osum[:, 0:TBL], after=[tv2]))
        aout_hist = qb_hist
        prev_res = res_toks[-1]
    return res_toks


def build_l0attna_program():
    nc = bass.Bass("TRN2", target_bir_lowering=False)
    x_in = nc.dram_tensor("x_in", [DM, T], F32, kind="ExternalInput").ap()
    x_out = nc.dram_tensor("x_out", [DM, T], F32, kind="ExternalOutput").ap()
    W = {}
    for name, shape in [("aa_wqp", [768, 1024]), ("aa_wkp", [768, 1024]),
                        ("aa_wv", [1024, 1024]), ("aa_wol", [1024, 1024]),
                        ("aa_bq", [128, 6]), ("aa_qph", [128, 6]),
                        ("aa_bk", [128, 6]), ("aa_kph", [128, 6]),
                        ("aa_bv", [1, 1024]), ("aa_bo", [128, 8]),
                        ("aa_temp", [1, 1]), ("aa_e3", [128, 3]),
                        ("aa_e3t", [3, 128])]:
        W[name] = nc.dram_tensor(name, shape, F32, kind="ExternalInput").ap()
    xln_hbm = nc.dram_tensor("xln_s", [DM, T], BF16, kind="Internal").ap()
    C = Core(nc)
    res = emit_l0attna(C, x_in, x_out, W, xln_hbm)
    C.em.finalize(res)
    return nc


def build_l0attnb_program(debug=False):
    nc = bass.Bass("TRN2", target_bir_lowering=False)
    x_in = nc.dram_tensor("x_in", [DM, T], F32, kind="ExternalInput").ap()
    x_out = nc.dram_tensor("x_out", [DM, T], F32, kind="ExternalOutput").ap()
    W = {}
    for name, shape in [("ab_wql", [1024, 1024]), ("ab_wkT", [1024, 512]),
                        ("ab_wvT", [1024, 512]), ("ab_wol", [1024, 1024]),
                        ("ab_bo", [128, 8]), ("ab_qaux", [128, T])]:
        W[name] = nc.dram_tensor(name, shape, F32, kind="ExternalInput").ap()
    xln_hbm = nc.dram_tensor("xln_s", [DM, T], BF16, kind="Internal").ap()
    C = Core(nc)
    dbg = None
    if debug:
        dbg = {}
        for name, shape in [("ksr", [1, 16]), ("kb0", [128, 512]),
                            ("bBr", [1, 16]), ("qb0", [128, 512]),
                            ("p00", [128, 512]), ("rec0", [1, 512]),
                            ("ao0", [128, 512]), ("s00", [128, 512]),
                            ("aoall", [1024, 512]),
                            ("aux00", [68, 512]), ("qa00", [68, 512])]:
            dbg[name] = nc.dram_tensor("dbg_" + name, shape, F32,
                                       kind="ExternalOutput").ap()
    res = emit_l0attnb(C, x_in, x_out, W, xln_hbm, dbg=dbg)
    C.em.finalize(res)
    return nc


def emit_cast_hbm(C, src_f32, dst_bf16, rows, cols, tag):
    em = C.em
    buf = [C.sb(tag, [128, cols], BF16) for _ in range(2)]
    tok_st = [None, None]
    for rc in range(rows // 128):
        b = rc % 2
        tl = em.dma("p", buf[b][:], src_f32[rc * 128:(rc + 1) * 128, :],
                    after=[tok_st[b]] if tok_st[b] else [])
        tok_st[b] = em.dma("y", dst_bf16[rc * 128:(rc + 1) * 128, :], buf[b][:],
                           after=[tl])
    return tok_st[(rows // 128 - 1) % 2]



def build_l1ffn_program():
    nc = bass.Bass("TRN2", target_bir_lowering=False)
    x_in = nc.dram_tensor("x_in", [DM, T], F32, kind="ExternalInput").ap()
    x_out = nc.dram_tensor("x_out", [DM, T], F32, kind="ExternalOutput").ap()
    W = {
        "l1f_bd1": nc.dram_tensor("l1f_bd1", [128, 4 * 8 * 128], F32,
                                  kind="ExternalInput").ap(),
        "l1f_bd2": nc.dram_tensor("l1f_bd2", [128, 4 * 32 * 128], F32,
                                  kind="ExternalInput").ap(),
        "l1f_cpT": nc.dram_tensor("l1f_cpT", [16384, 1024], F32,
                                  kind="ExternalInput").ap(),
    }
    cpbf = nc.dram_tensor("cpbf", [16384, 1024], BF16, kind="Internal").ap()
    C = Core(nc)
    tok_cast = emit_cast_hbm(C, W["l1f_cpT"], cpbf, 16384, 1024, "cpcast")
    res = emit_l1ffn(C, x_in, x_out, W, cpbf, tok_cast)
    C.em.finalize(res)
    return nc


def build_l0ffn_program():
    nc = bass.Bass("TRN2", target_bir_lowering=False)
    x_in = nc.dram_tensor("x_in", [DM, T], F32, kind="ExternalInput").ap()
    x_out = nc.dram_tensor("x_out", [DM, T], F32, kind="ExternalOutput").ap()
    W = {}
    for name, shape in [("ff_w1l", [8 * 32 * 128, 8 * 128]),
                        ("ff_w2l", [8 * 8 * 128, 32 * 128]),
                        ("ff_b1l", [128, 8 * 32]), ("ff_b2", [8, 1024]),
                        ("ff_rl", [1024, 12]), ("ff_rlb", [1, 12]),
                        ("ff_oh", [8, 8 * 128])]:
        W[name] = nc.dram_tensor(name, shape, F32, kind="ExternalInput").ap()
    w1bf = nc.dram_tensor("w1bf", [8 * 32 * 128, 8 * 128], BF16,
                          kind="Internal").ap()
    w2bf = nc.dram_tensor("w2bf", [8 * 8 * 128, 32 * 128], BF16,
                          kind="Internal").ap()
    xln_hbm = nc.dram_tensor("xln_s", [DM, T], BF16, kind="Internal").ap()
    C = Core(nc)
    t1 = emit_cast_hbm(C, W["ff_w1l"], w1bf, 8 * 32 * 128, 8 * 128, "w1cast")
    t2 = emit_cast_hbm(C, W["ff_w2l"], w2bf, 8 * 8 * 128, 32 * 128, "w2cast")
    res = emit_l0ffn(C, x_in, x_out, W, w1bf, w2bf, t2, xln_hbm)
    C.em.finalize(res)
    return nc


SUBLAYERS = ["L0_attn_a", "L0_ffn_1", "L0_attn_b", "L0_ffn_2",
             "L1_attn_a", "L1_ffn_1", "L1_attn_b", "L1_ffn_2"]

_BUILDERS = {
    "L0_attn_a": (build_l0attna_program, host_prep_l0attna),
    "L0_ffn": (build_l0ffn_program, host_prep_l0ffn),
    "L0_attn_b": (build_l0attnb_program, host_prep_l0attnb),
    "L1_attn_a": (build_l1attna_program, host_prep_l1attna),
    "L1_ffn": (build_l1ffn_program, host_prep_l1ffn),
    "L1_attn_b": (build_l1attnb_program, host_prep_l1attnb),
}

_W_SHAPES = {
    "aa_wqp": [768, 1024], "aa_wkp": [768, 1024], "aa_wv": [1024, 1024],
    "aa_wol": [1024, 1024], "aa_bq": [128, 6], "aa_qph": [128, 6],
    "aa_bk": [128, 6], "aa_kph": [128, 6], "aa_bv": [1, 1024],
    "aa_bo": [128, 8], "aa_temp": [1, 1], "aa_e3": [128, 3],
    "aa_e3t": [3, 128],
    "ff_w1l": [8 * 32 * 128, 8 * 128], "ff_w2l": [8 * 8 * 128, 32 * 128],
    "ff_b1l": [128, 8 * 32], "ff_b2": [8, 1024], "ff_rl": [1024, 12],
    "ff_rlb": [1, 12], "ff_oh": [8, 8 * 128],
    "ab_wql": [1024, 1024], "ab_wkT": [1024, 512], "ab_wvT": [1024, 512],
    "ab_wol": [1024, 1024], "ab_bo": [128, 8], "ab_qaux": [128, T],
    "la_gwql": [1024, 1024], "la_gwol": [1024, 1024],
    "la_lwql": [1024, 1024], "la_lwol": [1024, 1024],
    "la_gkd": [1024, 256], "la_gku": [256, 256], "la_gvu": [256, 256],
    "la_lwk": [1024, 256], "la_lwv": [1024, 256], "la_n1": [128, 8],
    "la_n2": [128, 8], "la_cosg": [128, T], "la_sing": [128, T],
    "la_cosl": [128, T], "la_sinl": [128, T],
    "lb_wql": [1024, 1024], "lb_wkl": [1024, 1024], "lb_wv": [1024, 1024],
    "lb_wol": [1024, 1024], "lb_bq": [128, 8], "lb_bk": [128, 8],
    "lb_bv": [1, 1024], "lb_bo": [128, 8],
    "l1f_bd1": [128, 4 * 8 * 128], "l1f_bd2": [128, 4 * 32 * 128],
    "l1f_cpT": [16384, 1024],
}


def build_fused_program():
    nc = bass.Bass("TRN2", target_bir_lowering=False)
    x_in = nc.dram_tensor("x_in", [DM, T], F32, kind="ExternalInput").ap()
    x_out = nc.dram_tensor("x_out", [DM, T], F32, kind="ExternalOutput").ap()
    stages = [x_in]
    for i in range(1, 8):
        stages.append(nc.dram_tensor("x_st%d" % i, [DM, T], F32,
                                     kind="Internal").ap())
    stages.append(x_out)
    W = {name: nc.dram_tensor(name, shape, F32, kind="ExternalInput").ap()
         for name, shape in _W_SHAPES.items()}
    xln_hbm = nc.dram_tensor("xln_s", [DM, T], BF16, kind="Internal").ap()
    k_hbm = nc.dram_tensor("k_s", [DM, T], BF16, kind="Internal").ap()
    vb_hbm = nc.dram_tensor("vb_s", [128, 32 * 1040], BF16, kind="Internal").ap()
    cpbf = nc.dram_tensor("cpbf", [16384, 1024], BF16, kind="Internal").ap()
    w1bf = nc.dram_tensor("w1bf", [8 * 32 * 128, 8 * 128], BF16,
                          kind="Internal").ap()
    w2bf = nc.dram_tensor("w2bf", [8 * 8 * 128, 32 * 128], BF16,
                          kind="Internal").ap()
    h0b = nc.dram_tensor("h0b_s", [DM, T], BF16, kind="Internal").ap()
    h0f = nc.dram_tensor("h0f_s", [DM, T], F32, kind="Internal").ap()
    h1f = nc.dram_tensor("h1f_s", [DM, T], F32, kind="Internal").ap()

    C = Core(nc)
    C.begin_scope()
    emit_cast_hbm(C, W["ff_w1l"], w1bf, 8 * 32 * 128, 8 * 128, "w1cast")
    t2 = emit_cast_hbm(C, W["ff_w2l"], w2bf, 8 * 8 * 128, 32 * 128, "w2cast")
    t3 = emit_cast_hbm(C, W["l1f_cpT"], cpbf, 16384, 1024, "cpcast")
    C.end_scope()
    C.em.set_phase()

    res = None
    for i, name in enumerate(SUBLAYERS):
        base = name[:-2] if name.endswith(("_1", "_2")) else name
        C.begin_scope()
        if base == "L0_attn_a":
            res = emit_l0attna(C, stages[i], stages[i + 1], W, xln_hbm)
        elif base == "L0_ffn":
            res = emit_l0ffn(C, stages[i], stages[i + 1], W, w1bf, w2bf,
                             t2, xln_hbm)
        elif base == "L0_attn_b":
            res = emit_l0attnb(C, stages[i], stages[i + 1], W, xln_hbm)
        elif base == "L1_attn_a":
            res = emit_l1attna(C, stages[i], stages[i + 1], W, h0b, h0f, h1f)
        elif base == "L1_ffn":
            res = emit_l1ffn(C, stages[i], stages[i + 1], W, cpbf, t3)
        elif base == "L1_attn_b":
            res = emit_l1attnb(C, stages[i], stages[i + 1], W, xln_hbm,
                               k_hbm, vb_hbm)
        C.end_scope()
        C.em.set_phase()
    C.em.finalize(res)
    return nc


def host_prep_all(inputs):
    W = {}
    for prep in (host_prep_l0attna, host_prep_l0ffn, host_prep_l0attnb,
                 host_prep_l1attna, host_prep_l1ffn, host_prep_l1attnb):
        W.update(prep(inputs))
    return W


def kernel(**inputs):
    x = np.asarray(inputs["L0_attn_a_x"], np.float32)
    xT = [np.ascontiguousarray(x[b].T) for b in range(N_CORES)]
    W = host_prep_all(inputs)
    nc = build_fused_program()
    in_maps = [{"x_in": xT[b], **W} for b in range(N_CORES)]
    res = run_bass_kernel_spmd(nc, in_maps, core_ids=list(range(N_CORES)))
    out = np.stack([np.ascontiguousarray(np.asarray(r["x_out"]).T)
                    for r in res.results])
    return out.astype(np.float32)


def kernel_unfused(**inputs):
    x = np.asarray(inputs["L0_attn_a_x"], np.float32)
    xT = [np.ascontiguousarray(x[b].T) for b in range(N_CORES)]
    progs = {}
    weights = {}
    for name in SUBLAYERS:
        base = name[:-2] if name.endswith(("_1", "_2")) else name
        if base not in progs:
            build, prep = _BUILDERS[base]
            progs[base] = build()
            weights[base] = prep(inputs)
        in_maps = [{"x_in": xT[b], **weights[base]} for b in range(N_CORES)]
        res = run_bass_kernel_spmd(progs[base], in_maps,
                                   core_ids=list(range(N_CORES)))
        xT = [np.asarray(r["x_out"]) for r in res.results]
    out = np.stack([np.ascontiguousarray(xb.T) for xb in xT])
    return out.astype(np.float32)


# version 03
# speedup vs baseline: 1.2659x; 1.2538x over previous
import numpy as np
import concourse.bass as bass
import concourse.mybir as mybir
from concourse.bass_utils import run_bass_kernel_spmd

F32 = mybir.dt.float32
BF16 = mybir.dt.bfloat16
AF = mybir.ActivationFunctionType
ALU = mybir.AluOpType
AX = mybir.AxisListType

T = 4096
DM = 1024
NCH = 8
N_CORES = 8

ENGINES = ("t", "v", "s", "p", "y")


class Em:

    def __init__(self, nc):
        self.nc = nc
        self.q = {e: [] for e in ENGINES}
        self.sem = {e: nc.alloc_semaphore("sem_" + e) for e in ENGINES}
        self.cnt = {e: 0 for e in ENGINES}
        self.hw = {e: {} for e in ENGINES}
        self.bufsem = {}
        self.free_sems = []
        self.phase = []

    def _waits(self, e, after):
        out = []
        for tok in list(after) + self.phase:
            if tok is None:
                continue
            sem, val = tok
            key = id(sem)
            if self.hw[e].get(key, -1) >= val:
                continue
            self.hw[e][key] = val
            out.append((sem, val))
        return out

    def op(self, e, fn, after=()):
        return self.ops(e, [fn], after)

    def ops(self, e, fns, after=()):
        waits = self._waits(e, after)
        self.cnt[e] += 1
        val = self.cnt[e]
        sem = self.sem[e]

        def emit(eng, fns=fns, waits=waits, sem=sem):
            for s, v in waits:
                eng.wait_ge(s, v)
            inst = None
            for fn in fns:
                inst = fn(eng)
            inst.then_inc(sem, 1)

        self.q[e].append(emit)
        return (sem, val)

    def dma(self, e, out, in_, after=()):
        waits = self._waits(e, after)
        name = out.tensor.name
        ent = self.bufsem.get(name)
        if ent is None:
            if self.free_sems:
                sem0, cnt0 = self.free_sems.pop()
                ent = [sem0, cnt0]
            else:
                ent = [self.nc.alloc_semaphore("dsem%d" % len(self.bufsem)), 0]
            self.bufsem[name] = ent
        ent[1] += 16
        val = ent[1]
        sem = ent[0]

        def emit(eng, out=out, in_=in_, waits=waits, sem=sem):
            for s, v in waits:
                eng.wait_ge(s, v)
            eng.dma_start(out=out, in_=in_).then_inc(sem, 16)

        self.q[e].append(emit)
        return (sem, val)

    def set_phase(self):
        toks = [(self.sem[e], self.cnt[e]) for e in ENGINES if self.cnt[e]]
        toks += [(ent[0], ent[1]) for ent in self.bufsem.values()]
        self.phase = toks

    def recycle_bufsems(self, names):
        for n in names:
            ent = self.bufsem.pop(n, None)
            if ent is not None:
                self.free_sems.append((ent[0], ent[1]))

    def finalize(self, final_tokens):
        nc = self.nc
        with nc.Block() as block:
            def run(eng, e):
                for emit in self.q[e]:
                    emit(eng)

            @block.tensor
            def _(eng):
                run(eng, "t")

            @block.vector
            def _(eng):
                run(eng, "v")

            @block.scalar
            def _(eng):
                run(eng, "s")

            @block.gpsimd
            def _(eng):
                run(eng, "p")

            @block.sync
            def _(eng):
                run(eng, "y")
                for s, v in final_tokens:
                    eng.wait_ge(s, v)


class BankView:

    def __init__(self, h, off):
        self.h = h
        self.off = off

    def __getitem__(self, idx):
        rs, cs = idx
        start = (cs.start or 0) + self.off
        stop = (512 if cs.stop is None else cs.stop) + self.off
        return self.h[rs, start:stop]


class Core:

    def __init__(self, nc):
        self.nc = nc
        self.em = Em(nc)
        self.uid = 0
        self.psA = nc.alloc_psum_tensor("pspA", [128, 1024], F32)
        s2 = nc.alloc_psum_tensor("pbank2", [128, 512], F32)
        s3 = nc.alloc_psum_tensor("pbank3", [128, 512], F32)
        s4 = nc.alloc_psum_tensor("pbank4", [128, 512], F32)
        self.psB = nc.alloc_psum_tensor("pspB", [128, 1024], F32)
        s7 = nc.alloc_psum_tensor("pbank7", [128, 512], F32)
        self.pb = [BankView(self.psA, 0), BankView(self.psA, 512),
                   s2, s3, s4,
                   BankView(self.psB, 0), BankView(self.psB, 512), s7]
        self.ones_col_bf = nc.alloc_sbuf_tensor("ones_col_bf", [128, 1], BF16)
        self.ones_row_bf = nc.alloc_sbuf_tensor("ones_row_bf", [1, 128], BF16)
        self.ones_col_f = nc.alloc_sbuf_tensor("ones_col_f", [128, 1], F32)
        self.ones_row_f = nc.alloc_sbuf_tensor("ones_row_f", [1, 128], F32)
        self.eps_col = nc.alloc_sbuf_tensor("eps_col", [128, 1], F32)
        em = self.em
        self.t_const = em.ops("v", [
            lambda v: v.memset(self.ones_col_bf[:], 1.0),
            lambda v: v.memset(self.ones_row_bf[:], 1.0),
            lambda v: v.memset(self.ones_col_f[:], 1.0),
            lambda v: v.memset(self.ones_row_f[:], 1.0),
            lambda v: v.memset(self.eps_col[:], 1e-5),
        ])
        self.ln_scratch()

    def name(self, base):
        self.uid += 1
        return "%s_%d" % (base, self.uid)

    def sb(self, base, shape, dtype=F32):
        nm = self.name(base)
        if getattr(self, "_scope", None) is not None:
            h = self._scope.enter_context(self.nc.sbuf_tensor(nm, shape, dtype))
            self._scope_names.append(nm)
            return h
        return self.nc.alloc_sbuf_tensor(nm, shape, dtype)

    def begin_scope(self):
        import contextlib
        self._scope = contextlib.ExitStack()
        self._scope_names = []

    def end_scope(self):
        names = self._scope_names
        self._scope.close()
        self._scope = None
        self._scope_names = []
        self.em.recycle_bufsems(names)

    def ln_scratch(self):
        if not hasattr(self, "_lns"):
            self._lns = {
                "sq": self.sb("lnx_sq", [128, 512], BF16),
                "m": self.sb("lnx_m", [1, 512], F32),
                "q": self.sb("lnx_q", [1, 512], F32),
                "v": self.sb("lnx_v", [1, 512], F32),
                "r": self.sb("lnx_r", [1, 512], F32),
                "s": self.sb("lnx_s", [1, 512], F32),
                "t": self.sb("lnx_t", [128, 512], F32),
            }
        return self._lns


def emit_ln(C, x_hbm, xb, xln, blk, tb, after=(), f32_hbm=None, f32_bufs=None):
    em = C.em
    t0 = tb * blk
    xb_toks = [em.dma("p", xb[c][:, 0:tb],
                      x_hbm[128 * c:128 * (c + 1), t0:t0 + tb],
                      after=after) for c in range(NCH)]
    lns = C.ln_scratch()
    sq = lns["sq"]
    ps_s = C.pb[6]
    ps_q = C.pb[7]
    mm_toks = []
    for c in range(NCH):
        tsq = em.op("s", lambda s, c=c: s.activation(sq[:, 0:tb], xb[c][:, 0:tb],
                                                     AF.Square),
                    after=[xb_toks[c]] + ([mm_toks[-1]] if c else list(after)))
        mm_toks.append(em.ops("t", [
            lambda t, c=c: t.matmul(ps_s[0:1, 0:tb], C.ones_col_bf[:],
                                    xb[c][:, 0:tb], start=(c == 0),
                                    stop=(c == NCH - 1)),
            lambda t, c=c: t.matmul(ps_q[0:1, 0:tb], C.ones_col_bf[:],
                                    sq[:, 0:tb], start=(c == 0),
                                    stop=(c == NCH - 1)),
        ], after=[xb_toks[c], tsq, C.t_const]))
    m_row = lns["m"][:, 0:tb]
    q_row = lns["q"][:, 0:tb]
    v_row = lns["v"][:, 0:tb]
    r_row = lns["r"][:, 0:tb]
    s_row = lns["s"][:, 0:tb]
    tk = mm_toks[-1]
    tk1 = em.op("v", lambda v: v.tensor_scalar_mul(m_row, ps_s[0:1, 0:tb],
                                                   1.0 / DM), after=[tk])
    tk2 = em.op("v", lambda v: v.tensor_scalar_mul(q_row, ps_q[0:1, 0:tb],
                                                   1.0 / DM), after=[tk])
    tk3 = em.op("v", lambda v: v.tensor_mul(v_row, m_row, m_row),
                after=[tk1])
    tk4 = em.op("v", lambda v: v.tensor_sub(v_row, q_row, v_row),
                after=[tk2, tk3])
    tk5 = em.op("s", lambda s: s.activation(v_row, v_row, AF.Sqrt,
                                            bias=C.eps_col[0:1, :]),
                after=[tk4, C.t_const])
    tk6 = em.op("v", lambda v: v.reciprocal(r_row, v_row), after=[tk5])
    tk7 = em.op("v", lambda v: v.tensor_mul(s_row, m_row, r_row),
                after=[tk6])
    tb1 = em.op("t", lambda t: t.matmul(ps_s[:, 0:tb], C.ones_row_f[:],
                                        r_row, start=True, stop=True),
                after=[tk6])
    tb2 = em.op("t", lambda t: t.matmul(ps_q[:, 0:tb], C.ones_row_f[:],
                                        s_row, start=True, stop=True),
                after=[tk7])
    tmp = lns["t"]
    out_toks = []
    f32_st = [None, None]
    for c in range(NCH):
        ta = em.op("v", lambda v, c=c: v.tensor_mul(tmp[:, 0:tb], xb[c][:, 0:tb],
                                                    ps_s[:, 0:tb]),
                   after=[tb1] + ([out_toks[-1]] if c else []))
        out_toks.append(em.op("v", lambda v, c=c: v.tensor_sub(
            xln[c][:, 0:tb], tmp[:, 0:tb], ps_q[:, 0:tb]), after=[tb2, ta]))
        if f32_hbm is not None:
            fb = f32_bufs[c % 2]
            t32 = em.op("v", lambda v, c=c, fb=fb: v.tensor_sub(
                fb[:, 0:tb], tmp[:, 0:tb], ps_q[:, 0:tb]),
                after=[tb2, ta] + ([f32_st[c % 2]] if f32_st[c % 2] else []))
            f32_st[c % 2] = em.dma("y", f32_hbm[128 * c:128 * (c + 1),
                                                t0:t0 + tb],
                                   fb[:, 0:tb], after=[t32])
            out_toks[-1] = t32
    return out_toks, xb_toks



def host_prep_l1ffn(inputs):
    rm = np.asarray(inputs["L1_ffn_rail_maker"])
    up = np.asarray(inputs["L1_ffn_rail_up_proj"])
    cp = np.asarray(inputs["L1_ffn_w_cproj"])
    bd1 = np.zeros((4, 8, 128, 128), np.float32)
    for h in range(4):
        for r in range(64):
            c, j = divmod(r, 8)
            bd1[h, c, 16 * j:16 * j + 16, 16 * j:16 * j + 16] = rm[h, r]
    bd2 = np.zeros((4, 32, 128, 128), np.float32)
    for h in range(4):
        for cz in range(32):
            r0 = 2 * cz
            base = 32 * (cz % 4)
            bd2[h, cz, base:base + 16, 0:64] = up[h, r0].T
            bd2[h, cz, base + 16:base + 32, 64:128] = up[h, r0 + 1].T
    return {"l1f_bd1": np.ascontiguousarray(
                bd1.transpose(2, 0, 1, 3).reshape(128, 4 * 8 * 128)),
            "l1f_bd2": np.ascontiguousarray(
                bd2.transpose(2, 0, 1, 3).reshape(128, 4 * 32 * 128)),
            "l1f_cpT": np.ascontiguousarray(cp.T)}


def emit_l1ffn(C, x_hbm, xo_hbm, W, cpbf_hbm, cpbf_ready):
    em = C.em
    G = 256
    NG = T // G
    bd1_sb = C.sb("bd1", [128, 4 * 8 * 128], BF16)
    bd2_sb = C.sb("bd2", [128, 4 * 32 * 128], BF16)
    tok_bd1 = em.dma("p", bd1_sb[:], W["l1f_bd1"])
    tok_bd = em.dma("p", bd2_sb[:], W["l1f_bd2"], after=[tok_bd1])

    def bd1_t(h, c):
        i = h * 8 + c
        return bd1_sb[:, i * 128:(i + 1) * 128]

    def bd2_t(h, cz):
        i = h * 32 + cz
        return bd2_sb[:, i * 128:(i + 1) * 128]

    xb = [C.sb("xb", [128, G], BF16) for _ in range(NCH)]
    xf = [C.sb("xf", [128, G], F32) for _ in range(2)]
    xln = [C.sb("xln", [128, G], BF16) for _ in range(NCH)]
    y_sb = [C.sb("yrail", [128, G], BF16) for _ in range(NCH)]
    hbuf = C.sb("hbuf", [128, 128 * G], BF16)
    cp_sb = [C.sb("cpslice", [128, 1024], BF16) for _ in range(4)]
    osum = C.sb("osum", [128, G], F32)

    prev_cproj = cpbf_ready
    prev_resid = [None] * NCH
    prev_res_v = None
    cp_toks = [None, None, None, None]
    mm_hist = [None, None, None, None]
    mm_last = None
    for g in range(NG):
        xln_toks, xb_toks = emit_ln(C, x_hbm, xb, xln, g, G,
                                    after=[tok_bd, prev_cproj, prev_res_v])
        gel_toks = []
        for h in range(4):
            y_toks = []
            for c in range(NCH):
                bank = C.pb[c % 2]
                tm = em.op("t", lambda t, h=h, c=c, bank=bank: t.matmul(
                    bank[:, 0:G], bd1_t(h, c), xln[c][:, 0:G],
                    start=True, stop=True),
                    after=[xln_toks[c], tok_bd] + ([y_toks[-2]] if len(y_toks) >= 2 else []))
                y_toks.append(em.op("v", lambda v, c=c, bank=bank: v.tensor_copy(
                    y_sb[c][:, 0:G], bank[:, 0:G]), after=[tm]))
            for cz in range(32):
                cj = cz // 4
                bank = C.pb[2 + cz % 2]
                tm = em.op("t", lambda t, h=h, cz=cz, cj=cj, bank=bank: t.matmul(
                    bank[:, 0:G], bd2_t(h, cz), y_sb[cj][:, 0:G],
                    start=True, stop=True),
                    after=[y_toks[cj]] + ([gel_toks[-2]] if len(gel_toks) >= 2 else []))
                slot = h * 32 + cz
                gel_toks.append(em.op("s", lambda s, slot=slot, bank=bank: s.activation(
                    hbuf[:, slot * G:(slot + 1) * G], bank[:, 0:G], AF.Gelu),
                    after=[tm, prev_cproj]))
        tok_h = gel_toks[-1]
        for kc in range(128):
            buf = kc % 4
            dep = [tok_bd]
            if cp_toks[buf] is not None:
                dep.append(mm_hist[buf])
            cp_toks[buf] = em.dma("y", cp_sb[buf][:],
                                  cpbf_hbm[kc * 128:(kc + 1) * 128, :],
                                  after=dep + [cpbf_ready])
            fns = []
            for oc in range(NCH):
                fns.append(lambda t, kc=kc, oc=oc, buf=buf: t.matmul(
                    C.pb[oc][:, 0:G], cp_sb[buf][:, oc * 128:(oc + 1) * 128],
                    hbuf[:, kc * G:(kc + 1) * G], start=(kc == 0),
                    stop=(kc == 127)))
            mm_last = em.ops("t", fns,
                             after=[cp_toks[buf], tok_h] +
                             ([prev_resid[-1]] if kc == 0 and prev_resid[-1] else []))
            mm_hist[buf] = mm_last
        res_toks = []
        for oc in range(NCH):
            xi = xf[oc % 2]
            tx = em.dma("y", xi[:, 0:G],
                        x_hbm[128 * oc:128 * (oc + 1), g * G:(g + 1) * G],
                        after=[res_toks[-2]] if len(res_toks) >= 2 else [])
            tv = em.op("v", lambda v, oc=oc, xi=xi: v.tensor_add(
                osum[:, 0:G], xi[:, 0:G], C.pb[oc][:, 0:G]),
                after=[mm_last, tx] + ([res_toks[-1]] if res_toks else []))
            res_toks.append(em.dma("y", xo_hbm[128 * oc:128 * (oc + 1),
                                               g * G:(g + 1) * G],
                                   osum[:, 0:G], after=[tv]))
            prev_res_v = tv
        prev_cproj = mm_last
        prev_resid = res_toks
    return res_toks



def host_prep_l0ffn(inputs):
    w1 = np.asarray(inputs["L0_ffn_w1"])
    w2 = np.asarray(inputs["L0_ffn_w2"])
    b1 = np.asarray(inputs["L0_ffn_b1"])
    b2 = np.asarray(inputs["L0_ffn_b2"])
    rw = np.asarray(inputs["L0_ffn_router_w"])
    rb = np.asarray(inputs["L0_ffn_router_b"])
    lw = np.asarray(inputs["L0_ffn_lang_w"])
    lb = np.asarray(inputs["L0_ffn_lang_b"])
    w1l = w1.reshape(8, 8, 128, 32, 128).transpose(0, 3, 2, 1, 4).reshape(
        8 * 32 * 128, 8 * 128)
    w2l = w2.reshape(8, 32, 128, 8, 128).transpose(0, 3, 2, 1, 4).reshape(
        8 * 8 * 128, 32 * 128)
    b1l = b1.reshape(8, 32, 128).transpose(2, 0, 1).reshape(128, 8 * 32)
    rl = np.concatenate([rw.T, lw.T], axis=1)
    rlb = np.concatenate([rb, lb])[None, :]
    oh = np.zeros((8, 8 * 128), np.float32)
    for e in range(8):
        oh[e, e * 128:(e + 1) * 128] = 1.0
    return {"ff_w1l": np.ascontiguousarray(w1l), "ff_w2l": np.ascontiguousarray(w2l),
            "ff_b1l": np.ascontiguousarray(b1l), "ff_b2": b2.astype(np.float32),
            "ff_rl": np.ascontiguousarray(rl), "ff_rlb": rlb.astype(np.float32),
            "ff_oh": oh}


def emit_l0ffn(C, x_hbm, xo_hbm, W, w1bf, w2bf, wcast_tok, xln_hbm):
    em = C.em
    TBL = 512
    NB = T // TBL
    rl_sb = C.sb("rl", [128, 8 * 12], BF16)
    rlb_sb = C.sb("rlb", [1, 12], BF16)
    b1_sb = C.sb("b1c", [128, 8 * 32], F32)
    b2_sb = C.sb("b2r", [8, 1024], F32)
    oh_sb = C.sb("ohr", [8, 8 * 128], F32)
    g_all = C.sb("gall", [8, T], F32)
    tok_w = [em.dma("p", rl_sb[:, kc * 12:(kc + 1) * 12],
                    W["ff_rl"][kc * 128:(kc + 1) * 128, :]) for kc in range(NCH)]
    tok_w.append(em.dma("p", rlb_sb[:], W["ff_rlb"]))
    tok_w.append(em.dma("y", b1_sb[:], W["ff_b1l"]))
    tok_w.append(em.dma("y", b2_sb[:], W["ff_b2"]))
    tok_w.append(em.dma("y", oh_sb[:], W["ff_oh"]))
    tok_w = tok_w[-1]

    xb = [C.sb("fxb", [128, TBL], BF16) for _ in range(NCH)]
    xln = [C.sb("fxln", [128, TBL], BF16) for _ in range(NCH)]
    ident_f = C.sb("identf", [128, 128], F32)
    iota_mp = C.sb("iotamp", [128, 128], F32)
    t_id0 = em.op("p", lambda g: g.iota(iota_mp[:], [[1, 128]], base=0,
                                        channel_multiplier=-1,
                                        allow_small_or_imprecise_dtypes=True))
    t_id = em.op("v", lambda v: v.tensor_single_scalar(
        ident_f[:], iota_mp[:], 0.0, ALU.is_equal), after=[t_id0])

    rt = C.sb("rt", [128, 12], F32)
    le = C.sb("rle", [128, 4], F32)
    ews = C.sb("rew", [128, 8], F32)
    rq = C.sb("rrq", [128, 8], F32)
    re_ = C.sb("rre", [128, 8], F32)
    rp = C.sb("rrp", [128, 8], F32)
    gt = C.sb("rgt", [128, 8], F32)
    c1 = C.sb("rc1", [128, 1], F32)
    c2 = C.sb("rc2", [128, 1], F32)
    c3 = C.sb("rc3", [128, 1], F32)

    last_route = None
    prev_st = None
    for blk in range(NB):
        xln_toks, _ = emit_ln(C, x_hbm, xb, xln, blk, TBL,
                              after=[tok_w, last_route, prev_st])
        st_toks = [em.dma("y", xln_hbm[128 * c:128 * (c + 1),
                                       blk * TBL:(blk + 1) * TBL],
                          xln[c][:, 0:TBL], after=[xln_toks[c]])
                   for c in range(NCH)]
        for ts_ in range(4):
            ps = C.pb[ts_ % 2]
            fns = []
            for kc in range(NCH):
                fns.append(lambda t, kc=kc, ts_=ts_, ps=ps: t.matmul(
                    ps[0:128, 0:12], xln[kc][:, ts_ * 128:(ts_ + 1) * 128],
                    rl_sb[:, kc * 12:(kc + 1) * 12],
                    start=(kc == 0), stop=False))
            fns.append(lambda t, ts_=ts_, ps=ps: t.matmul(
                ps[0:128, 0:12], C.ones_row_bf[:], rlb_sb[:],
                start=False, stop=True))
            tmm = em.ops("t", fns, after=xln_toks + [tok_w, C.t_const, last_route])
            o = em.op("v", lambda v, ps=ps: v.tensor_copy(rt[:], ps[0:128, 0:12]),
                      after=[tmm])
            o = em.op("v", lambda v: v.tensor_reduce(c1[:], rt[:, 8:12], AX.X,
                                                     ALU.max), after=[o])
            o = em.op("v", lambda v: v.tensor_scalar_mul(c1[:], c1[:], -1.0),
                      after=[o])
            o = em.op("s", lambda s: s.activation(le[:], rt[:, 8:12], AF.Exp,
                                                  bias=c1[:]), after=[o])
            o = em.op("v", lambda v: v.tensor_reduce(c2[:], le[:], AX.X, ALU.add),
                      after=[o])
            o = em.op("v", lambda v: v.reciprocal(c2[:], c2[:]), after=[o])
            o = em.op("v", lambda v: v.tensor_scalar_mul(le[:], le[:], c2[:]),
                      after=[o])
            fns = []
            for e in range(8):
                l = e % 4
                if l < 3:
                    fns.append(lambda v, e=e, l=l: v.tensor_add(
                        ews[:, e:e + 1], le[:, l:l + 1], le[:, l + 1:l + 2]))
                else:
                    fns.append(lambda v, e=e, l=l: v.tensor_copy(
                        ews[:, e:e + 1], le[:, l:l + 1]))
            o = em.ops("v", fns, after=[o])
            o = em.op("v", lambda v: v.scalar_tensor_tensor(
                rq[:], ews[:], 0.1, rt[:, 0:8], ALU.mult, ALU.add), after=[o])
            o = em.op("v", lambda v: v.tensor_reduce(c1[:], rq[:], AX.X, ALU.max),
                      after=[o])
            o = em.op("v", lambda v: v.tensor_scalar_mul(c1[:], c1[:], -1.0),
                      after=[o])
            o = em.op("s", lambda s: s.activation(re_[:], rq[:], AF.Exp,
                                                  bias=c1[:]), after=[o])
            o = em.op("v", lambda v: v.tensor_reduce(c2[:], re_[:], AX.X, ALU.add),
                      after=[o])
            o = em.op("v", lambda v: v.reciprocal(c2[:], c2[:]), after=[o])
            o = em.op("v", lambda v: v.tensor_scalar_mul(rp[:], re_[:], c2[:]),
                      after=[o])
            o = em.op("v", lambda v: v.tensor_reduce(c1[:], rp[:], AX.X, ALU.max),
                      after=[o])
            o = em.op("v", lambda v: v.tensor_single_scalar(gt[:], rp[:], c1[:],
                                                            ALU.is_equal),
                      after=[o])
            o = em.op("v", lambda v: v.scalar_tensor_tensor(
                gt[:], gt[:], -2.0, rp[:], ALU.mult, ALU.add), after=[o])
            o = em.op("v", lambda v: v.tensor_reduce(c2[:], gt[:], AX.X, ALU.max),
                      after=[o])
            o = em.op("v", lambda v: v.tensor_add(c3[:], c1[:], c2[:]), after=[o])
            o = em.op("v", lambda v: v.reciprocal(c3[:], c3[:]), after=[o])
            o = em.op("v", lambda v: v.tensor_single_scalar(gt[:], rp[:], c2[:],
                                                            ALU.is_ge), after=[o])
            o = em.op("v", lambda v: v.tensor_mul(gt[:], gt[:], rp[:]), after=[o])
            o = em.op("v", lambda v: v.tensor_scalar_mul(gt[:], gt[:], c3[:]),
                      after=[o])
            pst = C.pb[2 + ts_ % 2]
            o2 = em.op("t", lambda t, pst=pst: t.transpose(
                pst[0:8, 0:128], gt[:], ident_f[:]), after=[o, t_id])
            last_route = em.op("v", lambda v, blk=blk, ts_=ts_, pst=pst:
                               v.tensor_copy(g_all[0:8, blk * TBL + ts_ * 128:
                                                   blk * TBL + (ts_ + 1) * 128],
                                             pst[0:8, 0:128]), after=[o2])
        prev_st = st_toks[-1]
    route_done = last_route
    xlnst_done = st_toks[-1]

    w1sl = [C.sb("w1sl", [128, 8 * 128], BF16) for _ in range(2)]
    w2sl = [C.sb("w2sl", [128, 32 * 128], BF16) for _ in range(2)]
    hbuf = C.sb("fhbuf", [128, 32 * TBL], BF16)
    gbe = C.sb("gbe", [128, TBL], BF16)
    acc = [C.sb("facc", [128, TBL], F32) for _ in range(NCH)]
    tmpm = C.sb("ftmp", [128, TBL], F32)
    xfr = [C.sb("fxf", [128, TBL], F32) for _ in range(2)]
    osum = C.sb("fosum", [128, TBL], F32)

    w1_toks = [None, None]
    w2_toks = [None, None]
    w1_used = [None, None]
    w2_used = [None, None]
    xl_toks = [None] * NCH
    prev_blk_acc_read = None
    prev_store = None
    prev_gelu = None
    prev_accv = None
    for blk in range(NB):
        xl_toks = [em.dma("p", xln[c][:, 0:TBL],
                          xln_hbm[128 * c:128 * (c + 1),
                                  blk * TBL:(blk + 1) * TBL],
                          after=[xlnst_done, route_done, prev_gelu, prev_accv])
                   for c in range(NCH)]
        for dc in range(NCH):
            ps = C.pb[4 + dc % 2]
            tm = em.op("t", lambda t, dc=dc, blk=blk, ps=ps: t.matmul(
                ps[:, 0:TBL], b2_sb[0:8, dc * 128:(dc + 1) * 128],
                g_all[0:8, blk * TBL:(blk + 1) * TBL], start=True, stop=True),
                after=[route_done, tok_w, prev_accv])
            prev_accv = em.op("v", lambda v, dc=dc, ps=ps: v.tensor_copy(
                acc[dc][:, 0:TBL], ps[:, 0:TBL]), after=[tm, prev_store])
        for e in range(8):
            psg = C.pb[6]
            tg = em.op("t", lambda t, e=e, blk=blk: t.matmul(
                psg[:, 0:TBL], oh_sb[0:8, e * 128:(e + 1) * 128],
                g_all[0:8, blk * TBL:(blk + 1) * TBL], start=True, stop=True),
                after=[route_done, tok_w, prev_accv])
            tgb = em.op("v", lambda v: v.tensor_copy(gbe[:, 0:TBL],
                                                     psg[:, 0:TBL]), after=[tg])
            gelu_hist = [None, None]
            for hc in range(32):
                buf = hc % 2
                dep = [wcast_tok]
                if w1_used[buf] is not None:
                    dep.append(w1_used[buf])
                w1_toks[buf] = em.dma("y", w1sl[buf][:],
                                      w1bf[(e * 32 + hc) * 128:
                                           (e * 32 + hc + 1) * 128, :],
                                      after=dep)
                ps = C.pb[hc % 2]
                fns = []
                for kc in range(NCH):
                    fns.append(lambda t, kc=kc, buf=buf, ps=ps: t.matmul(
                        ps[:, 0:TBL], w1sl[buf][:, kc * 128:(kc + 1) * 128],
                        xln[kc][:, 0:TBL], start=(kc == 0), stop=(kc == NCH - 1)))
                tmm = em.ops("t", fns, after=[w1_toks[buf]] + xl_toks +
                             [gelu_hist[buf] if gelu_hist[buf] is not None
                              else prev_gelu])
                w1_used[buf] = tmm
                prev_gelu = em.op("s", lambda s, hc=hc, e=e, ps=ps: s.activation(
                    hbuf[:, hc * TBL:(hc + 1) * TBL], ps[:, 0:TBL], AF.Gelu,
                    bias=b1_sb[:, e * 32 + hc:e * 32 + hc + 1]),
                    after=[tmm, tok_w])
                gelu_hist[buf] = prev_gelu
            for dc in range(NCH):
                buf = dc % 2
                dep = [wcast_tok]
                if w2_used[buf] is not None:
                    dep.append(w2_used[buf])
                w2_toks[buf] = em.dma("y", w2sl[buf][:],
                                      w2bf[(e * 8 + dc) * 128:
                                           (e * 8 + dc + 1) * 128, :],
                                      after=dep)
                ps = C.pb[2 + dc % 2]
                fns = []
                for kc in range(32):
                    fns.append(lambda t, kc=kc, buf=buf, ps=ps: t.matmul(
                        ps[:, 0:TBL], w2sl[buf][:, kc * 128:(kc + 1) * 128],
                        hbuf[:, kc * TBL:(kc + 1) * TBL],
                        start=(kc == 0), stop=(kc == 31)))
                tmm = em.ops("t", fns, after=[w2_toks[buf], prev_gelu,
                                              prev_accv])
                w2_used[buf] = tmm
                tv1 = em.op("v", lambda v, ps=ps: v.tensor_mul(
                    tmpm[:, 0:TBL], ps[:, 0:TBL], gbe[:, 0:TBL]),
                    after=[tmm, tgb])
                prev_accv = em.op("v", lambda v, dc=dc: v.tensor_add(
                    acc[dc][:, 0:TBL], acc[dc][:, 0:TBL], tmpm[:, 0:TBL]),
                    after=[tv1])
        prev_blk_acc_read = prev_accv
        res_toks = []
        for dc in range(NCH):
            xi = xfr[dc % 2]
            tx = em.dma("y", xi[:, 0:TBL],
                        x_hbm[128 * dc:128 * (dc + 1),
                              blk * TBL:(blk + 1) * TBL])
            tv = em.op("v", lambda v, dc=dc, xi=xi: v.tensor_add(
                osum[:, 0:TBL], xi[:, 0:TBL], acc[dc][:, 0:TBL]),
                after=[prev_accv, tx] + ([res_toks[-1]] if res_toks else []))
            res_toks.append(em.dma("y", xo_hbm[128 * dc:128 * (dc + 1),
                                               blk * TBL:(blk + 1) * TBL],
                                   osum[:, 0:TBL], after=[tv]))
            prev_store = tv
        prev_accv = prev_store
    return res_toks



ABQ_H0 = [0, 1, 4, 5, 8, 9, 12, 13]
ABQ_H1 = [2, 3, 6, 7, 10, 11, 14, 15]


def host_prep_l0attnb(inputs):
    wq = np.asarray(inputs["L0_attn_b_Wq"])
    wk = np.asarray(inputs["L0_attn_b_Wk"])
    wv = np.asarray(inputs["L0_attn_b_Wv"])
    wo = np.asarray(inputs["L0_attn_b_Wo"])
    bo = np.asarray(inputs["L0_attn_b_bo"])
    tq = np.arange(T, dtype=np.float32)
    hi_q = np.floor(tq / 32.0)
    lo_q = tq - 32.0 * hi_q
    qaux = np.zeros((16, 2, 4, T), np.float32)
    for h in range(16):
        s = 2.0 ** (-(h + 4))
        qaux[h, 0, 0] = -32.0 * s * hi_q
        qaux[h, 0, 1] = 32.0 * s
        qaux[h, 0, 2] = s
        qaux[h, 0, 3] = -s * lo_q
        qaux[h, 1, 0] = 32.0 * s * hi_q
        qaux[h, 1, 1] = -32.0 * s
        qaux[h, 1, 2] = -s
        qaux[h, 1, 3] = s * lo_q
    perm = np.concatenate([np.r_[ABQ_H0[c] * 64:(ABQ_H0[c] + 1) * 64,
                                 ABQ_H1[c] * 64:(ABQ_H1[c] + 1) * 64]
                           for c in range(8)])
    wqp = wq.T[:, perm]
    wop = wo.T[perm, :]
    wql = wqp.reshape(8, 128, 8, 128).transpose(2, 1, 0, 3).reshape(8 * 128, 1024)
    wol = wop.reshape(8, 128, 8, 128).transpose(2, 1, 0, 3).reshape(8 * 128, 1024)
    return {"ab_wql": np.ascontiguousarray(wql),
            "ab_wkT": np.ascontiguousarray(wk.T),
            "ab_wvT": np.ascontiguousarray(wv.T),
            "ab_wol": np.ascontiguousarray(wol),
            "ab_bo": np.ascontiguousarray(bo.reshape(8, 128).T),
            "ab_qaux": qaux.reshape(128, T)}


def emit_l0attnb(C, x_hbm, xo_hbm, W, xln_hbm, dbg=None):
    em = C.em
    TBL = 512
    NB = T // TBL
    nc = C.nc

    def dump(name, src, after):
        if dbg is not None and name in dbg:
            return em.dma("p", dbg[name], src, after=after)
        return None
    wk_sb = C.sb("abwk", [128, NCH * 512], BF16)
    wv_sb = C.sb("abwv", [128, NCH * 512], BF16)
    tok_w = None
    for kc in range(NCH):
        em.dma("p", wk_sb[:, kc * 512:(kc + 1) * 512],
               W["ab_wkT"][kc * 128:(kc + 1) * 128, :])
        tok_w = em.dma("p", wv_sb[:, kc * 512:(kc + 1) * 512],
                       W["ab_wvT"][kc * 128:(kc + 1) * 128, :])
    bo_sb = C.sb("abbo", [128, 8], F32)
    tok_w = em.dma("y", bo_sb[:], W["ab_bo"], after=[tok_w])
    wq2 = [C.sb("abwq2", [128, 1024], BF16) for _ in range(2)]
    wo2 = [C.sb("abwo2", [128, 1024], BF16) for _ in range(2)]

    aux = C.sb("abaux", [68, T], BF16)
    t2 = em.op("p", lambda g: g.iota(aux[0:1, :], [[1, 128], [0, 32]], base=0,
                                     channel_multiplier=0,
                                     allow_small_or_imprecise_dtypes=True))
    t2a = em.dma("p", aux[1:2, :], aux[0:1, :], after=[t2])
    t2b = em.dma("p", aux[65:66, :], aux[0:1, :], after=[t2])
    t3 = em.op("p", lambda g: g.iota(aux[0:1, :], [[0, 128], [1, 32]], base=0,
                                     channel_multiplier=0,
                                     allow_small_or_imprecise_dtypes=True),
               after=[t2a, t2b])
    t3a = em.dma("p", aux[2:3, :], aux[0:1, :], after=[t3])
    t3b = em.dma("p", aux[66:67, :], aux[0:1, :], after=[t3])
    t1 = em.op("v", lambda v: v.memset(aux[0:1, :], 1.0), after=[t3a, t3b])
    em.dma("p", aux[64:65, :], aux[0:1, :], after=[t1])
    em.dma("p", aux[3:4, :], aux[0:1, :], after=[t1])
    tok_aux = em.dma("p", aux[67:68, :], aux[0:1, :], after=[t1])

    xb = [C.sb("bxb", [128, TBL], BF16) for _ in range(NCH)]
    xln = [C.sb("bxln", [128, TBL], BF16) for _ in range(NCH)]
    kbuf = [C.sb("bk", [128, T], BF16) for _ in range(4)]
    vbar = C.sb("bvbar", [128, 32 * 520], BF16)
    qbuf = [C.sb("bq", [128, TBL], BF16) for _ in range(NCH)]
    aout = [C.sb("bao", [128, TBL], BF16) for _ in range(NCH)]
    pbuf = [C.sb("bp", [128, 2 * TBL], BF16) for _ in range(2)]
    qa_l = C.sb("bqal", [68, T], BF16)
    qa_u = C.sb("bqau", [68, T], BF16)
    sqs = C.sb("bsq", [128, TBL], BF16)
    kB = C.sb("bkB", [1, 8], F32)
    ksr = C.sb("bksr", [1, 16], F32)
    qB = C.sb("bqB", [1, 16], F32)
    bBr = C.sb("bbBr", [1, 16], F32)
    r11 = C.sb("br11", [1, TBL], F32)
    rb = C.sb("brb", [64, TBL], F32)
    bias_col = C.sb("bbc", [128, 1], F32)
    tmpf = C.sb("btmp", [128, TBL], F32)
    xfr = [C.sb("bxf", [128, TBL], F32) for _ in range(2)]
    osum = C.sb("bosum", [128, TBL], F32)

    tok_vones = em.op("v", lambda v: v.memset(vbar[:, 64::65], 1.0))
    tok_kinit = em.op("v", lambda v: v.memset(kB[:], -3e38), after=[])

    kn_last = tok_kinit
    st_last = None
    va_last = None
    prev_ph = []
    for blk in range(NB):
        xln_toks, _ = emit_ln(C, x_hbm, xb, xln, blk, TBL,
                              after=[tok_w, st_last, kn_last, va_last])
        st_toks = [em.dma("y", xln_hbm[128 * c:128 * (c + 1),
                                       blk * TBL:(blk + 1) * TBL],
                          xln[c][:, 0:TBL], after=[xln_toks[c]])
                   for c in range(NCH)]
        st_last = st_toks[-1]
        for ck in range(4):
            bank = C.pb[ck % 2]
            fns = [lambda t, kc=kc, ck=ck, bank=bank: t.matmul(
                bank[:, 0:TBL], wk_sb[:, kc * 512 + ck * 128:kc * 512 + (ck + 1) * 128],
                xln[kc][:, 0:TBL], start=(kc == 0), stop=(kc == NCH - 1))
                for kc in range(NCH)]
            tmm = em.ops("t", fns, after=xln_toks + [tok_w] + prev_ph)
            tev = em.op("v", lambda v, ck=ck, blk=blk, bank=bank: v.tensor_copy(
                kbuf[ck][:, blk * TBL:(blk + 1) * TBL], bank[:, 0:TBL]),
                after=[tmm])
            tsq = em.op("s", lambda s, ck=ck, blk=blk: s.activation(
                sqs[:, 0:TBL], kbuf[ck][:, blk * TBL:(blk + 1) * TBL],
                AF.Square), after=[tev])
            for half in range(2):
                kvh = 2 * ck + half
                bank2 = C.pb[4]
                tm2 = em.op("t", lambda t, half=half, bank2=bank2: t.matmul(
                    bank2[0:1, 0:TBL], C.ones_col_bf[64 * half:64 * half + 64, :],
                    sqs[64 * half:64 * half + 64, 0:TBL], start=True, stop=True),
                    after=[tsq, kn_last])
                tr = em.op("v", lambda v, bank2=bank2: v.tensor_reduce(
                    r11[:, 0:1], bank2[0:1, 0:TBL], AX.X, ALU.max), after=[tm2])
                kn_last = em.op("v", lambda v, kvh=kvh: v.tensor_max(
                    kB[:, kvh:kvh + 1], kB[:, kvh:kvh + 1], r11[:, 0:1]),
                    after=[tr])
            prev_ph = [tev]
        for ts_ in range(4):
            bank = C.pb[2 + ts_ % 2]
            fns = [lambda t, kc=kc, ts_=ts_, bank=bank: t.matmul(
                bank[:, 0:512], xln[kc][:, ts_ * 128:(ts_ + 1) * 128],
                wv_sb[:, kc * 512:(kc + 1) * 512],
                start=(kc == 0), stop=(kc == NCH - 1)) for kc in range(NCH)]
            tmm = em.ops("t", fns, after=xln_toks + [tok_w] + prev_ph)
            j = blk * 4 + ts_
            fns2 = [lambda v, kvh=kvh, j=j, bank=bank: v.tensor_copy(
                vbar[:, j * 520 + kvh * 65:j * 520 + kvh * 65 + 64],
                bank[:, kvh * 64:(kvh + 1) * 64]) for kvh in range(8)]
            va_last = em.ops("v", fns2, after=[tmm, tok_vones])
            prev_ph = [va_last]
    phaseA = [kn_last, va_last, st_last]
    o = em.op("s", lambda s: s.activation(kB[:], kB[:], AF.Sqrt),
              after=phaseA)
    o1 = em.op("v", lambda v: v.tensor_copy(ksr[:, 0::2], kB[:]), after=[o])
    tok_ksr = em.op("v", lambda v: v.tensor_copy(ksr[:, 1::2], kB[:]),
                    after=[o1])
    dump("ksr", ksr[:], [tok_ksr])
    dump("kb0", kbuf[0][:, 0:512], [phaseA[0]])

    pb_hist = {}
    pbuf_hist = [None, None]
    qa_hist = [None, None]
    qa_used = None
    wq_toks = [None, None]
    wq_used = [None, None]
    wo_toks = [None, None]
    wo_used = [None, None]
    aout_hist = None
    qb_hist = None
    prev_res = None
    for blk in range(NB):
        tq0 = blk * TBL
        xl_toks = [em.dma("p", xln[c][:, 0:TBL],
                          xln_hbm[128 * c:128 * (c + 1), tq0:tq0 + TBL],
                          after=[st_last, prev_res, phaseA[1]] +
                          ([qb_hist] if qb_hist else []))
                   for c in range(NCH)]
        q_toks = []
        for ch in range(NCH):
            buf = ch % 2
            wdep = [wq_used[buf]] if wq_used[buf] else []
            wq_toks[buf] = em.dma("p", wq2[buf][:],
                                  W["ab_wql"][ch * 128:(ch + 1) * 128, :],
                                  after=wdep)
            bank = C.pb[5 + ch % 2]
            fns = [lambda t, kc=kc, buf=buf, bank=bank: t.matmul(
                bank[:, 0:TBL], wq2[buf][:, kc * 128:(kc + 1) * 128],
                xln[kc][:, 0:TBL], start=(kc == 0), stop=(kc == NCH - 1))
                for kc in range(NCH)]
            dep = xl_toks + [wq_toks[buf], pb_hist.get(5 + ch % 2)]
            tmm = em.ops("t", fns, after=dep)
            wq_used[buf] = tmm
            tev = em.op("s", lambda s, ch=ch, bank=bank: s.activation(
                qbuf[ch][:, 0:TBL], bank[:, 0:TBL], AF.Copy, scale=0.125),
                after=[tmm] + ([qb_hist] if qb_hist else []))
            pb_hist[5 + ch % 2] = tev
            q_toks.append(tev)
            tsq = em.op("s", lambda s, ch=ch: s.activation(
                sqs[:, 0:TBL], qbuf[ch][:, 0:TBL], AF.Square), after=[tev])
            for half in range(2):
                h = ABQ_H0[ch] if half == 0 else ABQ_H1[ch]
                bank2 = C.pb[4]
                tm2 = em.op("t", lambda t, half=half, bank2=bank2: t.matmul(
                    bank2[0:1, 0:TBL], C.ones_col_bf[64 * half:64 * half + 64, :],
                    sqs[64 * half:64 * half + 64, 0:TBL], start=True, stop=True),
                    after=[tsq, pb_hist.get(4)])
                tr = em.op("v", lambda v, bank2=bank2: v.tensor_reduce(
                    r11[:, 0:1], bank2[0:1, 0:TBL], AX.X, ALU.max), after=[tm2])
                pb_hist[4] = tr
                q_toks.append(em.op("v", lambda v, h=h: v.tensor_copy(
                    qB[:, h:h + 1], r11[:, 0:1]), after=[tr]))
        o = em.op("s", lambda s: s.activation(qB[:], qB[:], AF.Sqrt),
                  after=q_toks)
        o = em.op("v", lambda v: v.tensor_mul(bBr[:], qB[:], ksr[:]),
                  after=[o, tok_ksr])
        tok_bB = em.op("v", lambda v: v.tensor_scalar_mul(bBr[:], bBr[:], -1.0),
                       after=[o])
        if blk == 0:
            dump("bBr", bBr[:], [tok_bB])
            dump("qb0", qbuf[0][:, 0:TBL], [tok_bB])
        for h in range(16):
            kvh = h // 2
            ck, kvp = divmod(kvh, 2)
            kb64 = 64 * kvp
            if kvp == 0:
                ch = ABQ_H0.index(h)
            else:
                ch = ABQ_H1.index(h)
            qb64 = kb64
            for sgn, qa in ((0, qa_l), (1, qa_u)):
                dep = [qa_used] if qa_used is not None else []
                em.dma("p", qa[0:4, :],
                       W["ab_qaux"][(h * 2 + sgn) * 4:(h * 2 + sgn) * 4 + 4, :],
                       after=dep)
                qa_hist[sgn] = em.dma("p", qa[64:68, :],
                                      W["ab_qaux"][(h * 2 + sgn) * 4:
                                                   (h * 2 + sgn) * 4 + 4, :])
            tb = em.op("t", lambda t, h=h: t.matmul(
                C.pb[4][:, 0:1], C.ones_row_f[:], bBr[0:1, h:h + 1],
                start=True, stop=True), after=[tok_bB, pb_hist.get(4)])
            tbc = em.op("v", lambda v: v.tensor_copy(bias_col[:], C.pb[4][:, 0:1]),
                        after=[tb])
            pb_hist[4] = tbc
            tiles = []
            for jc in range(0, blk * 4):
                tiles.append((jc, 0, False))
            for jc in range(blk * 4, blk * 4 + 4):
                tiles.append((jc, 0, True))
                tiles.append((jc, 1, True))
            for jc in range(blk * 4 + 4, 32):
                tiles.append((jc, 1, False))
            obank = C.pb[2 + h % 2]
            first_dep = pb_hist.get(2 + h % 2)
            nt = len(tiles)
            prs = [tiles[i:i + 2] for i in range(0, nt, 2)]
            auxb = 64 if kb64 == 0 else 0
            for pi, pr in enumerate(prs):
                w = pi % 2
                wide = C.psA if w == 0 else C.psB
                bks = (0, 1) if w == 0 else (5, 6)
                fns = []
                deps = [phaseA[0], phaseA[1], tok_aux, q_toks[-1],
                        pb_hist.get(bks[0]), pb_hist.get(bks[1])]
                for sub, (jc, sgn, masked) in enumerate(pr):
                    qa = qa_l if sgn == 0 else qa_u
                    o0 = sub * TBL
                    fns.append(lambda t, jc=jc, o0=o0, ck=ck, kb64=kb64,
                               ch=ch, qb64=qb64, wide=wide: t.matmul(
                                   wide[:, o0:o0 + TBL],
                                   kbuf[ck][kb64:kb64 + 64,
                                            jc * 128:(jc + 1) * 128],
                                   qbuf[ch][qb64:qb64 + 64, 0:TBL],
                                   start=True, stop=False))
                    fns.append(lambda t, jc=jc, o0=o0, qa=qa, tq0=tq0,
                               wide=wide: t.matmul(
                                   wide[:, o0:o0 + TBL],
                                   aux[auxb:auxb + 4, jc * 128:(jc + 1) * 128],
                                   qa[auxb:auxb + 4, tq0:tq0 + TBL],
                                   start=False, stop=True))
                    deps.append(qa_hist[sgn])
                tmm = em.ops("t", fns, after=deps)
                wd = len(pr) * TBL
                texp = em.op("s", lambda s, w=w, wd=wd, wide=wide: s.activation(
                    pbuf[w][:, 0:wd], wide[:, 0:wd], AF.Exp, bias=bias_col[:]),
                    after=[tmm, tbc] +
                    ([pbuf_hist[w]] if pbuf_hist[w] else []))
                pb_hist[bks[0]] = texp
                pb_hist[bks[1]] = texp
                tlast = texp
                for sub, (jc, sgn, masked) in enumerate(pr):
                    if not masked:
                        continue
                    if sgn == 0:
                        base, mult, step = tq0 - jc * 128, -1, 1
                    else:
                        base, mult, step = jc * 128 - tq0 - 1, 1, -1
                    o0 = sub * TBL
                    tlast = em.op("p", lambda g, w=w, o0=o0, base=base,
                                  mult=mult, step=step:
                                  g.affine_select(pbuf[w][:, o0:o0 + TBL],
                                                  pbuf[w][:, o0:o0 + TBL],
                                                  [[step, TBL]], ALU.is_ge, 0.0,
                                                  base=base,
                                                  channel_multiplier=mult),
                                  after=[tlast])
                fns = []
                for sub, (jc, sgn, masked) in enumerate(pr):
                    gi = 2 * pi + sub
                    fns.append(lambda t, jc=jc, sub=sub, gi=gi, w=w,
                               obank=obank, kvh=kvh, nt=nt: t.matmul(
                                   obank[0:65, 0:TBL],
                                   vbar[:, jc * 520 + kvh * 65:
                                        jc * 520 + kvh * 65 + 65],
                                   pbuf[w][:, sub * TBL:(sub + 1) * TBL],
                                   start=(gi == 0), stop=(gi == nt - 1)))
                tv = em.ops("t", fns, after=[tlast] +
                            ([first_dep] if pi == 0 else []))
                pbuf_hist[w] = tv
                qa_used = tv
            tr = em.op("v", lambda v, obank=obank: v.reciprocal(
                r11[:, 0:TBL], obank[64:65, 0:TBL]), after=[tv])
            if blk == 0 and h == 0:
                dump("rec0", r11[:, 0:TBL], [tr])
            tb2 = em.op("t", lambda t: t.matmul(
                C.pb[4][0:64, 0:TBL], C.ones_row_f[:, 0:64], r11[:, 0:TBL],
                start=True, stop=True), after=[tr, pb_hist.get(4)])
            tb3 = em.op("v", lambda v: v.tensor_copy(rb[:, 0:TBL],
                                                     C.pb[4][0:64, 0:TBL]),
                        after=[tb2])
            pb_hist[4] = tb3
            tao = em.op("v", lambda v, ch=ch, qb64=qb64, obank=obank:
                        v.tensor_mul(aout[ch][qb64:qb64 + 64, 0:TBL],
                                     obank[0:64, 0:TBL], rb[:, 0:TBL]),
                        after=[tb3, aout_hist])
            pb_hist[2 + h % 2] = tao
            if blk == 0 and h == 0:
                dump("ao0", aout[ch][:, 0:TBL], [tao])
            if blk == 0 and h == 15 and dbg is not None and "aoall" in dbg:
                for cc2 in range(NCH):
                    em.dma("p", dbg["aoall"][128 * cc2:128 * (cc2 + 1), :],
                           aout[cc2][:, 0:TBL], after=[tao])
        aout_last = tao
        res_toks = []
        for oc in range(NCH):
            buf = oc % 2
            wdep = [wo_used[buf]] if wo_used[buf] else []
            wo_toks[buf] = em.dma("p", wo2[buf][:],
                                  W["ab_wol"][oc * 128:(oc + 1) * 128, :],
                                  after=wdep)
            bank = C.pb[5 + oc % 2]
            fns = [lambda t, kc=kc, buf=buf, bank=bank: t.matmul(
                bank[:, 0:TBL], wo2[buf][:, kc * 128:(kc + 1) * 128],
                aout[kc][:, 0:TBL], start=(kc == 0), stop=(kc == NCH - 1))
                for kc in range(NCH)]
            tmm = em.ops("t", fns, after=[aout_last, wo_toks[buf],
                                          pb_hist.get(5 + oc % 2)])
            wo_used[buf] = tmm
            qb_hist = tmm
            xi = xfr[oc % 2]
            tx = em.dma("y", xi[:, 0:TBL],
                        x_hbm[128 * oc:128 * (oc + 1), tq0:tq0 + TBL])
            tv2 = em.op("v", lambda v, oc=oc, xi=xi, bank=bank:
                        v.scalar_tensor_tensor(osum[:, 0:TBL], bank[:, 0:TBL],
                                               bo_sb[:, oc:oc + 1],
                                               xi[:, 0:TBL], ALU.add, ALU.add),
                        after=[tmm, tx] + ([res_toks[-1]] if res_toks else []))
            pb_hist[5 + oc % 2] = tv2
            res_toks.append(em.dma("y", xo_hbm[128 * oc:128 * (oc + 1),
                                               tq0:tq0 + TBL],
                                   osum[:, 0:TBL], after=[tv2]))
        aout_hist = qb_hist
        prev_res = res_toks[-1]
    return res_toks



LQ_H0 = [0, 1, 2, 3, 8, 9, 10, 11]
LQ_H1 = [4, 5, 6, 7, 12, 13, 14, 15]


def _rope_tiles(base):
    hd = 64
    inv = 1.0 / base ** (np.arange(0, hd, 2, dtype=np.float32) / hd)
    tt = np.arange(T, dtype=np.float32)
    fr = tt[None, :] * inv[:, None]
    cos = np.cos(fr)
    sin = np.sin(fr)
    cosr = np.repeat(cos, 2, axis=0)
    sinr = np.empty((64, T), np.float32)
    sinr[0::2] = -sin
    sinr[1::2] = sin
    return (np.ascontiguousarray(np.tile(cosr, (2, 1))),
            np.ascontiguousarray(np.tile(sinr, (2, 1))))


def host_prep_l1attna(inputs):
    n1 = np.asarray(inputs["L1_attn_a_norm1_w"])
    n2 = np.asarray(inputs["L1_attn_a_norm2_w"])
    gwq = np.asarray(inputs["L1_attn_a_g_wq"])
    gkd = np.asarray(inputs["L1_attn_a_g_kv_down"])
    gku = np.asarray(inputs["L1_attn_a_g_k_up"])
    gvu = np.asarray(inputs["L1_attn_a_g_v_up"])
    gwo = np.asarray(inputs["L1_attn_a_g_wo"])
    lwq = np.asarray(inputs["L1_attn_a_l_wq"])
    lwk = np.asarray(inputs["L1_attn_a_l_wk"])
    lwv = np.asarray(inputs["L1_attn_a_l_wv"])
    lwo = np.asarray(inputs["L1_attn_a_l_wo"])
    perm = np.concatenate([np.r_[LQ_H0[c] * 64:(LQ_H0[c] + 1) * 64,
                                 LQ_H1[c] * 64:(LQ_H1[c] + 1) * 64]
                           for c in range(8)])

    def stream_layout(w):
        return np.ascontiguousarray(
            w.reshape(8, 128, 8, 128).transpose(2, 1, 0, 3).reshape(1024, 1024))

    cos_g, sin_g = _rope_tiles(1000000.0)
    cos_l, sin_l = _rope_tiles(10000.0)
    return {"la_gwql": stream_layout(gwq[:, perm]),
            "la_gwol": stream_layout(gwo[perm, :]),
            "la_lwql": stream_layout(lwq[:, perm]),
            "la_lwol": stream_layout(lwo[perm, :]),
            "la_gkd": np.ascontiguousarray(gkd),
            "la_gku": np.ascontiguousarray(gku),
            "la_gvu": np.ascontiguousarray(gvu),
            "la_lwk": np.ascontiguousarray(lwk),
            "la_lwv": np.ascontiguousarray(lwv),
            "la_n1": np.ascontiguousarray(n1.reshape(8, 128).T),
            "la_n2": np.ascontiguousarray(n2.reshape(8, 128).T),
            "la_cosg": cos_g, "la_sing": sin_g,
            "la_cosl": cos_l, "la_sinl": sin_l}


def emit_l1attna(C, x_hbm, xo_hbm, W, h0b_hbm, h0f_hbm, h1f_hbm, dbg=None):
    em = C.em
    nc = C.nc

    def dump(name, src, after):
        if dbg is not None and name in dbg:
            return em.dma("p", dbg[name], src, after=after)
        return None
    TBL = 512
    NB = T // TBL
    eps6 = C.sb("laeps6", [128, 1], F32)
    tok_e6 = em.op("v", lambda v: v.memset(eps6[:], 1e-6))
    bias_col = C.sb("labc", [128, 1], F32)
    tok_bc = em.op("v", lambda v: v.memset(bias_col[:], -0.13))

    xb = [C.sb("laxb", [128, TBL], BF16) for _ in range(NCH)]
    xln = [C.sb("laxln", [128, TBL], BF16) for _ in range(NCH)]
    xf32 = [C.sb("laxf3", [128, TBL], F32) for _ in range(2)]
    nb_ = [C.sb("lanb", [128, TBL], BF16) for _ in range(NCH)]
    lat = [C.sb("lalat", [128, TBL], BF16) for _ in range(2)]
    kg = [C.sb("lakg", [128, T], BF16) for _ in range(2)]
    vbar = C.sb("lavb", [128, 32 * 260], BF16)
    qg = [C.sb("laqg", [128, TBL], BF16) for _ in range(NCH)]
    aout = [C.sb("laao", [128, TBL], BF16) for _ in range(NCH)]
    pbuf = [C.sb("lapb", [128, 2 * TBL], BF16) for _ in range(2)]
    wstr = [C.sb("law", [128, 1024], BF16) for _ in range(2)]
    cosb = C.sb("lacos", [128, T], F32)
    sinb = C.sb("lasin", [128, T], F32)
    qr = C.sb("laqr", [128, TBL], F32)
    qs = C.sb("laqs", [128, TBL], F32)
    ta_ = C.sb("lata", [128, TBL], F32)
    rot = C.sb("larot", [128, TBL], F32)
    sqs = C.sb("lasq", [128, TBL], BF16)
    nrw = C.sb("lanrw", [1, TBL], F32)
    r11 = C.sb("lar11", [1, TBL], F32)
    rb = C.sb("larb", [64, TBL], F32)
    osum = C.sb("laos", [128, TBL], F32)
    xres = C.sb("laxres", [128, TBL], F32)
    n1c = C.sb("lan1c", [128, 8], F32)
    n2c = C.sb("lan2c", [128, 8], F32)
    gkd_sb = C.sb("lagkd", [128, NCH * 256], BF16)
    gku_sb = C.sb("lagku", [128, 2 * 256], BF16)
    gvu_sb = C.sb("lagvu", [128, 2 * 256], BF16)
    lwk_sb = C.sb("lalwk", [128, NCH * 256], BF16)
    lwv_sb = C.sb("lalwv", [128, NCH * 256], BF16)
    em.dma("y", n1c[:], W["la_n1"])
    tok_w = em.dma("y", n2c[:], W["la_n2"])
    for kc in range(NCH):
        em.dma("p", gkd_sb[:, kc * 256:(kc + 1) * 256],
               W["la_gkd"][kc * 128:(kc + 1) * 128, :])
        em.dma("p", lwk_sb[:, kc * 256:(kc + 1) * 256],
               W["la_lwk"][kc * 128:(kc + 1) * 128, :])
        tok_w = em.dma("p", lwv_sb[:, kc * 256:(kc + 1) * 256],
                       W["la_lwv"][kc * 128:(kc + 1) * 128, :])
    for kc in range(2):
        em.dma("p", gku_sb[:, kc * 256:(kc + 1) * 256],
               W["la_gku"][kc * 128:(kc + 1) * 128, :])
        tok_w = em.dma("p", gvu_sb[:, kc * 256:(kc + 1) * 256],
                       W["la_gvu"][kc * 128:(kc + 1) * 128, :])

    tok_vones = em.op("v", lambda v: v.memset(vbar[:, 64::65], 1.0))
    pb_hist = {}
    wstr_toks = [None, None]
    wstr_used = [None, None]

    st_last = None
    prev_sf = None
    for blk in range(NB):
        xln_toks, _ = emit_ln(C, x_hbm, xb, xln, blk, TBL,
                              after=[tok_w, st_last],
                              f32_hbm=h0f_hbm, f32_bufs=xf32)
        st_toks = [em.dma("y", h0b_hbm[128 * c:128 * (c + 1),
                                       blk * TBL:(blk + 1) * TBL],
                          xln[c][:, 0:TBL], after=[xln_toks[c]])
                   for c in range(NCH)]
        st_last = st_toks[-1]
    p0_done = st_last

    def rms_apply(src_hbm, blk, wcol, extra):
        t0 = blk * TBL
        ld = [em.dma("p", xb[c][:, 0:TBL],
                     src_hbm[128 * c:128 * (c + 1), t0:t0 + TBL],
                     after=extra) for c in range(NCH)]
        lns = C.ln_scratch()
        sq = lns["sq"]
        ps_q = C.pb[7]
        mm_toks = []
        for c in range(NCH):
            tsq = em.op("s", lambda s, c=c: s.activation(
                sq[:, 0:TBL], xb[c][:, 0:TBL], AF.Square),
                after=[ld[c]] + ([mm_toks[-1]] if c else list(extra)))
            mm_toks.append(em.op("t", lambda t, c=c: t.matmul(
                ps_q[0:1, 0:TBL], C.ones_col_bf[:], sq[:, 0:TBL],
                start=(c == 0), stop=(c == NCH - 1)),
                after=[ld[c], tsq, C.t_const, pb_hist.get(7)]))
        r_row = lns["r"][:, 0:TBL]
        tk1 = em.op("v", lambda v: v.tensor_scalar_mul(
            r_row, ps_q[0:1, 0:TBL], 1.0 / DM), after=[mm_toks[-1]])
        tk2 = em.op("s", lambda s: s.activation(r_row, r_row, AF.Sqrt,
                                                bias=eps6[0:1, :]),
                    after=[tk1, tok_e6])
        tk3 = em.op("v", lambda v: v.reciprocal(r_row, r_row), after=[tk2])
        tb1 = em.op("t", lambda t: t.matmul(ps_q[:, 0:TBL], C.ones_row_f[:],
                                            r_row, start=True, stop=True),
                    after=[tk3])
        pb_hist[7] = tb1
        out_toks = []
        for c in range(NCH):
            t1 = em.op("v", lambda v, c=c: v.tensor_mul(
                qr[:, 0:TBL], xb[c][:, 0:TBL], ps_q[:, 0:TBL]),
                after=[tb1] + ([out_toks[-1]] if c else []))
            out_toks.append(em.op("v", lambda v, c=c, wcol=wcol: v.tensor_scalar_mul(
                nb_[c][:, 0:TBL], qr[:, 0:TBL], wcol[:, c:c + 1]),
                after=[t1, tok_w]))
        pb_hist[7] = out_toks[-1]
        return out_toks, ld

    def rope_norm(bank, nparts, dst, dst_off, blk, inv8, extra):
        t0 = blk * TBL
        t1 = em.op("s", lambda s, bank=bank, nparts=nparts: s.copy(
            qr[0:nparts, 0:TBL], bank[0:nparts, 0:TBL]), after=extra)
        mask = [(i ^ 1) for i in range(32)]
        t2 = em.op("v", lambda v, nparts=nparts: v.stream_shuffle(
            qs[0:nparts, 0:TBL], qr[0:nparts, 0:TBL], mask), after=[t1])
        t3 = em.op("v", lambda v, nparts=nparts, t0=t0: v.tensor_mul(
            ta_[0:nparts, 0:TBL], qr[0:nparts, 0:TBL],
            cosb[0:nparts, t0:t0 + TBL]), after=[t1])
        t4 = em.op("v", lambda v, nparts=nparts, t0=t0: v.tensor_mul(
            qs[0:nparts, 0:TBL], qs[0:nparts, 0:TBL],
            sinb[0:nparts, t0:t0 + TBL]), after=[t2])
        t5 = em.op("v", lambda v, nparts=nparts: v.tensor_add(
            rot[0:nparts, 0:TBL], ta_[0:nparts, 0:TBL], qs[0:nparts, 0:TBL]),
            after=[t3, t4])
        t6 = em.op("s", lambda s, nparts=nparts: s.activation(
            sqs[0:nparts, 0:TBL], rot[0:nparts, 0:TBL], AF.Square),
            after=[t5])
        res = t6
        for half in range(nparts // 64):
            b = 64 * half
            t7 = em.op("t", lambda t, b=b: t.matmul(
                C.pb[4][0:1, 0:TBL], C.ones_col_bf[b:b + 64, :],
                sqs[b:b + 64, 0:TBL], start=True, stop=True),
                after=[t6, C.t_const, pb_hist.get(4)])
            t8 = em.op("v", lambda v: v.tensor_scalar_mul(
                nrw[:, 0:TBL], C.pb[4][0:1, 0:TBL], 1.0 / 64.0), after=[t7])
            t9 = em.op("s", lambda s: s.activation(nrw[:, 0:TBL], nrw[:, 0:TBL],
                                                   AF.Sqrt, bias=eps6[0:1, :]),
                       after=[t8, tok_e6])
            t10 = em.op("v", lambda v: v.reciprocal(nrw[:, 0:TBL],
                                                    nrw[:, 0:TBL]), after=[t9])
            if inv8:
                t10 = em.op("v", lambda v: v.tensor_scalar_mul(
                    nrw[:, 0:TBL], nrw[:, 0:TBL], 0.125), after=[t10])
            t11 = em.op("t", lambda t, b=b: t.matmul(
                C.pb[4][b:b + 64, 0:TBL], C.ones_row_f[:, 0:64],
                nrw[:, 0:TBL], start=True, stop=True), after=[t10])
            res = em.op("v", lambda v, b=b, dst=dst, dst_off=dst_off:
                        v.tensor_mul(dst[b:b + 64, dst_off:dst_off + TBL],
                                     rot[b:b + 64, 0:TBL],
                                     C.pb[4][b:b + 64, 0:TBL]), after=[t11])
            pb_hist[4] = res
        return res

    def attn_branch(local, src_hbm, res1_hbm, out_hbm, wcol, wql, wol,
                    kw_sb, kw_cols, vw_sb, cos_hbm, sin_hbm, use_latent,
                    extra0):
        dpre = "ll" if local else "lg"
        """One attention branch; writes out_hbm = res1 + attn_out (+x for
        local, handled by caller via res streams).  Returns last token."""
        tok_cs = em.dma("y", cosb[:], cos_hbm, after=extra0)
        tok_cs = em.dma("y", sinb[:], sin_hbm, after=[tok_cs])
        pbuf_hist = [None, None]
        aout_hist = None
        qb_hist = None
        prev_res = None
        kv_guard = []
        x2_used = None
        for blk in range(NB):
            tq0 = blk * TBL
            nm_toks, _ = rms_apply(src_hbm, blk, wcol,
                                   [tok_cs, prev_res, p0_done] + kv_guard +
                                   ([qb_hist] if qb_hist else []))
            lat_toks = nm_toks
            if use_latent:
                lat_toks = []
                for lc in range(2):
                    bank = C.pb[lc % 2]
                    fns = [lambda t, kc=kc, lc=lc, bank=bank: t.matmul(
                        bank[:, 0:TBL],
                        gkd_sb[:, kc * 256 + lc * 128:kc * 256 + (lc + 1) * 128],
                        nb_[kc][:, 0:TBL], start=(kc == 0), stop=(kc == NCH - 1))
                        for kc in range(NCH)]
                    tmm = em.ops("t", fns, after=nm_toks + [tok_w,
                                                            pb_hist.get(lc % 2)])
                    lat_toks.append(em.op("v", lambda v, lc=lc, bank=bank:
                                          v.tensor_copy(lat[lc][:, 0:TBL],
                                                        bank[:, 0:TBL]),
                                          after=[tmm]))
                    pb_hist[lc % 2] = lat_toks[-1]
                ksrc = lat
                k_nch = 2
                kk_sb = gku_sb
            else:
                ksrc = nb_
                k_nch = NCH
                kk_sb = kw_sb
            for ck in range(2):
                bank = C.pb[2 + ck % 2]
                fns = [lambda t, kc=kc, ck=ck, bank=bank, kk_sb=kk_sb,
                       k_nch=k_nch: t.matmul(
                           bank[:, 0:TBL],
                           kk_sb[:, kc * 256 + ck * 128:kc * 256 + (ck + 1) * 128],
                           ksrc[kc][:, 0:TBL], start=(kc == 0),
                           stop=(kc == k_nch - 1)) for kc in range(k_nch)]
                tmm = em.ops("t", fns, after=(lat_toks if use_latent else nm_toks)
                             + [tok_w, pb_hist.get(2 + ck % 2)])
                tkn = rope_norm(bank, 128, kg[ck], blk * TBL, blk, False,
                                [tmm, tok_cs])
                pb_hist[2 + ck % 2] = tkn
            for ts_ in range(4):
                bank = C.pb[2 + ts_ % 2]
                vsrc = lat if use_latent else nb_
                v_nch = 2 if use_latent else NCH
                vw = gvu_sb if use_latent else vw_sb
                fns = [lambda t, kc=kc, ts_=ts_, bank=bank, vw=vw, vsrc=vsrc,
                       v_nch=v_nch: t.matmul(
                           bank[:, 0:256], vsrc[kc][:, ts_ * 128:(ts_ + 1) * 128],
                           vw[:, kc * 256:(kc + 1) * 256],
                           start=(kc == 0), stop=(kc == v_nch - 1))
                       for kc in range(v_nch)]
                tmm = em.ops("t", fns,
                             after=(lat_toks if use_latent else nm_toks) +
                             [tok_w, pb_hist.get(2 + ts_ % 2)])
                j = blk * 4 + ts_
                fns2 = [lambda v, hh=hh, j=j, bank=bank: v.tensor_copy(
                    vbar[:, j * 260 + hh * 65:j * 260 + hh * 65 + 64],
                    bank[:, hh * 64:(hh + 1) * 64]) for hh in range(4)]
                tvv = em.ops("v", fns2, after=[tmm, tok_vones])
                pb_hist[2 + ts_ % 2] = tvv
            kv_ready = [tkn, tvv]
            if blk == 0:
                for c_ in range(NCH):
                    dump(dpre + "n2_%d" % c_, nb_[c_][:, 0:TBL], nm_toks)
                dump(dpre + "kg0", kg[0][:, 0:TBL], [tkn])
                dump(dpre + "vb0", vbar[0:128, 0:260], [tvv])
            q_last = None
            for ch in range(NCH):
                buf = ch % 2
                wdep = [wstr_used[buf]] if wstr_used[buf] else []
                wstr_toks[buf] = em.dma("p", wstr[buf][:],
                                        wql[ch * 128:(ch + 1) * 128, :],
                                        after=wdep)
                bank = C.pb[ch % 2]
                fns = [lambda t, kc=kc, buf=buf, bank=bank: t.matmul(
                    bank[:, 0:TBL], wstr[buf][:, kc * 128:(kc + 1) * 128],
                    nb_[kc][:, 0:TBL], start=(kc == 0), stop=(kc == NCH - 1))
                    for kc in range(NCH)]
                tmm = em.ops("t", fns, after=nm_toks + [wstr_toks[buf],
                                                        pb_hist.get(ch % 2)])
                wstr_used[buf] = tmm
                q_last = rope_norm(bank, 128, qg[ch], 0, blk, True,
                                   [tmm, tok_cs])
                pb_hist[ch % 2] = q_last
            for h in range(16):
                kvh = h // 4
                ck, kvp = divmod(kvh, 2)
                b64 = 64 * kvp
                if kvp == 0:
                    ch = LQ_H0.index(h)
                else:
                    ch = LQ_H1.index(h)
                jlo = 0 if not local else max(blk * 4 - 2, 0)
                njc = blk * 4 + 4
                obank = C.pb[2 + h % 2]
                first_dep = pb_hist.get(2 + h % 2)
                idxs = list(range(jlo, njc))
                n_ = len(idxs)
                prs = [idxs[i:i + 2] for i in range(0, n_, 2)]
                for pi, pr in enumerate(prs):
                    w = pi % 2
                    wide = C.psA if w == 0 else C.psB
                    bks = (0, 1) if w == 0 else (5, 6)
                    fns = []
                    for sub, jc in enumerate(pr):
                        fns.append(lambda t, jc=jc, o0=sub * TBL, ck=ck,
                                   b64=b64, ch=ch, wide=wide: t.matmul(
                                       wide[:, o0:o0 + TBL],
                                       kg[ck][b64:b64 + 64,
                                              jc * 128:(jc + 1) * 128],
                                       qg[ch][b64:b64 + 64, 0:TBL],
                                       start=True, stop=True))
                    tmm = em.ops("t", fns, after=[q_last, pb_hist.get(bks[0]),
                                                  pb_hist.get(bks[1])] +
                                 kv_ready)
                    wd = len(pr) * TBL
                    texp = em.op("s", lambda s, w=w, wd=wd, wide=wide:
                                 s.activation(pbuf[w][:, 0:wd],
                                              wide[:, 0:wd], AF.Exp,
                                              bias=bias_col[:]),
                                 after=[tmm, tok_bc] +
                                 ([pbuf_hist[w]] if pbuf_hist[w] else []))
                    pb_hist[bks[0]] = texp
                    pb_hist[bks[1]] = texp
                    tlast = texp
                    for sub, jc in enumerate(pr):
                        o0 = sub * TBL
                        if jc >= blk * 4:
                            tlast = em.op("p", lambda g, w=w, o0=o0,
                                          base=tq0 - jc * 128:
                                          g.affine_select(
                                              pbuf[w][:, o0:o0 + TBL],
                                              pbuf[w][:, o0:o0 + TBL],
                                              [[1, TBL]], ALU.is_ge,
                                              0.0, base=base,
                                              channel_multiplier=-1),
                                          after=[tlast])
                        if local and jc * 128 < tq0 + TBL - 255:
                            tlast = em.op("p", lambda g, w=w, o0=o0,
                                          base=jc * 128 - tq0 + 255:
                                          g.affine_select(
                                              pbuf[w][:, o0:o0 + TBL],
                                              pbuf[w][:, o0:o0 + TBL],
                                              [[-1, TBL]], ALU.is_ge,
                                              0.0, base=base,
                                              channel_multiplier=1),
                                          after=[tlast])
                    fns = []
                    for sub, jc in enumerate(pr):
                        gi = 2 * pi + sub
                        fns.append(lambda t, jc=jc, sub=sub, gi=gi, w=w,
                                   obank=obank, kvh=kvh, n=n_: t.matmul(
                                       obank[0:65, 0:TBL],
                                       vbar[:, jc * 260 + kvh * 65:
                                            jc * 260 + kvh * 65 + 65],
                                       pbuf[w][:, sub * TBL:(sub + 1) * TBL],
                                       start=(gi == 0), stop=(gi == n - 1)),)
                    tv = em.ops("t", fns, after=[tlast] +
                                ([first_dep] if pi == 0 else []))
                    pbuf_hist[w] = tv
                kv_guard = [tv]
                tr = em.op("v", lambda v, obank=obank: v.reciprocal(
                    r11[:, 0:TBL], obank[64:65, 0:TBL]), after=[tv])
                tb2 = em.op("t", lambda t: t.matmul(
                    C.pb[4][0:64, 0:TBL], C.ones_row_f[:, 0:64], r11[:, 0:TBL],
                    start=True, stop=True), after=[tr, pb_hist.get(4)])
                tb3 = em.op("v", lambda v: v.tensor_copy(
                    rb[:, 0:TBL], C.pb[4][0:64, 0:TBL]), after=[tb2])
                pb_hist[4] = tb3
                tao = em.op("v", lambda v, ch=ch, b64=b64, obank=obank:
                            v.tensor_mul(aout[ch][b64:b64 + 64, 0:TBL],
                                         obank[0:64, 0:TBL], rb[:, 0:TBL]),
                            after=[tb3, aout_hist])
                pb_hist[2 + h % 2] = tao
                if blk == 0 and h == 15:
                    for c_ in range(NCH):
                        dump(dpre + "ao_%d" % c_, aout[c_][:, 0:TBL], [tao])
            aout_last = tao
            res_toks = []
            for oc in range(NCH):
                buf = oc % 2
                wdep = [wstr_used[buf]] if wstr_used[buf] else []
                wstr_toks[buf] = em.dma("p", wstr[buf][:],
                                        wol[oc * 128:(oc + 1) * 128, :],
                                        after=wdep)
                bank = C.pb[5 + oc % 2]
                fns = [lambda t, kc=kc, buf=buf, bank=bank: t.matmul(
                    bank[:, 0:TBL], wstr[buf][:, kc * 128:(kc + 1) * 128],
                    aout[kc][:, 0:TBL], start=(kc == 0), stop=(kc == NCH - 1))
                    for kc in range(NCH)]
                tmm = em.ops("t", fns, after=[aout_last, wstr_toks[buf],
                                              pb_hist.get(5 + oc % 2)])
                wstr_used[buf] = tmm
                qb_hist = tmm
                if blk == 0 and oc == 0 and dbg is not None:
                    tcp9 = em.op("v", lambda v, bank=bank: v.tensor_copy(
                        rot[:, 0:TBL], bank[:, 0:TBL]), after=[tmm])
                    pb_hist[5 + oc % 2] = tcp9
                    dump(dpre + "wops", rot[:, 0:TBL], [tcp9])
                    dump(dpre + "wstr0", wstr[0][:, 0:1024], [tcp9])
                xi = xf32[oc % 2]
                dump_resid = (blk == 0 and oc == 0 and dbg is not None)
                tx = em.dma("y", xi[:, 0:TBL],
                            res1_hbm[128 * oc:128 * (oc + 1), tq0:tq0 + TBL])
                if local:
                    tx2 = em.dma("y", xres[:, 0:TBL],
                                 x_hbm[128 * oc:128 * (oc + 1), tq0:tq0 + TBL],
                                 after=[x2_used] if x2_used else [])
                    tv1 = em.op("v", lambda v, xi=xi, bank=bank: v.tensor_add(
                        osum[:, 0:TBL], xi[:, 0:TBL], bank[:, 0:TBL]),
                        after=[tmm, tx] + ([res_toks[-1]] if res_toks else []))
                    tv2 = em.op("v", lambda v: v.tensor_add(
                        osum[:, 0:TBL], osum[:, 0:TBL], xres[:, 0:TBL]),
                        after=[tv1, tx2])
                    x2_used = tv2
                else:
                    tv2 = em.op("v", lambda v, xi=xi, bank=bank: v.tensor_add(
                        osum[:, 0:TBL], xi[:, 0:TBL], bank[:, 0:TBL]),
                        after=[tmm, tx] + ([res_toks[-1]] if res_toks else []))
                pb_hist[5 + oc % 2] = tv2
                if dump_resid:
                    dump(dpre + "xi0", xi[:, 0:TBL], [tv2])
                    dump(dpre + "os0", osum[:, 0:TBL], [tv2])
                    if local:
                        dump(dpre + "qr0", qr[:, 0:TBL], [tv2])
                res_toks.append(em.dma("y", out_hbm[128 * oc:128 * (oc + 1),
                                                    tq0:tq0 + TBL],
                                       osum[:, 0:TBL], after=[tv2]))
            aout_hist = qb_hist
            prev_res = res_toks[-1]
        return res_toks

    res1 = attn_branch(False, h0b_hbm, h0f_hbm, h1f_hbm, n1c,
                       W["la_gwql"], W["la_gwol"], None, None, None,
                       W["la_cosg"], W["la_sing"], True, [p0_done])
    res2 = attn_branch(True, h1f_hbm, h1f_hbm, xo_hbm, n2c,
                       W["la_lwql"], W["la_lwol"], lwk_sb, None, lwv_sb,
                       W["la_cosl"], W["la_sinl"], False, [res1[-1]])
    return res2


def build_l1attna_program(debug=False):
    nc = bass.Bass("TRN2", target_bir_lowering=False)
    x_in = nc.dram_tensor("x_in", [DM, T], F32, kind="ExternalInput").ap()
    x_out = nc.dram_tensor("x_out", [DM, T], F32, kind="ExternalOutput").ap()
    W = {}
    for name, shape in [("la_gwql", [1024, 1024]), ("la_gwol", [1024, 1024]),
                        ("la_lwql", [1024, 1024]), ("la_lwol", [1024, 1024]),
                        ("la_gkd", [1024, 256]), ("la_gku", [256, 256]),
                        ("la_gvu", [256, 256]), ("la_lwk", [1024, 256]),
                        ("la_lwv", [1024, 256]), ("la_n1", [128, 8]),
                        ("la_n2", [128, 8]), ("la_cosg", [128, T]),
                        ("la_sing", [128, T]), ("la_cosl", [128, T]),
                        ("la_sinl", [128, T])]:
        W[name] = nc.dram_tensor(name, shape, F32, kind="ExternalInput").ap()
    kind = "ExternalOutput" if debug else "Internal"
    h0b = nc.dram_tensor("h0b_s", [DM, T], BF16, kind="Internal").ap()
    h0f = nc.dram_tensor("h0f_s", [DM, T], F32, kind=kind).ap()
    h1f = nc.dram_tensor("h1f_s", [DM, T], F32, kind=kind).ap()
    C = Core(nc)
    dbg = None
    if debug:
        dbg = {}
        for pre in ("lg", "ll"):
            for c_ in range(NCH):
                dbg[pre + "n2_%d" % c_] = nc.dram_tensor(
                    "dbg_" + pre + "n2_%d" % c_, [128, 512], F32,
                    kind="ExternalOutput").ap()
                dbg[pre + "ao_%d" % c_] = nc.dram_tensor(
                    "dbg_" + pre + "ao_%d" % c_, [128, 512], F32,
                    kind="ExternalOutput").ap()
            for nm, sh in [("kg0", [128, 512]), ("vb0", [128, 260]),
                           ("p0", [128, 512]), ("q0", [128, 512]),
                           ("wops", [128, 512]), ("wstr0", [128, 1024]),
                           ("xi0", [128, 512]), ("os0", [128, 512]),
                           ("qr0", [128, 512])]:
                dbg[pre + nm] = nc.dram_tensor("dbg_" + pre + nm, sh, F32,
                                               kind="ExternalOutput").ap()
    res = emit_l1attna(C, x_in, x_out, W, h0b, h0f, h1f, dbg=dbg)
    C.em.finalize(res)
    return nc



def host_prep_l1attnb(inputs):
    wqkv = np.asarray(inputs["L1_attn_b_w_qkv"])
    bqkv = np.asarray(inputs["L1_attn_b_b_qkv"])
    wout = np.asarray(inputs["L1_attn_b_w_out"])
    bout = np.asarray(inputs["L1_attn_b_b_out"])

    def stream_layout(w):
        return np.ascontiguousarray(
            w.reshape(8, 128, 8, 128).transpose(2, 1, 0, 3).reshape(1024, 1024))

    return {"lb_wql": stream_layout(wqkv[:, 0:1024]),
            "lb_wkl": stream_layout(wqkv[:, 1024:2048]),
            "lb_wv": np.ascontiguousarray(wqkv[:, 2048:3072]),
            "lb_wol": stream_layout(wout),
            "lb_bq": np.ascontiguousarray(bqkv[0:1024].reshape(8, 128).T),
            "lb_bk": np.ascontiguousarray(bqkv[1024:2048].reshape(8, 128).T),
            "lb_bv": np.ascontiguousarray(bqkv[None, 2048:3072]),
            "lb_bo": np.ascontiguousarray(bout.reshape(8, 128).T)}


def emit_l1attnb(C, x_hbm, xo_hbm, W, xln_hbm, k_hbm, vb_hbm):
    em = C.em
    TBL = 512
    NB = T // TBL
    wv_sb = C.sb("lbwv", [128, NCH * 1024], BF16)
    tok_w = None
    for kc in range(NCH):
        tok_w = em.dma("p", wv_sb[:, kc * 1024:(kc + 1) * 1024],
                       W["lb_wv"][kc * 128:(kc + 1) * 128, :])
    bq_sb = C.sb("lbbq", [128, 8], F32)
    bk_sb = C.sb("lbbk", [128, 8], F32)
    bv_sb = C.sb("lbbv", [1, 1024], BF16)
    bo_sb = C.sb("lbbo", [128, 8], F32)
    em.dma("y", bq_sb[:], W["lb_bq"])
    em.dma("y", bk_sb[:], W["lb_bk"])
    em.dma("p", bv_sb[:], W["lb_bv"])
    tok_w = em.dma("y", bo_sb[:], W["lb_bo"], after=[tok_w])

    xb = [C.sb("lbxb", [128, TBL], BF16) for _ in range(NCH)]
    xln = [C.sb("lbxln", [128, TBL], BF16) for _ in range(NCH)]
    qbuf = [C.sb("lbq", [128, TBL], BF16) for _ in range(NCH)]
    kbl = [C.sb("lbkb", [128, 768], BF16) for _ in range(NCH)]
    vbl = C.sb("lbvb", [128, 6 * 1040], BF16)
    aout = [C.sb("lbao", [128, TBL], BF16) for _ in range(NCH)]
    pbuf = [C.sb("lbp", [128, 2 * TBL], BF16) for _ in range(2)]
    wstr = [C.sb("lbw", [128, 1024], BF16) for _ in range(2)]
    sqs = C.sb("lbsq", [128, TBL], BF16)
    kB = C.sb("lbkB", [1, 16], F32)
    qB = C.sb("lbqB", [1, 16], F32)
    bBr = C.sb("lbbBr", [1, 16], F32)
    r11 = C.sb("lbr11", [1, TBL], F32)
    rb = C.sb("lbrb", [64, TBL], F32)
    bias_col = C.sb("lbbc", [128, 1], F32)
    osum = C.sb("lbos", [128, TBL], F32)
    xfr = [C.sb("lbxf", [128, TBL], F32) for _ in range(2)]
    vsc = C.sb("lbvsc", [128, 1040], BF16)

    tok_ki = em.op("v", lambda v: v.memset(kB[:], -3e38))
    tok_vo = em.op("v", lambda v: v.memset(vsc[:, 64::65], 1.0))

    pb_hist = {}
    wstr_toks = [None, None]
    wstr_used = [None, None]

    st_last = None
    kn_last = tok_ki
    ph_last = []
    for blk in range(NB):
        xln_toks, _ = emit_ln(C, x_hbm, xb, xln, blk, TBL,
                              after=[tok_w, st_last, kn_last] + ph_last)
        st_toks = [em.dma("y", xln_hbm[128 * c:128 * (c + 1),
                                       blk * TBL:(blk + 1) * TBL],
                          xln[c][:, 0:TBL], after=[xln_toks[c]])
                   for c in range(NCH)]
        st_last = st_toks[-1]
        for ck in range(NCH):
            buf = ck % 2
            wdep = [wstr_used[buf]] if wstr_used[buf] else []
            wstr_toks[buf] = em.dma("p", wstr[buf][:],
                                    W["lb_wkl"][ck * 128:(ck + 1) * 128, :],
                                    after=wdep)
            bank = C.pb[ck % 2]
            fns = [lambda t, kc=kc, buf=buf, bank=bank: t.matmul(
                bank[:, 0:TBL], wstr[buf][:, kc * 128:(kc + 1) * 128],
                xln[kc][:, 0:TBL], start=(kc == 0), stop=(kc == NCH - 1))
                for kc in range(NCH)]
            tmm = em.ops("t", fns, after=xln_toks + [wstr_toks[buf],
                                                     pb_hist.get(ck % 2)])
            wstr_used[buf] = tmm
            tev = em.op("v", lambda v, ck=ck, bank=bank: v.tensor_scalar_add(
                sqs[:, 0:TBL], bank[:, 0:TBL], bk_sb[:, ck:ck + 1]),
                after=[tmm, tok_w, kn_last])
            pb_hist[ck % 2] = tev
            tst = em.dma("y", k_hbm[128 * ck:128 * (ck + 1),
                                    blk * TBL:(blk + 1) * TBL],
                         sqs[:, 0:TBL], after=[tev])
            tsq = em.op("s", lambda s: s.activation(sqs[:, 0:TBL],
                                                    sqs[:, 0:TBL], AF.Square),
                        after=[tev, tst])
            for half in range(2):
                h = 2 * ck + half
                bank2 = C.pb[4]
                tm2 = em.op("t", lambda t, half=half, bank2=bank2: t.matmul(
                    bank2[0:1, 0:TBL], C.ones_col_bf[64 * half:64 * half + 64, :],
                    sqs[64 * half:64 * half + 64, 0:TBL], start=True, stop=True),
                    after=[tsq, pb_hist.get(4), C.t_const])
                tr = em.op("v", lambda v, bank2=bank2: v.tensor_reduce(
                    r11[:, 0:1], bank2[0:1, 0:TBL], AX.X, ALU.max), after=[tm2])
                pb_hist[4] = tr
                kn_last = em.op("v", lambda v, h=h: v.tensor_max(
                    kB[:, h:h + 1], kB[:, h:h + 1], r11[:, 0:1]), after=[tr])
        for ts_ in range(4):
            for vc in range(2):
                bank = C.pb[2 + (2 * ts_ + vc) % 2]
                fns = [lambda t, kc=kc, ts_=ts_, vc=vc, bank=bank: t.matmul(
                    bank[:, 0:512], xln[kc][:, ts_ * 128:(ts_ + 1) * 128],
                    wv_sb[:, kc * 1024 + vc * 512:kc * 1024 + (vc + 1) * 512],
                    start=(kc == 0), stop=False) for kc in range(NCH)]
                fns.append(lambda t, vc=vc, bank=bank: t.matmul(
                    bank[:, 0:512], C.ones_row_bf[:],
                    bv_sb[:, vc * 512:(vc + 1) * 512], start=False, stop=True))
                tmm = em.ops("t", fns, after=xln_toks + [tok_w, C.t_const] +
                             ph_last)
                fns2 = [lambda v, hh=hh, vc=vc, bank=bank: v.tensor_copy(
                    vsc[:, (8 * vc + hh) * 65:(8 * vc + hh) * 65 + 64],
                    bank[:, hh * 64:(hh + 1) * 64]) for hh in range(8)]
                tvv = em.ops("v", fns2, after=[tmm, tok_vo, ph_last[-1] if ph_last else None])
                j = blk * 4 + ts_
                tvs = em.dma("y", vb_hbm[:, j * 1040:(j + 1) * 1040], vsc[:],
                             after=[tvv])
                ph_last = [tvv, tvs]
    phaseA = [kn_last, st_last] + ph_last
    o = em.op("s", lambda s: s.activation(kB[:], kB[:], AF.Sqrt), after=phaseA)
    tok_ks = o

    pbuf_hist = [None, None]
    aout_hist = None
    prev_res = None
    qb_hist = None
    kv_used = None
    for blk in range(NB):
        tq0 = blk * TBL
        jlo = min(max(blk * 4 - 1, 0), 26)
        xl_toks = [em.dma("p", xln[c][:, 0:TBL],
                          xln_hbm[128 * c:128 * (c + 1), tq0:tq0 + TBL],
                          after=[st_last, prev_res] + phaseA +
                          ([qb_hist] if qb_hist else []))
                   for c in range(NCH)]
        kv_toks = [em.dma("y", kbl[ck][:],
                          k_hbm[128 * ck:128 * (ck + 1),
                                jlo * 128:jlo * 128 + 768],
                          after=[kv_used] + phaseA) for ck in range(NCH)]
        kv_toks.append(em.dma("y", vbl[:],
                              vb_hbm[:, jlo * 1040:jlo * 1040 + 6 * 1040],
                              after=[kv_used] + phaseA))
        q_toks = []
        for ch in range(NCH):
            buf = ch % 2
            wdep = [wstr_used[buf]] if wstr_used[buf] else []
            wstr_toks[buf] = em.dma("p", wstr[buf][:],
                                    W["lb_wql"][ch * 128:(ch + 1) * 128, :],
                                    after=wdep)
            bank = C.pb[5 + ch % 2]
            fns = [lambda t, kc=kc, buf=buf, bank=bank: t.matmul(
                bank[:, 0:TBL], wstr[buf][:, kc * 128:(kc + 1) * 128],
                xln[kc][:, 0:TBL], start=(kc == 0), stop=(kc == NCH - 1))
                for kc in range(NCH)]
            tmm = em.ops("t", fns, after=xl_toks + [wstr_toks[buf],
                                                    pb_hist.get(5 + ch % 2)])
            wstr_used[buf] = tmm
            tev = em.op("v", lambda v, ch=ch, bank=bank: v.tensor_scalar(
                qbuf[ch][:, 0:TBL], bank[:, 0:TBL], bq_sb[:, ch:ch + 1], 0.125,
                ALU.add, ALU.mult), after=[tmm, tok_w] +
                ([qb_hist] if qb_hist else []))
            pb_hist[5 + ch % 2] = tev
            tsq = em.op("s", lambda s, ch=ch: s.activation(
                sqs[:, 0:TBL], qbuf[ch][:, 0:TBL], AF.Square), after=[tev])
            for half in range(2):
                h = 2 * ch + half
                bank2 = C.pb[4]
                tm2 = em.op("t", lambda t, half=half, bank2=bank2: t.matmul(
                    bank2[0:1, 0:TBL], C.ones_col_bf[64 * half:64 * half + 64, :],
                    sqs[64 * half:64 * half + 64, 0:TBL], start=True, stop=True),
                    after=[tsq, pb_hist.get(4)])
                tr = em.op("v", lambda v, bank2=bank2: v.tensor_reduce(
                    r11[:, 0:1], bank2[0:1, 0:TBL], AX.X, ALU.max), after=[tm2])
                pb_hist[4] = tr
                q_toks.append(em.op("v", lambda v, h=h: v.tensor_copy(
                    qB[:, h:h + 1], r11[:, 0:1]), after=[tr]))
        o = em.op("s", lambda s: s.activation(qB[:], qB[:], AF.Sqrt),
                  after=q_toks)
        o = em.op("v", lambda v: v.tensor_mul(bBr[:], qB[:], kB[:]),
                  after=[o, tok_ks])
        tok_bB = em.op("v", lambda v: v.tensor_scalar_mul(bBr[:], bBr[:], -1.0),
                       after=[o])
        for h in range(16):
            ch, hp = divmod(h, 2)
            b64 = 64 * hp
            tb = em.op("t", lambda t, h=h: t.matmul(
                C.pb[4][:, 0:1], C.ones_row_f[:], bBr[0:1, h:h + 1],
                start=True, stop=True), after=[tok_bB, pb_hist.get(4)])
            tbc = em.op("v", lambda v: v.tensor_copy(bias_col[:],
                                                     C.pb[4][:, 0:1]),
                        after=[tb])
            pb_hist[4] = tbc
            obank = C.pb[2 + h % 2]
            first_dep = pb_hist.get(2 + h % 2)
            for pi in range(3):
                w = pi % 2
                wide = C.psA if w == 0 else C.psB
                bks = (0, 1) if w == 0 else (5, 6)
                fns = []
                for sub in range(2):
                    i = 2 * pi + sub
                    fns.append(lambda t, i=i, o0=sub * TBL, ch=ch, b64=b64,
                               wide=wide: t.matmul(
                                   wide[:, o0:o0 + TBL],
                                   kbl[ch][b64:b64 + 64, i * 128:(i + 1) * 128],
                                   qbuf[ch][b64:b64 + 64, 0:TBL],
                                   start=True, stop=True))
                tmm = em.ops("t", fns, after=[q_toks[-1], pb_hist.get(bks[0]),
                                              pb_hist.get(bks[1])] + kv_toks)
                texp = em.op("s", lambda s, w=w, wide=wide: s.activation(
                    pbuf[w][:, 0:2 * TBL], wide[:, 0:2 * TBL], AF.Exp,
                    bias=bias_col[:]),
                    after=[tmm, tbc] +
                    ([pbuf_hist[w]] if pbuf_hist[w] else []))
                pb_hist[bks[0]] = texp
                pb_hist[bks[1]] = texp
                tlast = texp
                for sub in range(2):
                    i = 2 * pi + sub
                    jc = jlo + i
                    o0 = sub * TBL
                    tlast = em.op("p", lambda g, w=w, o0=o0, jc=jc, tq0=tq0:
                                  g.affine_select(pbuf[w][:, o0:o0 + TBL],
                                                  pbuf[w][:, o0:o0 + TBL],
                                                  [[1, TBL]], ALU.is_ge, 0.0,
                                                  base=tq0 - jc * 128 + 64,
                                                  channel_multiplier=-1),
                                  after=[tlast])
                    tlast = em.op("p", lambda g, w=w, o0=o0, jc=jc, tq0=tq0:
                                  g.affine_select(pbuf[w][:, o0:o0 + TBL],
                                                  pbuf[w][:, o0:o0 + TBL],
                                                  [[-1, TBL]], ALU.is_ge, 0.0,
                                                  base=jc * 128 - tq0 + 64,
                                                  channel_multiplier=1),
                                  after=[tlast])
                fns = []
                for sub in range(2):
                    i = 2 * pi + sub
                    fns.append(lambda t, i=i, sub=sub, w=w, obank=obank, h=h:
                               t.matmul(obank[0:65, 0:TBL],
                                        vbl[:, i * 1040 + h * 65:
                                            i * 1040 + h * 65 + 65],
                                        pbuf[w][:, sub * TBL:(sub + 1) * TBL],
                                        start=(i == 0), stop=(i == 5)))
                tv = em.ops("t", fns, after=[tlast] +
                            ([first_dep] if pi == 0 else []))
                pbuf_hist[w] = tv
                kv_used = tv
            tr = em.op("v", lambda v, obank=obank: v.reciprocal(
                r11[:, 0:TBL], obank[64:65, 0:TBL]), after=[tv])
            tb2 = em.op("t", lambda t: t.matmul(
                C.pb[4][0:64, 0:TBL], C.ones_row_f[:, 0:64], r11[:, 0:TBL],
                start=True, stop=True), after=[tr, pb_hist.get(4)])
            tb3 = em.op("v", lambda v: v.tensor_copy(rb[:, 0:TBL],
                                                     C.pb[4][0:64, 0:TBL]),
                        after=[tb2])
            pb_hist[4] = tb3
            tao = em.op("v", lambda v, ch=ch, b64=b64, obank=obank:
                        v.tensor_mul(aout[ch][b64:b64 + 64, 0:TBL],
                                     obank[0:64, 0:TBL], rb[:, 0:TBL]),
                        after=[tb3, aout_hist])
            pb_hist[2 + h % 2] = tao
        aout_last = tao
        res_toks = []
        for oc in range(NCH):
            buf = oc % 2
            wdep = [wstr_used[buf]] if wstr_used[buf] else []
            wstr_toks[buf] = em.dma("p", wstr[buf][:],
                                    W["lb_wol"][oc * 128:(oc + 1) * 128, :],
                                    after=wdep)
            bank = C.pb[5 + oc % 2]
            fns = [lambda t, kc=kc, buf=buf, bank=bank: t.matmul(
                bank[:, 0:TBL], wstr[buf][:, kc * 128:(kc + 1) * 128],
                aout[kc][:, 0:TBL], start=(kc == 0), stop=(kc == NCH - 1))
                for kc in range(NCH)]
            tmm = em.ops("t", fns, after=[aout_last, wstr_toks[buf],
                                          pb_hist.get(5 + oc % 2)])
            wstr_used[buf] = tmm
            qb_hist = tmm
            xi = xfr[oc % 2]
            tx = em.dma("y", xi[:, 0:TBL],
                        x_hbm[128 * oc:128 * (oc + 1), tq0:tq0 + TBL])
            tv2 = em.op("v", lambda v, oc=oc, xi=xi, bank=bank:
                        v.scalar_tensor_tensor(osum[:, 0:TBL], bank[:, 0:TBL],
                                               bo_sb[:, oc:oc + 1],
                                               xi[:, 0:TBL], ALU.add, ALU.add),
                        after=[tmm, tx] + ([res_toks[-1]] if res_toks else []))
            pb_hist[5 + oc % 2] = tv2
            res_toks.append(em.dma("y", xo_hbm[128 * oc:128 * (oc + 1),
                                               tq0:tq0 + TBL],
                                   osum[:, 0:TBL], after=[tv2]))
        aout_hist = qb_hist
        prev_res = res_toks[-1]
    return res_toks


def build_l1attnb_program():
    nc = bass.Bass("TRN2", target_bir_lowering=False)
    x_in = nc.dram_tensor("x_in", [DM, T], F32, kind="ExternalInput").ap()
    x_out = nc.dram_tensor("x_out", [DM, T], F32, kind="ExternalOutput").ap()
    W = {}
    for name, shape in [("lb_wql", [1024, 1024]), ("lb_wkl", [1024, 1024]),
                        ("lb_wv", [1024, 1024]), ("lb_wol", [1024, 1024]),
                        ("lb_bq", [128, 8]), ("lb_bk", [128, 8]),
                        ("lb_bv", [1, 1024]), ("lb_bo", [128, 8])]:
        W[name] = nc.dram_tensor(name, shape, F32, kind="ExternalInput").ap()
    xln_hbm = nc.dram_tensor("xln_s", [DM, T], BF16, kind="Internal").ap()
    k_hbm = nc.dram_tensor("k_s", [DM, T], BF16, kind="Internal").ap()
    vb_hbm = nc.dram_tensor("vb_s", [128, 32 * 1040], BF16, kind="Internal").ap()
    C = Core(nc)
    res = emit_l1attnb(C, x_in, x_out, W, xln_hbm, k_hbm, vb_hbm)
    C.em.finalize(res)
    return nc



def host_prep_l0attna(inputs):
    wq = np.asarray(inputs["L0_attn_a_Wq"])
    wk = np.asarray(inputs["L0_attn_a_Wk"])
    wv = np.asarray(inputs["L0_attn_a_Wv"])
    wo = np.asarray(inputs["L0_attn_a_Wo"])
    bq = np.asarray(inputs["L0_attn_a_bq"])
    bk = np.asarray(inputs["L0_attn_a_bk"])
    bv = np.asarray(inputs["L0_attn_a_bv"])
    bo = np.asarray(inputs["L0_attn_a_bo"])
    qph = np.asarray(inputs["L0_attn_a_q_phase"]).reshape(256)
    kph = np.asarray(inputs["L0_attn_a_k_phase"]).reshape(256)
    temp = np.asarray(inputs["L0_attn_a_temperature"]).reshape(1, 1)

    def pad_w(w):
        wp = np.zeros((1024, 768), np.float32)
        for h in range(16):
            c, j = divmod(h, 3)
            wp[:, 128 * c + 32 * j:128 * c + 32 * j + 16] = \
                w[:, 16 * h:16 * h + 16]
        return wp.reshape(8, 128, 6, 128).transpose(2, 1, 0, 3).reshape(768, 1024)

    def pad_b(b):
        bp = np.zeros((128, 6), np.float32)
        for h in range(16):
            c, j = divmod(h, 3)
            bp[32 * j:32 * j + 16, c] = b[16 * h:16 * h + 16]
        return bp

    wol = wo.reshape(8, 128, 8, 128).transpose(2, 1, 0, 3).reshape(8 * 128, 1024)
    e3 = np.zeros((128, 3), np.float32)
    e3t = np.zeros((3, 128), np.float32)
    for j in range(3):
        e3[32 * j:32 * j + 16, j] = 1.0
        e3t[j, 32 * j:32 * j + 16] = 1.0
    return {"aa_wqp": np.ascontiguousarray(pad_w(wq)),
            "aa_wkp": np.ascontiguousarray(pad_w(wk)),
            "aa_wv": np.ascontiguousarray(wv),
            "aa_wol": np.ascontiguousarray(wol),
            "aa_bq": pad_b(bq + 0.0), "aa_qph": pad_b(qph),
            "aa_bk": pad_b(bk + 0.0), "aa_kph": pad_b(kph),
            "aa_bv": np.ascontiguousarray(bv[None, :]),
            "aa_bo": np.ascontiguousarray(bo.reshape(8, 128).T),
            "aa_temp": temp, "aa_e3": e3, "aa_e3t": e3t}


def emit_l0attna(C, x_hbm, xo_hbm, W, xln_hbm):
    em = C.em
    TBL = 512
    NB = T // TBL
    NPC = 6
    wv_sb = C.sb("aawv", [128, NCH * 1024], BF16)
    tok_w = None
    for kc in range(NCH):
        tok_w = em.dma("p", wv_sb[:, kc * 1024:(kc + 1) * 1024],
                       W["aa_wv"][kc * 128:(kc + 1) * 128, :])
    bqc = C.sb("aabq", [128, NPC], F32)
    bkc = C.sb("aabk", [128, NPC], F32)
    tmp6 = C.sb("aat6", [128, NPC], F32)
    em.dma("y", bqc[:], W["aa_bq"])
    em.dma("y", bkc[:], W["aa_bk"])
    t_b1 = em.dma("y", tmp6[:], W["aa_qph"])
    t_b2 = em.op("v", lambda v: v.tensor_add(bqc[:], bqc[:], tmp6[:]),
                 after=[t_b1])
    t_b3 = em.dma("y", tmp6[:], W["aa_kph"], after=[t_b2])
    tok_b = em.op("v", lambda v: v.tensor_add(bkc[:], bkc[:], tmp6[:]),
                  after=[t_b3])
    bo_sb = C.sb("aabo", [128, 8], F32)
    bv_sb = C.sb("aabv", [1, 1024], BF16)
    e3_sb = C.sb("aae3", [128, 3], BF16)
    e3t_sb = C.sb("aae3t", [3, 128], F32)
    tok_w = em.dma("y", bo_sb[:], W["aa_bo"], after=[tok_w])
    tok_w = em.dma("p", bv_sb[:], W["aa_bv"], after=[tok_w])
    tok_w = em.dma("p", e3_sb[:], W["aa_e3"], after=[tok_w])
    tok_w = em.dma("y", e3t_sb[:], W["aa_e3t"], after=[tok_w])
    tsc = C.sb("aats", [1, 1], F32)
    it3 = C.sb("aait3", [128, 1], F32)
    bias_col = C.sb("aabc", [128, 1], F32)
    tt0 = em.dma("y", tsc[:], W["aa_temp"])
    tt1 = em.op("v", lambda v: v.reciprocal(tsc[:], tsc[:]), after=[tt0])
    tt2 = em.op("t", lambda t: t.matmul(C.pb[4][:, 0:1], C.ones_row_f[:],
                                        tsc[:], start=True, stop=True),
                after=[tt1, C.t_const])
    tt3 = em.op("v", lambda v: v.tensor_copy(it3[:], C.pb[4][:, 0:1]),
                after=[tt2])
    tt4a = em.op("v", lambda v: v.tensor_scalar_mul(bias_col[:], it3[:], -1.0),
                 after=[tt3])
    tt4 = em.op("v", lambda v: v.tensor_max(bias_col[:], bias_col[:], it3[:]),
                after=[tt4a])
    tt5 = em.op("v", lambda v: v.tensor_scalar_min(bias_col[:], bias_col[:],
                                                   40.0), after=[tt4])
    tok_bias = em.op("v", lambda v: v.tensor_scalar_mul(bias_col[:],
                                                        bias_col[:], -1.0),
                     after=[tt5])

    xb = [C.sb("axb", [128, TBL], BF16) for _ in range(NCH)]
    xln = [C.sb("axln", [128, TBL], BF16) for _ in range(NCH)]
    kt = [C.sb("akt", [128, T], BF16) for _ in range(NPC)]
    vbar = C.sb("avbar", [128, 32 * (16 * 65)], BF16)
    qt = [C.sb("aqt", [128, TBL], BF16) for _ in range(NPC)]
    aout = [C.sb("aao", [128, TBL], BF16) for _ in range(NCH)]
    pbuf = [C.sb("apb", [128, 2 * TBL], BF16) for _ in range(2)]
    wstr = [C.sb("awstr", [128, 1024], BF16) for _ in range(2)]
    qraw = C.sb("aqraw", [128, TBL], F32)
    sqs = C.sb("asq", [128, TBL], BF16)
    nrow = C.sb("anr", [3, TBL], F32)
    r11 = C.sb("ar11", [1, TBL], F32)
    rb = C.sb("arb", [64, TBL], F32)
    osum = C.sb("aosum", [128, TBL], F32)
    xfr = [C.sb("axf", [128, TBL], F32) for _ in range(2)]

    tok_vones = em.op("v", lambda v: v.memset(vbar[:, 64::65], 1.0))

    def h_slot(h):
        c, j = divmod(h, 3)
        return c, 32 * j

    def qk_chain(src_w, pc, blk, bias6, dst, dst_off, xln_toks, extra,
                 scale_inv, wbuf, wtoks, wused):
        buf = pc % 2
        wdep = [wused[buf]] if wused[buf] else []
        wtoks[buf] = em.dma("p", wbuf[buf][:],
                            src_w[pc * 128:(pc + 1) * 128, :], after=wdep)
        bank = C.pb[pc % 2]
        fns = [lambda t, kc=kc, buf=buf, bank=bank: t.matmul(
            bank[:, 0:TBL], wbuf[buf][:, kc * 128:(kc + 1) * 128],
            xln[kc][:, 0:TBL], start=(kc == 0), stop=(kc == NCH - 1))
            for kc in range(NCH)]
        tmm = em.ops("t", fns, after=xln_toks + [wtoks[buf]] + extra)
        wused[buf] = tmm
        t1 = em.op("v", lambda v, pc=pc, bank=bank: v.tensor_scalar_add(
            qraw[:, 0:TBL], bank[:, 0:TBL], bias6[:, pc:pc + 1]),
            after=[tmm, tok_b])
        t2 = em.op("s", lambda s: s.activation(sqs[:, 0:TBL], qraw[:, 0:TBL],
                                               AF.Square), after=[t1])
        t3 = em.op("t", lambda t: t.matmul(C.pb[4][0:3, 0:TBL], e3_sb[:],
                                           sqs[:, 0:TBL], start=True,
                                           stop=True),
                   after=[t2, tok_w, pb_hist.get(4)])
        t4 = em.op("s", lambda s: s.activation(nrow[:, 0:TBL],
                                               C.pb[4][0:3, 0:TBL], AF.Sqrt),
                   after=[t3])
        t5 = em.op("v", lambda v: v.tensor_scalar_max(nrow[:, 0:TBL],
                                                      nrow[:, 0:TBL], 1e-12),
                   after=[t4])
        t6 = em.op("v", lambda v: v.reciprocal(nrow[:, 0:TBL], nrow[:, 0:TBL]),
                   after=[t5])
        if scale_inv:
            t6 = em.op("v", lambda v: v.tensor_scalar_mul(
                nrow[:, 0:TBL], nrow[:, 0:TBL], it3[0:3, 0:1]),
                after=[t6, tt3])
        t7 = em.op("t", lambda t: t.matmul(C.pb[5][:, 0:TBL], e3t_sb[:],
                                           nrow[:, 0:TBL], start=True,
                                           stop=True),
                   after=[t6, tok_w, pb_hist.get(5)])
        pb_hist[4] = t7
        t8 = em.op("v", lambda v, dst=dst, dst_off=dst_off: v.tensor_mul(
            dst[:, dst_off:dst_off + TBL], qraw[:, 0:TBL], C.pb[5][:, 0:TBL]),
            after=[t7])
        pb_hist[5] = t8
        pb_hist[pc % 2] = t1
        return t8

    pb_hist = {}
    wstr_toks = [None, None]
    wstr_used = [None, None]
    st_last = None
    ph_last = []
    for blk in range(NB):
        xln_toks, _ = emit_ln(C, x_hbm, xb, xln, blk, TBL,
                              after=[tok_w] + ph_last)
        st_toks = [em.dma("y", xln_hbm[128 * c:128 * (c + 1),
                                       blk * TBL:(blk + 1) * TBL],
                          xln[c][:, 0:TBL], after=[xln_toks[c]])
                   for c in range(NCH)]
        st_last = st_toks[-1]
        for pc in range(NPC):
            tk8 = qk_chain(W["aa_wkp"], pc, blk, bkc, kt[pc], blk * TBL,
                           xln_toks, ph_last, False, wstr, wstr_toks, wstr_used)
            ph_last = [tk8]
        for ts_ in range(4):
            for vc in range(2):
                bank = C.pb[2 + (2 * ts_ + vc) % 2]
                fns = [lambda t, kc=kc, ts_=ts_, vc=vc, bank=bank: t.matmul(
                    bank[:, 0:512], xln[kc][:, ts_ * 128:(ts_ + 1) * 128],
                    wv_sb[:, kc * 1024 + vc * 512:kc * 1024 + (vc + 1) * 512],
                    start=(kc == 0), stop=False) for kc in range(NCH)]
                fns.append(lambda t, vc=vc, bank=bank: t.matmul(
                    bank[:, 0:512], C.ones_row_bf[:],
                    bv_sb[:, vc * 512:(vc + 1) * 512], start=False, stop=True))
                tmm = em.ops("t", fns,
                             after=xln_toks + [tok_w, C.t_const] + ph_last)
                j = blk * 4 + ts_
                fns2 = [lambda v, hh=hh, j=j, vc=vc, bank=bank: v.tensor_copy(
                    vbar[:, j * 1040 + (8 * vc + hh) * 65:
                         j * 1040 + (8 * vc + hh) * 65 + 64],
                    bank[:, hh * 64:(hh + 1) * 64]) for hh in range(8)]
                tvv = em.ops("v", fns2, after=[tmm, tok_vones])
                ph_last = [tvv]
    phaseA = ph_last + [st_last]

    pbuf_hist = [None, None]
    aout_hist = None
    prev_res = None
    qb_hist = None
    for blk in range(NB):
        tq0 = blk * TBL
        xl_toks = [em.dma("p", xln[c][:, 0:TBL],
                          xln_hbm[128 * c:128 * (c + 1), tq0:tq0 + TBL],
                          after=[st_last, prev_res] + phaseA +
                          ([qb_hist] if qb_hist else []))
                   for c in range(NCH)]
        q_toks = []
        for pc in range(NPC):
            t8 = qk_chain(W["aa_wqp"], pc, blk, bqc, qt[pc], 0,
                          xl_toks, phaseA + ([qb_hist] if qb_hist else []),
                          True, wstr, wstr_toks, wstr_used)
            q_toks.append(t8)
        for h in range(16):
            hc, hb = h_slot(h)
            nj = blk * 4 + 4
            obank = C.pb[2 + h % 2]
            first_dep = pb_hist.get(2 + h % 2)
            prs = [list(range(nj))[i:i + 2] for i in range(0, nj, 2)]
            for pi, pr in enumerate(prs):
                w = pi % 2
                wide = C.psA if w == 0 else C.psB
                bks = (0, 1) if w == 0 else (5, 6)
                fns = []
                for sub, jc in enumerate(pr):
                    fns.append(lambda t, jc=jc, o0=sub * TBL, hc=hc, hb=hb,
                               wide=wide: t.matmul(
                                   wide[:, o0:o0 + TBL],
                                   kt[hc][hb:hb + 16, jc * 128:(jc + 1) * 128],
                                   qt[hc][hb:hb + 16, 0:TBL],
                                   start=True, stop=True))
                tmm = em.ops("t", fns, after=[q_toks[-1], pb_hist.get(bks[0]),
                                              pb_hist.get(bks[1])] + phaseA)
                wd = len(pr) * TBL
                texp = em.op("s", lambda s, w=w, wd=wd, wide=wide: s.activation(
                    pbuf[w][:, 0:wd], wide[:, 0:wd], AF.Exp,
                    bias=bias_col[:]),
                    after=[tmm, tok_bias] +
                    ([pbuf_hist[w]] if pbuf_hist[w] else []))
                pb_hist[bks[0]] = texp
                pb_hist[bks[1]] = texp
                tlast = texp
                for sub, jc in enumerate(pr):
                    if jc < blk * 4:
                        continue
                    base = tq0 - jc * 128
                    tlast = em.op("p", lambda g, w=w, o0=sub * TBL, base=base:
                                  g.affine_select(pbuf[w][:, o0:o0 + TBL],
                                                  pbuf[w][:, o0:o0 + TBL],
                                                  [[1, TBL]], ALU.is_ge, 0.0,
                                                  base=base,
                                                  channel_multiplier=-1),
                                  after=[tlast])
                fns = []
                for sub, jc in enumerate(pr):
                    fns.append(lambda t, jc=jc, sub=sub, w=w, obank=obank, h=h,
                               nj=nj: t.matmul(
                                   obank[0:65, 0:TBL],
                                   vbar[:, jc * 1040 + h * 65:
                                        jc * 1040 + h * 65 + 65],
                                   pbuf[w][:, sub * TBL:(sub + 1) * TBL],
                                   start=(jc == 0), stop=(jc == nj - 1)))
                tv = em.ops("t", fns, after=[tlast] +
                            ([first_dep] if pi == 0 else []))
                pbuf_hist[w] = tv
            tr = em.op("v", lambda v, obank=obank: v.reciprocal(
                r11[:, 0:TBL], obank[64:65, 0:TBL]), after=[tv])
            tb2 = em.op("t", lambda t: t.matmul(
                C.pb[4][0:64, 0:TBL], C.ones_row_f[:, 0:64], r11[:, 0:TBL],
                start=True, stop=True), after=[tr, pb_hist.get(4)])
            tb3 = em.op("v", lambda v: v.tensor_copy(rb[:, 0:TBL],
                                                     C.pb[4][0:64, 0:TBL]),
                        after=[tb2])
            pb_hist[4] = tb3
            tao = em.op("v", lambda v, h=h, obank=obank: v.tensor_mul(
                aout[h // 2][64 * (h % 2):64 * (h % 2) + 64, 0:TBL],
                obank[0:64, 0:TBL], rb[:, 0:TBL]),
                after=[tb3, aout_hist])
            pb_hist[2 + h % 2] = tao
        aout_last = tao
        res_toks = []
        for oc in range(NCH):
            buf = oc % 2
            wdep = [wstr_used[buf]] if wstr_used[buf] else []
            wstr_toks[buf] = em.dma("p", wstr[buf][:],
                                    W["aa_wol"][oc * 128:(oc + 1) * 128, :],
                                    after=wdep)
            bank = C.pb[5 + oc % 2]
            fns = [lambda t, kc=kc, buf=buf, bank=bank: t.matmul(
                bank[:, 0:TBL], wstr[buf][:, kc * 128:(kc + 1) * 128],
                aout[kc][:, 0:TBL], start=(kc == 0), stop=(kc == NCH - 1))
                for kc in range(NCH)]
            tmm = em.ops("t", fns, after=[aout_last, wstr_toks[buf],
                                          pb_hist.get(5 + oc % 2)])
            wstr_used[buf] = tmm
            qb_hist = tmm
            xi = xfr[oc % 2]
            tx = em.dma("y", xi[:, 0:TBL],
                        x_hbm[128 * oc:128 * (oc + 1), tq0:tq0 + TBL])
            tv2 = em.op("v", lambda v, oc=oc, xi=xi, bank=bank:
                        v.scalar_tensor_tensor(osum[:, 0:TBL], bank[:, 0:TBL],
                                               bo_sb[:, oc:oc + 1],
                                               xi[:, 0:TBL], ALU.add, ALU.add),
                        after=[tmm, tx] + ([res_toks[-1]] if res_toks else []))
            pb_hist[5 + oc % 2] = tv2
            res_toks.append(em.dma("y", xo_hbm[128 * oc:128 * (oc + 1),
                                               tq0:tq0 + TBL],
                                   osum[:, 0:TBL], after=[tv2]))
        aout_hist = qb_hist
        prev_res = res_toks[-1]
    return res_toks


def build_l0attna_program():
    nc = bass.Bass("TRN2", target_bir_lowering=False)
    x_in = nc.dram_tensor("x_in", [DM, T], F32, kind="ExternalInput").ap()
    x_out = nc.dram_tensor("x_out", [DM, T], F32, kind="ExternalOutput").ap()
    W = {}
    for name, shape in [("aa_wqp", [768, 1024]), ("aa_wkp", [768, 1024]),
                        ("aa_wv", [1024, 1024]), ("aa_wol", [1024, 1024]),
                        ("aa_bq", [128, 6]), ("aa_qph", [128, 6]),
                        ("aa_bk", [128, 6]), ("aa_kph", [128, 6]),
                        ("aa_bv", [1, 1024]), ("aa_bo", [128, 8]),
                        ("aa_temp", [1, 1]), ("aa_e3", [128, 3]),
                        ("aa_e3t", [3, 128])]:
        W[name] = nc.dram_tensor(name, shape, F32, kind="ExternalInput").ap()
    xln_hbm = nc.dram_tensor("xln_s", [DM, T], BF16, kind="Internal").ap()
    C = Core(nc)
    res = emit_l0attna(C, x_in, x_out, W, xln_hbm)
    C.em.finalize(res)
    return nc


def build_l0attnb_program(debug=False):
    nc = bass.Bass("TRN2", target_bir_lowering=False)
    x_in = nc.dram_tensor("x_in", [DM, T], F32, kind="ExternalInput").ap()
    x_out = nc.dram_tensor("x_out", [DM, T], F32, kind="ExternalOutput").ap()
    W = {}
    for name, shape in [("ab_wql", [1024, 1024]), ("ab_wkT", [1024, 512]),
                        ("ab_wvT", [1024, 512]), ("ab_wol", [1024, 1024]),
                        ("ab_bo", [128, 8]), ("ab_qaux", [128, T])]:
        W[name] = nc.dram_tensor(name, shape, F32, kind="ExternalInput").ap()
    xln_hbm = nc.dram_tensor("xln_s", [DM, T], BF16, kind="Internal").ap()
    C = Core(nc)
    dbg = None
    if debug:
        dbg = {}
        for name, shape in [("ksr", [1, 16]), ("kb0", [128, 512]),
                            ("bBr", [1, 16]), ("qb0", [128, 512]),
                            ("p00", [128, 512]), ("rec0", [1, 512]),
                            ("ao0", [128, 512]), ("s00", [128, 512]),
                            ("aoall", [1024, 512]),
                            ("aux00", [68, 512]), ("qa00", [68, 512])]:
            dbg[name] = nc.dram_tensor("dbg_" + name, shape, F32,
                                       kind="ExternalOutput").ap()
    res = emit_l0attnb(C, x_in, x_out, W, xln_hbm, dbg=dbg)
    C.em.finalize(res)
    return nc


def emit_cast_hbm(C, src_f32, dst_bf16, rows, cols, tag):
    em = C.em
    buf = [C.sb(tag, [128, cols], BF16) for _ in range(2)]
    tok_st = [None, None]
    for rc in range(rows // 128):
        b = rc % 2
        tl = em.dma("p", buf[b][:], src_f32[rc * 128:(rc + 1) * 128, :],
                    after=[tok_st[b]] if tok_st[b] else [])
        tok_st[b] = em.dma("y", dst_bf16[rc * 128:(rc + 1) * 128, :], buf[b][:],
                           after=[tl])
    return tok_st[(rows // 128 - 1) % 2]



def build_l1ffn_program():
    nc = bass.Bass("TRN2", target_bir_lowering=False)
    x_in = nc.dram_tensor("x_in", [DM, T], F32, kind="ExternalInput").ap()
    x_out = nc.dram_tensor("x_out", [DM, T], F32, kind="ExternalOutput").ap()
    W = {
        "l1f_bd1": nc.dram_tensor("l1f_bd1", [128, 4 * 8 * 128], F32,
                                  kind="ExternalInput").ap(),
        "l1f_bd2": nc.dram_tensor("l1f_bd2", [128, 4 * 32 * 128], F32,
                                  kind="ExternalInput").ap(),
        "l1f_cpT": nc.dram_tensor("l1f_cpT", [16384, 1024], F32,
                                  kind="ExternalInput").ap(),
    }
    cpbf = nc.dram_tensor("cpbf", [16384, 1024], BF16, kind="Internal").ap()
    C = Core(nc)
    tok_cast = emit_cast_hbm(C, W["l1f_cpT"], cpbf, 16384, 1024, "cpcast")
    res = emit_l1ffn(C, x_in, x_out, W, cpbf, tok_cast)
    C.em.finalize(res)
    return nc


def build_l0ffn_program():
    nc = bass.Bass("TRN2", target_bir_lowering=False)
    x_in = nc.dram_tensor("x_in", [DM, T], F32, kind="ExternalInput").ap()
    x_out = nc.dram_tensor("x_out", [DM, T], F32, kind="ExternalOutput").ap()
    W = {}
    for name, shape in [("ff_w1l", [8 * 32 * 128, 8 * 128]),
                        ("ff_w2l", [8 * 8 * 128, 32 * 128]),
                        ("ff_b1l", [128, 8 * 32]), ("ff_b2", [8, 1024]),
                        ("ff_rl", [1024, 12]), ("ff_rlb", [1, 12]),
                        ("ff_oh", [8, 8 * 128])]:
        W[name] = nc.dram_tensor(name, shape, F32, kind="ExternalInput").ap()
    w1bf = nc.dram_tensor("w1bf", [8 * 32 * 128, 8 * 128], BF16,
                          kind="Internal").ap()
    w2bf = nc.dram_tensor("w2bf", [8 * 8 * 128, 32 * 128], BF16,
                          kind="Internal").ap()
    xln_hbm = nc.dram_tensor("xln_s", [DM, T], BF16, kind="Internal").ap()
    C = Core(nc)
    t1 = emit_cast_hbm(C, W["ff_w1l"], w1bf, 8 * 32 * 128, 8 * 128, "w1cast")
    t2 = emit_cast_hbm(C, W["ff_w2l"], w2bf, 8 * 8 * 128, 32 * 128, "w2cast")
    res = emit_l0ffn(C, x_in, x_out, W, w1bf, w2bf, t2, xln_hbm)
    C.em.finalize(res)
    return nc


SUBLAYERS = ["L0_attn_a", "L0_ffn_1", "L0_attn_b", "L0_ffn_2",
             "L1_attn_a", "L1_ffn_1", "L1_attn_b", "L1_ffn_2"]

_BUILDERS = {
    "L0_attn_a": (build_l0attna_program, host_prep_l0attna),
    "L0_ffn": (build_l0ffn_program, host_prep_l0ffn),
    "L0_attn_b": (build_l0attnb_program, host_prep_l0attnb),
    "L1_attn_a": (build_l1attna_program, host_prep_l1attna),
    "L1_ffn": (build_l1ffn_program, host_prep_l1ffn),
    "L1_attn_b": (build_l1attnb_program, host_prep_l1attnb),
}

_W_SHAPES = {
    "aa_wqp": [768, 1024], "aa_wkp": [768, 1024], "aa_wv": [1024, 1024],
    "aa_wol": [1024, 1024], "aa_bq": [128, 6], "aa_qph": [128, 6],
    "aa_bk": [128, 6], "aa_kph": [128, 6], "aa_bv": [1, 1024],
    "aa_bo": [128, 8], "aa_temp": [1, 1], "aa_e3": [128, 3],
    "aa_e3t": [3, 128],
    "ff_w1l": [8 * 32 * 128, 8 * 128], "ff_w2l": [8 * 8 * 128, 32 * 128],
    "ff_b1l": [128, 8 * 32], "ff_b2": [8, 1024], "ff_rl": [1024, 12],
    "ff_rlb": [1, 12], "ff_oh": [8, 8 * 128],
    "ab_wql": [1024, 1024], "ab_wkT": [1024, 512], "ab_wvT": [1024, 512],
    "ab_wol": [1024, 1024], "ab_bo": [128, 8], "ab_qaux": [128, T],
    "la_gwql": [1024, 1024], "la_gwol": [1024, 1024],
    "la_lwql": [1024, 1024], "la_lwol": [1024, 1024],
    "la_gkd": [1024, 256], "la_gku": [256, 256], "la_gvu": [256, 256],
    "la_lwk": [1024, 256], "la_lwv": [1024, 256], "la_n1": [128, 8],
    "la_n2": [128, 8], "la_cosg": [128, T], "la_sing": [128, T],
    "la_cosl": [128, T], "la_sinl": [128, T],
    "lb_wql": [1024, 1024], "lb_wkl": [1024, 1024], "lb_wv": [1024, 1024],
    "lb_wol": [1024, 1024], "lb_bq": [128, 8], "lb_bk": [128, 8],
    "lb_bv": [1, 1024], "lb_bo": [128, 8],
    "l1f_bd1": [128, 4 * 8 * 128], "l1f_bd2": [128, 4 * 32 * 128],
    "l1f_cpT": [16384, 1024],
}


def build_fused_program():
    nc = bass.Bass("TRN2", target_bir_lowering=False)
    x_in = nc.dram_tensor("x_in", [DM, T], F32, kind="ExternalInput").ap()
    x_out = nc.dram_tensor("x_out", [DM, T], F32, kind="ExternalOutput").ap()
    stages = [x_in]
    for i in range(1, 8):
        stages.append(nc.dram_tensor("x_st%d" % i, [DM, T], F32,
                                     kind="Internal").ap())
    stages.append(x_out)
    W = {name: nc.dram_tensor(name, shape, F32, kind="ExternalInput").ap()
         for name, shape in _W_SHAPES.items()}
    xln_hbm = nc.dram_tensor("xln_s", [DM, T], BF16, kind="Internal").ap()
    k_hbm = nc.dram_tensor("k_s", [DM, T], BF16, kind="Internal").ap()
    vb_hbm = nc.dram_tensor("vb_s", [128, 32 * 1040], BF16, kind="Internal").ap()
    cpbf = nc.dram_tensor("cpbf", [16384, 1024], BF16, kind="Internal").ap()
    w1bf = nc.dram_tensor("w1bf", [8 * 32 * 128, 8 * 128], BF16,
                          kind="Internal").ap()
    w2bf = nc.dram_tensor("w2bf", [8 * 8 * 128, 32 * 128], BF16,
                          kind="Internal").ap()
    h0b = nc.dram_tensor("h0b_s", [DM, T], BF16, kind="Internal").ap()
    h0f = nc.dram_tensor("h0f_s", [DM, T], F32, kind="Internal").ap()
    h1f = nc.dram_tensor("h1f_s", [DM, T], F32, kind="Internal").ap()

    C = Core(nc)
    C.begin_scope()
    emit_cast_hbm(C, W["ff_w1l"], w1bf, 8 * 32 * 128, 8 * 128, "w1cast")
    t2 = emit_cast_hbm(C, W["ff_w2l"], w2bf, 8 * 8 * 128, 32 * 128, "w2cast")
    t3 = emit_cast_hbm(C, W["l1f_cpT"], cpbf, 16384, 1024, "cpcast")
    C.end_scope()
    C.em.set_phase()

    res = None
    for i, name in enumerate(SUBLAYERS):
        base = name[:-2] if name.endswith(("_1", "_2")) else name
        C.begin_scope()
        if base == "L0_attn_a":
            res = emit_l0attna(C, stages[i], stages[i + 1], W, xln_hbm)
        elif base == "L0_ffn":
            res = emit_l0ffn(C, stages[i], stages[i + 1], W, w1bf, w2bf,
                             t2, xln_hbm)
        elif base == "L0_attn_b":
            res = emit_l0attnb(C, stages[i], stages[i + 1], W, xln_hbm)
        elif base == "L1_attn_a":
            res = emit_l1attna(C, stages[i], stages[i + 1], W, h0b, h0f, h1f)
        elif base == "L1_ffn":
            res = emit_l1ffn(C, stages[i], stages[i + 1], W, cpbf, t3)
        elif base == "L1_attn_b":
            res = emit_l1attnb(C, stages[i], stages[i + 1], W, xln_hbm,
                               k_hbm, vb_hbm)
        C.end_scope()
        C.em.set_phase()
    C.em.finalize(res)
    return nc


def host_prep_all(inputs):
    W = {}
    for prep in (host_prep_l0attna, host_prep_l0ffn, host_prep_l0attnb,
                 host_prep_l1attna, host_prep_l1ffn, host_prep_l1attnb):
        W.update(prep(inputs))
    return W


def kernel(**inputs):
    x = np.asarray(inputs["L0_attn_a_x"], np.float32)
    xT = [np.ascontiguousarray(x[b].T) for b in range(N_CORES)]
    W = host_prep_all(inputs)
    nc = build_fused_program()
    in_maps = [{"x_in": xT[b], **W} for b in range(N_CORES)]
    res = run_bass_kernel_spmd(nc, in_maps, core_ids=list(range(N_CORES)))
    out = np.stack([np.ascontiguousarray(np.asarray(r["x_out"]).T)
                    for r in res.results])
    return out.astype(np.float32)


def kernel_unfused(**inputs):
    x = np.asarray(inputs["L0_attn_a_x"], np.float32)
    xT = [np.ascontiguousarray(x[b].T) for b in range(N_CORES)]
    progs = {}
    weights = {}
    for name in SUBLAYERS:
        base = name[:-2] if name.endswith(("_1", "_2")) else name
        if base not in progs:
            build, prep = _BUILDERS[base]
            progs[base] = build()
            weights[base] = prep(inputs)
        in_maps = [{"x_in": xT[b], **weights[base]} for b in range(N_CORES)]
        res = run_bass_kernel_spmd(progs[base], in_maps,
                                   core_ids=list(range(N_CORES)))
        xT = [np.asarray(r["x_out"]) for r in res.results]
    out = np.stack([np.ascontiguousarray(xb.T) for xb in xT])
    return out.astype(np.float32)
